# version 09 (final)
# speedup vs baseline: 1.3618x; 1.0103x over previous
import contextlib
import numpy as np
import ml_dtypes
import concourse.bass as bass
import concourse.mybir as mybir
from concourse.bass_utils import run_bass_kernel_spmd

F32 = mybir.dt.float32
BF16 = mybir.dt.bfloat16
I8 = mybir.dt.int8
AF = mybir.ActivationFunctionType
ALU = mybir.AluOpType
AXX = mybir.AxisListType.X

NCORE = 8
T = 2048
D = 1024
DC = D // 128
TT = 512
NTT = T // TT
BF = ml_dtypes.bfloat16

MAGIC = float(np.float32(1.5 * 2 ** 23))


class Sem:
    def __init__(self, C, name):
        self.h = C.ctx.enter_context(C.nc.semaphore(name))
        self.n = 0

    def inc(self, instr, k=1):
        instr.then_inc(self.h, k)
        self.n += k
        return instr


class BC:

    def __init__(self, nc, ctx):
        self.nc = nc
        self.ctx = ctx
        self.block = None
        self.sems = {}
        self.uid = 0
        self.io = {}
        self.cur = None
        self.prefix = ""

    def sem(self, name):
        if name not in self.sems:
            self.sems[name] = Sem(self, name)
        return self.sems[name]

    def sb(self, name, shape, dt=F32):
        stack = self.cur if self.cur is not None else self.ctx
        return stack.enter_context(self.nc.sbuf_tensor(self.prefix + name, shape, dt))

    def ps(self, name, shape, dt=F32):
        return self.ctx.enter_context(self.nc.psum_tensor(name, shape, dt))

    def nid(self):
        self.uid += 1
        return self.uid

    def row(self, i):
        return self.ROWS[0:1, i * T:(i + 1) * T]

    def PS(self, bank, nbanks=1, rows=128):
        assert bank + nbanks <= 7
        return self.PSUM[0:rows, bank * TT:(bank + nbanks) * TT]

    def wait_all(self, eng, names=("ld", "dve", "act", "ev", "mm", "pe", "w")):
        for nm in names:
            if nm in self.sems and self.sems[nm].n > 0:
                eng.wait_ge(self.sems[nm].h, self.sems[nm].n)

    def constf(self, val, dt=F32):
        key = (float(val), dt)
        if not hasattr(self, "_cvals"):
            self._cvals = {}
            self.CPOOL = self.sb("cpool", [128, 32], F32)
            self.CPOOLB = self.sb("cpoolb", [128, 8], BF16)
        if key not in self._cvals:
            pool = self.CPOOL if dt == F32 else self.CPOOLB
            idx = sum(1 for k in self._cvals if k[1] == dt)
            self._cvals[key] = idx
            ini = self.sem("ini")

            @self.block.gpsimd
            def _(g):
                ini.inc(g.memset(pool[:, idx:idx + 1], float(val)))
        idx = self._cvals[key]
        pool = self.CPOOL if dt == F32 else self.CPOOLB
        return pool[:, idx:idx + 1]


def Xc(C, dc, t0=0, tn=T):
    return C.X[:, dc * T + t0: dc * T + t0 + tn]


def const_inputs():
    return {
        "ones_col_bf": np.ones((128, 1), BF),
        "ones_col_f32": np.ones((128, 1), np.float32),
        "ones_row_bf": np.ones((1, 128), BF),
        "ones_row_f32": np.ones((1, 128), np.float32),
    }


def emit_consts(C):
    C.ONESB = C.sb("onesb", [128, 1], BF16)
    C.ONESF = C.sb("onesf", [128, 1], F32)
    C.ONEROWB = C.sb("onerowb", [1, 128], BF16)
    C.ONEROWF = C.sb("onerowf", [1, 128], F32)


def emit_load_x(C):
    s = C.sem("ld")
    xin = C.io["x_fm"]

    @C.block.sync
    def _(sy):
        for dc in range(DC):
            s.inc(sy.dma_start(out=Xc(C, dc), in_=xin[dc * 128:(dc + 1) * 128, :]), 16)
        s.inc(sy.dma_start(out=C.ONESB[:, :], in_=C.io["ones_col_bf"][:, :]), 16)
        s.inc(sy.dma_start(out=C.ONESF[:, :], in_=C.io["ones_col_f32"][:, :]), 16)
        s.inc(sy.dma_start(out=C.ONEROWB[:, :], in_=C.io["ones_row_bf"][:, :]), 16)
        s.inc(sy.dma_start(out=C.ONEROWF[:, :], in_=C.io["ones_row_f32"][:, :]), 16)


def emit_store_x(C):
    st = C.sem("st")
    yout = C.io["y_fm"]

    @C.block.sync
    def _(sy):
        for nm, sem in list(C.sems.items()):
            if nm != "st" and sem.n > 0:
                sy.wait_ge(sem.h, sem.n)
        for dc in range(DC):
            st.inc(sy.dma_start(out=yout[dc * 128:(dc + 1) * 128, :], in_=Xc(C, dc)), 16)
        sy.wait_ge(st.h, st.n)


def emit_ln(C, dst, src_tiles=None, kind="ln", gamma_col=None, nchunks=DC,
            eps=1e-5, eps2=1e-6, dst_off=0, dstT=None, keep_rows=False, ntok=T):
    nc = C.nc
    dstT = dstT or ntok
    dim = nchunks * 128
    if src_tiles is None:
        src_tiles = lambda dc: Xc(C, dc)
    uid = C.nid()
    if not hasattr(C, "LNXB"):
        C.LNXB = C.sb("LNXB", [128, 2 * TT], BF16)
        C.LNSQ = C.sb("LNSQ", [128, 2 * TT], BF16)
    xb = C.LNXB
    sq = C.LNSQ
    prA = C.PS(5)
    prB = C.PS(6)
    m_row = C.ROWS[0:1, 0 * T: 0 * T + ntok]
    q_row = C.ROWS[0:1, 1 * T: 1 * T + ntok]
    r_row = q_row
    msqt = C.LNTMP[0:1, 0:ntok]
    pb = C.PS(5, 2)
    s_act = C.sem("act")
    s_pe = C.sem("pe")
    s_dve = C.sem("dve")
    s_ld = C.sem("ld")
    do_mean = kind in ("ln", "ln2")
    eps_ap = C.constf(eps)
    eps2_ap = C.constf(eps2) if kind == "ln2" else None

    act_base = s_act.n
    pe_base = s_pe.n
    dve_base = s_dve.n

    s_ev2 = C.sem("ev2")
    ev2_base = s_ev2.n
    per_i = 2 if do_mean else 1

    @C.block.scalar
    def _(sc):
        sc.wait_ge(s_ld.h, s_ld.n)
        C.wait_all(sc, ("dve", "ev", "mm", "pe"))

    @C.block.tensor
    def _(te):
        te.wait_ge(s_ld.h, s_ld.n)

    for i in range(ntok // TT):
        @C.block.scalar
        def _(sc, i=i):
            for dc in range(nchunks):
                step = i * nchunks + dc
                if step >= 2:
                    sc.wait_ge(s_pe.h, pe_base + step - 1)
                s_act.inc(sc.activation(out=xb[:, (step % 2) * TT:(step % 2 + 1) * TT],
                                        in_=src_tiles(dc)[:, i * TT:(i + 1) * TT],
                                        func=AF.Identity))

        @C.block.vector
        def _(v, i=i):
            for dc in range(nchunks):
                step = i * nchunks + dc
                v.wait_ge(s_act.h, act_base + step + 1)
                sl = slice((step % 2) * TT, (step % 2 + 1) * TT)
                s_dve.inc(v.tensor_mul(sq[:, sl], xb[:, sl], xb[:, sl]))

        @C.block.tensor
        def _(te, i=i):
            if i > 0:
                te.wait_ge(s_ev2.h, s_ev2.n)
            for dc in range(nchunks):
                step = i * nchunks + dc
                te.wait_ge(s_dve.h, dve_base + step + 1)
                sl = slice((step % 2) * TT, (step % 2 + 1) * TT)
                if do_mean:
                    te.matmul(prA[0:1, 0:TT], C.ONESB[:, :], xb[:, sl],
                              start=(dc == 0), stop=(dc == nchunks - 1))
                s_pe.inc(te.matmul(prB[0:1, 0:TT], C.ONESB[:, :], sq[:, sl],
                                   start=(dc == 0), stop=(dc == nchunks - 1)))

        @C.block.vector
        def _(v, i=i):
            v.wait_ge(s_pe.h, pe_base + (i + 1) * nchunks)
            if do_mean:
                s_ev2.inc(v.tensor_scalar(C.ROWS[0:1, 0 * T + i * TT: 0 * T + (i + 1) * TT], prA[0:1, 0:TT],
                                          1.0 / dim, None, op0=ALU.mult))
            s_ev2.inc(v.tensor_scalar(C.ROWS[0:1, 1 * T + i * TT: 1 * T + (i + 1) * TT], prB[0:1, 0:TT],
                                      1.0 / dim, None, op0=ALU.mult))

    @C.block.scalar
    def _(sc):
        sc.wait_ge(s_ev2.h, s_ev2.n)
        if do_mean:
            s_act.inc(sc.activation(out=msqt, in_=m_row, func=AF.Square))

    @C.block.vector
    def _(v):
        v.wait_ge(s_act.h, s_act.n)
        v.wait_ge(s_ev2.h, s_ev2.n)
        if do_mean:
            s_dve.inc(v.tensor_sub(q_row, q_row, msqt))

    @C.block.scalar
    def _(sc):
        sc.wait_ge(s_dve.h, s_dve.n)
        sc.wait_ge(C.sem("ini").h, C.sem("ini").n)
        s_act.inc(sc.activation(out=r_row, in_=q_row, func=AF.Sqrt,
                                bias=eps_ap[0:1, :]))

    @C.block.vector
    def _(v):
        v.wait_ge(s_act.h, s_act.n)
        v.wait_ge(s_dve.h, s_dve.n)
        s_dve.inc(v.reciprocal(r_row, r_row))

    if kind == "ln2":
        t_row = C.LNTMP[0:1, 0:ntok]

        @C.block.vector
        def _(v):
            v.wait_ge(s_dve.h, s_dve.n)
            s_dve.inc(v.tensor_mul(t_row, q_row, r_row))
            v.wait_ge(s_dve.h, s_dve.n)
            s_dve.inc(v.tensor_mul(t_row, t_row, r_row))

        @C.block.scalar
        def _(sc):
            sc.wait_ge(s_dve.h, s_dve.n)
            sc.wait_ge(C.sem("ini").h, C.sem("ini").n)
            s_act.inc(sc.activation(out=t_row, in_=t_row, func=AF.Sqrt,
                                    bias=eps2_ap[0:1, :]))

        @C.block.vector
        def _(v):
            v.wait_ge(s_act.h, s_act.n)
            v.wait_ge(s_dve.h, s_dve.n)
            s_dve.inc(v.reciprocal(t_row, t_row))
            v.wait_ge(s_dve.h, s_dve.n)
            s_dve.inc(v.tensor_mul(r_row, r_row, t_row))

    @C.block.tensor
    def _(te):
        te.wait_ge(s_dve.h, s_dve.n)

    for i in range(ntok // TT):
        @C.block.tensor
        def _(te, i=i):
            if i > 0:
                te.wait_ge(s_dve.h, s_dve.n)
            if do_mean:
                te.matmul(pb[:, TT:2 * TT], C.ONEROWF[:, :], C.ROWS[0:1, 0 * T + i * TT: 0 * T + (i + 1) * TT],
                          start=True, stop=True)
            s_pe.inc(te.matmul(pb[:, 0:TT], C.ONEROWF[:, :], C.ROWS[0:1, 1 * T + i * TT: 1 * T + (i + 1) * TT],
                               start=True, stop=True))

        @C.block.vector
        def _(v, i=i):
            v.wait_ge(s_pe.h, s_pe.n)
            s_dve.inc(v.tensor_copy(C.RB[:, i * TT:(i + 1) * TT], pb[:, 0:TT]))
            if do_mean:
                s_dve.inc(v.tensor_copy(C.MB[:, i * TT:(i + 1) * TT], pb[:, TT:2 * TT]))

    tmp = C.LNTMP

    @C.block.vector
    def _(v):
        v.wait_ge(s_dve.h, s_dve.n)
        for dc in range(nchunks):
            o = dst_off + dc * dstT
            if do_mean:
                v.wait_ge(s_dve.h, s_dve.n)
                s_dve.inc(v.tensor_sub(tmp[:, 0:ntok], src_tiles(dc), C.MB[:, 0:ntok]))
                v.wait_ge(s_dve.h, s_dve.n)
                srcx = tmp[:, 0:ntok]
            else:
                srcx = src_tiles(dc)
            if gamma_col is not None:
                s_dve.inc(v.scalar_tensor_tensor(dst[:, o:o + ntok], srcx, gamma_col(dc),
                                                 C.RB[:, 0:ntok], op0=ALU.mult, op1=ALU.mult))
            else:
                s_dve.inc(v.tensor_mul(dst[:, o:o + ntok], srcx, C.RB[:, 0:ntok]))


def whole_w_ap(C, name, din, dout):
    h = C.io[name]
    return bass.AP(h, 0, [[dout, 128], [128 * dout, din // 128], [1, dout]])


def emit_L1_ffn(C):
    io = C.io
    lnx = C.sb("lnx", [128, DC * T], BF16)
    emit_ln(C, lnx, kind="ln")
    s_dve = C.sem("dve")
    s_ev = C.sem("ev")
    s_act = C.sem("act")
    s_w = C.sem("w")
    s_mm = C.sem("mm")

    u1 = C.sb("u1", [128, 8 * 256], BF16)
    v1 = C.sb("v1", [128, 2 * 4096], BF16)
    u2 = C.sb("u2", [128, 32 * 256], BF16)
    v2 = C.sb("v2", [128, 2 * 1024], BF16)
    b1 = C.sb("b1c", [128, 32], F32)
    b2 = C.sb("b2c", [128, 8], F32)

    @C.block.gpsimd
    def _(g):
        s_w.inc(g.dma_start(out=u1[:, :], in_=whole_w_ap(C, "L1_ffn_U1", 1024, 256)), 16)
        s_w.inc(g.dma_start(out=v1[:, :], in_=whole_w_ap(C, "L1_ffn_V1", 256, 4096)), 16)
        s_w.inc(g.dma_start(out=u2[:, :], in_=whole_w_ap(C, "L1_ffn_U2", 4096, 256)), 16)
        s_w.inc(g.dma_start(out=v2[:, :], in_=whole_w_ap(C, "L1_ffn_V2", 256, 1024)), 16)
        s_w.inc(g.dma_start(out=b1[:, :], in_=io["b1_fm"][:, :]), 16)
        s_w.inc(g.dma_start(out=b2[:, :], in_=io["b2_fm"][:, :]), 16)

    pbank = [C.PS(4), C.PS(5), C.PS(4), C.PS(6)]
    pP = C.PS(0, 2)
    pQ = C.PS(2, 2)
    Pb = C.sb("ffn_P", [128, 2 * TT], BF16)
    Hb = C.sb("ffn_H", [128, 2 * TT], BF16)
    Qb = C.sb("ffn_Q", [128, 2 * TT], BF16)

    for i in range(NTT):
        t0 = i * TT

        @C.block.tensor
        def _(te, i=i, t0=t0):
            if i == 0:
                te.wait_ge(s_w.h, s_w.n)
                te.wait_ge(s_dve.h, s_dve.n)
            else:
                te.wait_ge(s_ev.h, s_ev.n)
                te.wait_ge(s_act.h, s_act.n)
            for fc in range(2):
                for kc in range(8):
                    ins = te.matmul(pP[:, fc * TT:(fc + 1) * TT],
                              u1[:, kc * 256 + fc * 128: kc * 256 + (fc + 1) * 128],
                              lnx[:, kc * T + t0: kc * T + t0 + TT],
                              start=(kc == 0), stop=(kc == 7))
            s_mm.inc(ins)

        mmP = s_mm.n

        @C.block.vector
        def _(v, mmP=mmP):
            v.wait_ge(s_mm.h, mmP)
            s_ev.inc(v.tensor_copy(Pb[:, 0:TT], pP[:, 0:TT]))
            s_ev.inc(v.tensor_copy(Pb[:, TT:2 * TT], pP[:, TT:2 * TT]))

        evP = s_ev.n
        h_act_base = s_act.n
        h_mm_base = s_mm.n

        @C.block.tensor
        def _(te, evP=evP, h_act_base=h_act_base, h_mm_base=h_mm_base):
            te.wait_ge(s_ev.h, evP)
            for fc in range(32):
                bank = fc % 2
                if fc >= 2:
                    te.wait_ge(s_act.h, h_act_base + fc - 1)
                for kc in range(2):
                    ins = te.matmul(pbank[bank][:, :],
                              v1[:, kc * 4096 + fc * 128: kc * 4096 + (fc + 1) * 128],
                              Pb[:, kc * TT:(kc + 1) * TT],
                              start=(kc == 0), stop=(kc == 1))
                s_mm.inc(ins)
                te.wait_ge(s_act.h, h_act_base + fc + 1)
                for fcq in range(2):
                    ins = te.matmul(pQ[:, fcq * TT:(fcq + 1) * TT],
                              u2[:, fc * 256 + fcq * 128: fc * 256 + (fcq + 1) * 128],
                              Hb[:, (fc % 2) * TT:(fc % 2 + 1) * TT],
                              start=(fc == 0), stop=(fc == 31))
                s_mm.inc(ins)

        @C.block.scalar
        def _(sc, h_act_base=h_act_base, h_mm_base=h_mm_base):
            for fc in range(32):
                sc.wait_ge(s_mm.h, h_mm_base + 2 * fc + 1)
                if fc >= 2:
                    sc.wait_ge(s_mm.h, h_mm_base + 2 * (fc - 2) + 2)
                s_act.inc(sc.activation(out=Hb[:, (fc % 2) * TT:(fc % 2 + 1) * TT],
                                        in_=pbank[fc % 2][:, :],
                                        func=AF.Gelu_apprx_tanh,
                                        bias=b1[:, fc:fc + 1]))

        mmQ = s_mm.n

        @C.block.vector
        def _(v, mmQ=mmQ):
            v.wait_ge(s_mm.h, mmQ)
            s_ev.inc(v.tensor_copy(Qb[:, 0:TT], pQ[:, 0:TT]))
            s_ev.inc(v.tensor_copy(Qb[:, TT:2 * TT], pQ[:, TT:2 * TT]))

        evQ = s_ev.n

        @C.block.tensor
        def _(te, evQ=evQ):
            te.wait_ge(s_ev.h, evQ)
            for fc in range(8):
                bank = fc % 2 + 2
                if fc >= 2:
                    te.wait_ge(s_ev.h, evQ + fc - 1)
                for kc in range(2):
                    ins = te.matmul(pbank[bank][:, :],
                              v2[:, kc * 1024 + fc * 128: kc * 1024 + (kc + 1) * 0 + (fc + 1) * 128],
                              Qb[:, kc * TT:(kc + 1) * TT],
                              start=(kc == 0), stop=(kc == 1))
                s_mm.inc(ins)

        mmO = s_mm.n - 8

        @C.block.vector
        def _(v, mmO=mmO, t0=t0):
            for fc in range(8):
                v.wait_ge(s_mm.h, mmO + fc + 1)
                s_ev.inc(v.scalar_tensor_tensor(
                    Xc(C, fc, t0, TT), pbank[fc % 2 + 2][:, :], b2[:, fc:fc + 1],
                    Xc(C, fc, t0, TT), op0=ALU.add, op1=ALU.add))


def emit_L3_ffn(C):
    io = C.io
    lnx = C.sb("lnx3", [128, DC * T], BF16)
    emit_ln(C, lnx, kind="ln")
    s_dve = C.sem("dve")
    s_act = C.sem("act")
    s_ev = C.sem("ev")
    s_pe = C.sem("pe")
    s_mm = C.sem("mm")
    s_w = C.sem("w")
    s_fl = C.sem("fl")

    wq = C.sb("wq3", [128, 8 * 1024], BF16)
    wsc = C.sb("wsc3", [128, 8], F32)
    bia = C.sb("bia3", [128, 8], F32)
    idn = C.sb("idn3", [128, 128], BF16)

    @C.block.gpsimd
    def _(g):
        s_w.inc(g.dma_start(out=wq[:, :], in_=whole_w_ap(C, "wqT_i8", 1024, 1024)), 16)
        s_w.inc(g.dma_start(out=wsc[:, :], in_=io["wsc_fm"][:, :]), 16)
        s_w.inc(g.dma_start(out=bia[:, :], in_=io["bia_fm"][:, :]), 16)
        s_w.inc(g.dma_start(out=idn[:, :], in_=io["ident_bf"][:, :]), 16)

    S1 = C.sb("l3S1", [128, T], F32)
    SF = C.sb("l3SF", [64, T], F32)
    S3b = C.sb("l3S3b", [32, T], BF16)
    AM = C.sb("l3AM", [128, 16], F32)
    arow = C.ROWS[0:1, 1 * T: 2 * T]
    srow = C.ROWS[0:1, 0 * T: 1 * T]
    rrow = C.ROWS[0:1, 1 * T: 2 * T]

    FT = C.sb("l3FT", [128, T], F32)

    @C.block.vector
    def _(v):
        v.wait_ge(s_dve.h, s_dve.n)
        s_dve.inc(v.tensor_mul(S1[:, 0:T], lnx[:, 0:T], lnx[:, 0:T]))
        for dc in range(1, DC):
            s_dve.inc(v.tensor_mul(FT[:, :], lnx[:, dc * T:(dc + 1) * T],
                                   lnx[:, dc * T:(dc + 1) * T]))
            v.wait_ge(s_dve.h, s_dve.n)
            s_dve.inc(v.tensor_tensor(S1[:, 0:T], S1[:, 0:T], FT[:, :], op=ALU.max))
            v.wait_ge(s_dve.h, s_dve.n)


    @C.block.sync
    def _(sy):
        sy.wait_ge(s_dve.h, s_dve.n)
        s_fl.inc(sy.dma_start(out=SF[:, :], in_=S1[64:128, :]), 16)

    @C.block.vector
    def _(v):
        v.wait_ge(s_fl.h, s_fl.n)
        s_dve.inc(v.tensor_tensor(S1[0:64, :], S1[0:64, :], SF[0:64, :], op=ALU.max))

    @C.block.sync
    def _(sy):
        sy.wait_ge(s_dve.h, s_dve.n)
        s_fl.inc(sy.dma_start(out=SF[0:32, :], in_=S1[32:64, :]), 16)

    @C.block.vector
    def _(v):
        v.wait_ge(s_fl.h, s_fl.n)
        s_dve.inc(v.tensor_tensor(S1[0:32, :], S1[0:32, :], SF[0:32, :], op=ALU.max))
        v.wait_ge(s_dve.h, s_dve.n)
        s_dve.inc(v.tensor_copy(S3b[:, :], S1[0:32, :]))

    tr_base = s_pe.n

    @C.block.tensor
    def _(te):
        te.wait_ge(s_dve.h, s_dve.n)
        te.wait_ge(s_w.h, s_w.n)
        for j in range(16):
            s_pe.inc(te.transpose(C.PSB[:, j * 32:(j + 1) * 32], S3b[:, j * 128:(j + 1) * 128],
                                  idn[0:32, 0:32]))

    @C.block.vector
    def _(v):
        v.wait_ge(s_pe.h, s_pe.n)
        for j in range(16):
            s_dve.inc(v.tensor_reduce(AM[:, j:j + 1], C.PSB[:, j * 32:(j + 1) * 32],
                                      axis=AXX, op=ALU.max))

    @C.block.sync
    def _(sy):
        sy.wait_ge(s_dve.h, s_dve.n)
        for j in range(16):
            s_fl.inc(sy.dma_start(out=C.ROWS[0:1, 1 * T + j * 128: 1 * T + (j + 1) * 128], in_=AM[:, j:j + 1]), 16)

    @C.block.scalar
    def _(sc):
        sc.wait_ge(s_fl.h, s_fl.n)
        s_act.inc(sc.activation(out=arow, in_=arow, func=AF.Sqrt))

    @C.block.vector
    def _(v):
        v.wait_ge(s_act.h, s_act.n)
        s_dve.inc(v.tensor_scalar(srow, arow, 1e-6, 1.0 / 127.0,
                                  op0=ALU.max, op1=ALU.mult))
        v.wait_ge(s_dve.h, s_dve.n)
        s_dve.inc(v.reciprocal(rrow, srow))

    for i in range(NTT):
        @C.block.tensor
        def _(te, i=i):
            te.wait_ge(s_dve.h, s_dve.n)
            te.matmul(C.PS(5)[:, :], C.ONEROWF[:, :],
                      C.ROWS[0:1, 1 * T + i * TT: 1 * T + (i + 1) * TT],
                      start=True, stop=True)
            s_pe.inc(te.matmul(C.PS(6)[:, :], C.ONEROWF[:, :],
                               C.ROWS[0:1, 0 * T + i * TT: 0 * T + (i + 1) * TT],
                               start=True, stop=True))

        @C.block.vector
        def _(v, i=i):
            v.wait_ge(s_pe.h, s_pe.n)
            s_dve.inc(v.tensor_copy(C.RB[:, i * TT:(i + 1) * TT], C.PS(5)[:, :]))
            s_dve.inc(v.tensor_copy(C.MB[:, i * TT:(i + 1) * TT], C.PS(6)[:, :]))

    qa = C.LNTMP

    @C.block.vector
    def _(v):
        v.wait_ge(s_dve.h, s_dve.n)
        for dc in range(DC):
            v.wait_ge(s_dve.h, s_dve.n)
            s_dve.inc(v.tensor_tensor(qa[:, :], lnx[:, dc * T:(dc + 1) * T], C.RB[:, :],
                                      op=ALU.mult))
            v.wait_ge(s_dve.h, s_dve.n)
            s_dve.inc(v.tensor_scalar(qa[:, :], qa[:, :], MAGIC, MAGIC,
                                      op0=ALU.add, op1=ALU.subtract))
            v.wait_ge(s_dve.h, s_dve.n)
            s_dve.inc(v.tensor_scalar(lnx[:, dc * T:(dc + 1) * T], qa[:, :], -128.0, 127.0,
                                      op0=ALU.max, op1=ALU.min))

    qdve = s_dve.n
    mm_base = s_mm.n
    ev_base = s_ev.n

    @C.block.tensor
    def _(te):
        te.wait_ge(s_dve.h, qdve)
        for fc in range(DC):
            for i in range(NTT):
                t0 = i * TT
                bank = (fc * NTT + i) % 4
                if fc * NTT + i >= 4:
                    te.wait_ge(s_ev.h, ev_base + (fc * NTT + i - 4) + 1)
                for kc in range(DC):
                    ins = te.matmul(C.PS(bank)[:, :],
                                    wq[:, kc * 1024 + fc * 128: kc * 1024 + (fc + 1) * 128],
                                    lnx[:, kc * T + t0: kc * T + t0 + TT],
                                    start=(kc == 0), stop=(kc == DC - 1))
                s_mm.inc(ins)

    @C.block.vector
    def _(v):
        tmpe = qa
        for fc in range(DC):
            for i in range(NTT):
                idx = fc * NTT + i
                v.wait_ge(s_mm.h, mm_base + idx + 1)
                v.wait_ge(s_ev.h, s_ev.n)
                bank = idx % 4
                s_dve.inc(v.scalar_tensor_tensor(
                    tmpe[:, 0:TT], C.PS(bank)[:, :], wsc[:, fc:fc + 1],
                    C.MB[:, i * TT:(i + 1) * TT], op0=ALU.mult, op1=ALU.mult))
                v.wait_ge(s_dve.h, s_dve.n)
                s_ev.inc(v.scalar_tensor_tensor(
                    Xc(C, fc, i * TT, TT), tmpe[:, 0:TT], bia[:, fc:fc + 1],
                    Xc(C, fc, i * TT, TT), op0=ALU.add, op1=ALU.add))


def fold_scalar_max(C, col, colB, out11, tag):
    s_dve = C.sem("dve")
    s_fl = C.sem("fl")

    @C.block.sync
    def _(sy):
        sy.wait_ge(s_dve.h, s_dve.n)
        s_fl.inc(sy.dma_start(out=colB[0:64, :], in_=col[64:128, :]), 16)

    @C.block.vector
    def _(v):
        v.wait_ge(s_fl.h, s_fl.n)
        s_dve.inc(v.tensor_tensor(col[0:64, :], col[0:64, :], colB[0:64, :], op=ALU.max))

    @C.block.sync
    def _(sy):
        sy.wait_ge(s_dve.h, s_dve.n)
        s_fl.inc(sy.dma_start(out=colB[0:32, :], in_=col[32:64, :]), 16)

    @C.block.vector
    def _(v):
        v.wait_ge(s_fl.h, s_fl.n)
        s_dve.inc(v.tensor_tensor(col[0:32, :], col[0:32, :], colB[0:32, :], op=ALU.max))

    rowt = C.sb(f"fsm{tag}", [1, 32], F32)

    @C.block.sync
    def _(sy):
        sy.wait_ge(s_dve.h, s_dve.n)
        s_fl.inc(sy.dma_start(out=rowt[0:1, :], in_=col[0:32, :]), 16)

    @C.block.vector
    def _(v):
        v.wait_ge(s_fl.h, s_fl.n)
        s_dve.inc(v.tensor_reduce(out11, rowt[0:1, :], axis=AXX, op=ALU.max))


def allreduce_max_scalar(C, val11, out11, tag):
    nc = C.nc
    cci = C.sb(f"cci{tag}", [1, 8], F32)
    cco = C.sb(f"cco{tag}", [1, 8], F32)
    dri = nc.dram_tensor(f"cc_in_{tag}", [1, 8], F32)
    dro = nc.dram_tensor(f"cc_out_{tag}", [1, 8], F32)
    s_dve = C.sem("dve")
    s_cc = C.sem("cc")
    s_fl = C.sem("fl")
    s_swd = C.sem("swd")
    s_ms = C.sem("ms")

    @C.block.gpsimd
    def _(g):
        g.wait_ge(s_dve.h, s_dve.n)
        g.wait_ge(s_fl.h, s_fl.n)
        s_ms.inc(g.memset(cci[:, :], 0.0))
        g.wait_ge(s_ms.h, s_ms.n)
        s_swd.inc(g.dma_start(out=cci[0:1, 0:1], in_=val11), 16)
        g.wait_ge(s_swd.h, s_swd.n)
        s_swd.inc(g.dma_start(out=dri[:, :], in_=cci[:, :]), 16)
        g.wait_ge(s_swd.h, s_swd.n)
        s_cc.inc(g.collective_compute(
            "AllReduce", ALU.max, replica_groups=[list(range(NCORE))],
            ins=[dri.ap().opt()], outs=[dro.ap().opt()]))
        g.wait_ge(s_cc.h, s_cc.n)
        s_swd.inc(g.dma_start(out=cco[:, :], in_=dro[:, :]), 16)
        g.wait_ge(s_swd.h, s_swd.n)
        s_swd.inc(g.dma_start(out=out11, in_=cco[0:1, 0:1]), 16)


def bcast_scalar_col(C, val11, out_col, bank=5):
    s_pe = C.sem("pe")
    s_dve = C.sem("dve")

    @C.block.tensor
    def _(te):
        te.wait_ge(C.sem("fl").h, C.sem("fl").n)
        te.wait_ge(C.sem("swd").h, C.sem("swd").n)
        te.wait_ge(s_dve.h, s_dve.n)
        s_pe.inc(te.matmul(C.PS(bank)[:, 0:1], C.ONEROWF[:, :], val11, start=True, stop=True))

    @C.block.vector
    def _(v):
        v.wait_ge(s_pe.h, s_pe.n)
        s_dve.inc(v.tensor_copy(out_col, C.PS(bank)[:, 0:1]))


def emit_L2_ffn(C):
    io = C.io
    NI = 2816
    NIC = NI // 128
    lnx = C.sb("lnx2f", [128, DC * T], BF16)
    emit_ln(C, lnx, kind="ln")
    s_dve = C.sem("dve")
    s_act = C.sem("act")
    s_ev = C.sem("ev")
    s_pe = C.sem("pe")
    s_mm = C.sem("mm")
    s_w = C.sem("w")
    s_fl = C.sem("fl")
    s_hd = C.sem("hd")

    h_scr = C.nc.dram_tensor("h_scr", [NI, T], BF16)

    clipx = C.sb("l2cx", [1, 1], F32)
    clipd = C.sb("l2cd", [1, 1], F32)

    @C.block.gpsimd
    def _(g):
        s_w.inc(g.dma_start(out=clipx[:, :], in_=io["clip_x"][0:1, 0:1]), 16)
        s_w.inc(g.dma_start(out=clipd[:, :], in_=io["clip_down"][0:1, 0:1]), 16)

    wrow = C.sb("l2wrow", [128, NI], BF16)
    sgc = C.sb("l2sg", [128, NIC], F32)
    suc = C.sb("l2su", [128, NIC], F32)
    sdc = C.sb("l2sd", [128, DC], F32)
    seq = []
    for nm, nch, wid, ci in (("wg_oi", NIC, 1024, 0), ("wu_oi", NIC, 1024, 1),
                             ("wd_oi", DC, NI, 2)):
        for ch in range(nch):
            seq.append((nm, ch, wid, ci))
    cols = [sgc, suc, sdc]
    dve_w_base = s_dve.n

    for si, (nm, ch, wid, ci) in enumerate(seq):
        @C.block.gpsimd
        def _(g, si=si, nm=nm, ch=ch, wid=wid):
            if si >= 1:
                g.wait_ge(s_dve.h, dve_w_base + si)
            s_w.inc(g.dma_start(out=wrow[:, 0:wid],
                                in_=io[nm][ch * 128:(ch + 1) * 128, :]), 16)

        @C.block.vector
        def _(v, si=si, ch=ch, wid=wid, ci=ci):
            v.wait_ge(s_w.h, s_w.n)
            s_dve.inc(v.tensor_reduce(cols[ci][:, ch:ch + 1], wrow[:, 0:wid],
                                      axis=AXX, op=ALU.max, apply_absolute_value=True))

    S1 = C.RB
    FT = C.LNTMP
    xcol = C.sb("l2xcol", [128, 1], F32)
    colB = C.sb("l2colB", [64, 1], F32)
    gx11 = C.sb("l2gx", [1, 1], F32)
    sx11 = C.sb("l2sx", [1, 1], F32)
    sig11 = C.sb("l2sig", [1, 1], F32)

    @C.block.vector
    def _(v):
        v.wait_ge(s_dve.h, s_dve.n)
        s_dve.inc(v.tensor_mul(S1[:, 0:T], lnx[:, 0:T], lnx[:, 0:T]))
        for dc in range(1, DC):
            s_dve.inc(v.tensor_mul(FT[:, :], lnx[:, dc * T:(dc + 1) * T],
                                   lnx[:, dc * T:(dc + 1) * T]))
            v.wait_ge(s_dve.h, s_dve.n)
            s_dve.inc(v.tensor_tensor(S1[:, 0:T], S1[:, 0:T], FT[:, :], op=ALU.max))
            v.wait_ge(s_dve.h, s_dve.n)
        s_dve.inc(v.tensor_reduce(xcol[:, :], S1[:, 0:T], axis=AXX, op=ALU.max))

    fold_scalar_max(C, xcol[:, :], colB, gx11[0:1, 0:1], "x2")
    allreduce_max_scalar(C, gx11[0:1, 0:1], gx11[0:1, 0:1], "x")

    rsx_c = C.sb("l2rsx", [128, 1], F32)
    sx_c = C.sb("l2sxc", [128, 1], F32)

    @C.block.scalar
    def _(sc):
        sc.wait_ge(s_fl.h, s_fl.n)
        sc.wait_ge(C.sem("swd").h, C.sem("swd").n)
        sc.wait_ge(s_w.h, s_w.n)
        s_act.inc(sc.activation(out=gx11[0:1, 0:1], in_=gx11[0:1, 0:1], func=AF.Sqrt))
        s_act.inc(sc.activation(out=sig11[0:1, 0:1], in_=clipx[0:1, 0:1], func=AF.Sigmoid))

    @C.block.vector
    def _(v):
        v.wait_ge(s_act.h, s_act.n)
        s_dve.inc(v.tensor_mul(sx11[0:1, 0:1], gx11[0:1, 0:1], sig11[0:1, 0:1]))
        v.wait_ge(s_dve.h, s_dve.n)
        s_dve.inc(v.tensor_scalar(sx11[0:1, 0:1], sx11[0:1, 0:1], 1.0 / 127.0, None,
                                  op0=ALU.mult))

    bcast_scalar_col(C, sx11[0:1, 0:1], sx_c[:, :], bank=5)

    @C.block.vector
    def _(v):
        v.wait_ge(s_dve.h, s_dve.n)
        s_dve.inc(v.reciprocal(rsx_c[:, 0:1], sx_c[:, 0:1]))

    @C.block.vector
    def _(v):
        v.wait_ge(s_dve.h, s_dve.n)
        for dc in range(DC):
            v.wait_ge(s_dve.h, s_dve.n)
            s_dve.inc(v.tensor_scalar(FT[:, :], lnx[:, dc * T:(dc + 1) * T],
                                      rsx_c[:, 0:1], MAGIC, op0=ALU.mult, op1=ALU.add))
            v.wait_ge(s_dve.h, s_dve.n)
            s_dve.inc(v.tensor_scalar(FT[:, :], FT[:, :], MAGIC, -128.0,
                                      op0=ALU.subtract, op1=ALU.max))
            v.wait_ge(s_dve.h, s_dve.n)
            s_dve.inc(v.tensor_scalar(lnx[:, dc * T:(dc + 1) * T], FT[:, :], 127.0, None,
                                      op0=ALU.min))

    rgrow = C.sb("l2rg", [1, NI], BF16)
    rurow = C.sb("l2ru", [1, NI], BF16)
    rdrow = C.sb("l2rd", [1, 1024], BF16)
    rcol = C.sb("l2rc", [128, NIC], F32)
    rcolb = C.sb("l2rcb", [128, NIC], BF16)

    for cc, n, rrow in ((sgc, NIC, rgrow), (suc, NIC, rurow), (sdc, DC, rdrow)):
        @C.block.vector
        def _(v, cc=cc, n=n):
            v.wait_ge(s_dve.h, s_dve.n)
            v.wait_ge(s_fl.h, s_fl.n)
            s_dve.inc(v.tensor_scalar(cc[:, 0:n], cc[:, 0:n], 1.0 / 127.0, None,
                                      op0=ALU.mult))
            v.wait_ge(s_dve.h, s_dve.n)
            s_dve.inc(v.reciprocal(rcol[:, 0:n], cc[:, 0:n]))
            v.wait_ge(s_dve.h, s_dve.n)
            s_dve.inc(v.tensor_copy(rcolb[:, 0:n], rcol[:, 0:n]))

        @C.block.sync
        def _(sy, n=n, rrow=rrow):
            sy.wait_ge(s_dve.h, s_dve.n)
            for ch in range(n):
                s_fl.inc(sy.dma_start(out=rrow[0:1, ch * 128:(ch + 1) * 128],
                                      in_=rcolb[:, ch:ch + 1]), 16)

    @C.block.vector
    def _(v):
        v.wait_ge(s_fl.h, s_fl.n)
        for cc, n in ((sgc, NIC), (suc, NIC)):
            v.wait_ge(s_dve.h, s_dve.n)
            s_dve.inc(v.tensor_scalar(cc[:, 0:n], cc[:, 0:n], sx_c[:, 0:1], None,
                                      op0=ALU.mult))

    wqg = C.sb("l2wqg", [128, 8 * TT], BF16)
    wqu = C.sb("l2wqu", [128, 8 * TT], BF16)
    class _MBView:
        def __init__(self, base):
            self.base = base
        def __getitem__(self, sl):
            p, f = sl
            st = f.start if f.start is not None else 0
            sp = f.stop if f.stop is not None else TT
            return C.MB[p, self.base + st: self.base + sp]
    rqb = _MBView(0)
    qtmp = _MBView(TT)
    sgt = _MBView(2 * TT)
    sgt2 = _MBView(3 * TT)
    htile = C.sb("l2ht", [128, 2 * TT], BF16)
    hmcol = C.sb("l2hm", [128, 1], F32)
    gh11 = C.sb("l2gh", [1, 1], F32)
    sh11 = C.sb("l2sh", [1, 1], F32)

    s_ms = C.sem("ms")

    @C.block.gpsimd
    def _(g):
        g.wait_ge(s_dve.h, s_dve.n)
        s_ms.inc(g.memset(hmcol[:, :], 0.0))

    fcbs = [(b * 512, min(512, NI - b * 512)) for b in range((NI + 511) // 512)]
    for gi, (o0, ow) in enumerate(fcbs):
        nfc = ow // 128
        for wname, wdst, rrow_ in (("wgT", wqg, rgrow), ("wuT", wqu, rurow)):
            @C.block.gpsimd
            def _(g, wname=wname, o0=o0, ow=ow, wdst=wdst):
                g.wait_ge(s_dve.h, s_dve.n)
                g.wait_ge(s_mm.h, s_mm.n)
                s_w.inc(g.dma_start(out=wdst[:, 0:8 * ow],
                                    in_=bass.AP(io[wname], o0,
                                                [[NI, 128], [128 * NI, 8], [1, ow]])), 16)

            @C.block.tensor
            def _(te, o0=o0, ow=ow, rrow_=rrow_):
                te.wait_ge(s_dve.h, s_dve.n)
                s_pe.inc(te.matmul(C.PS(5)[:, 0:ow], C.ONEROWB[:, :],
                                   rrow_[0:1, o0:o0 + ow], start=True, stop=True))

            @C.block.vector
            def _(v, ow=ow, wdst=wdst):
                v.wait_ge(s_pe.h, s_pe.n)
                s_dve.inc(v.tensor_copy(rqb[:, 0:ow], C.PS(5)[:, 0:ow]))
                v.wait_ge(s_w.h, s_w.n)
                for kc in range(8):
                    v.wait_ge(s_dve.h, s_dve.n)
                    s_dve.inc(v.tensor_tensor(qtmp[:, 0:ow], wdst[:, kc * ow:(kc + 1) * ow],
                                              rqb[:, 0:ow], op=ALU.mult))
                    v.wait_ge(s_dve.h, s_dve.n)
                    s_dve.inc(v.tensor_scalar(wdst[:, kc * ow:(kc + 1) * ow], qtmp[:, 0:ow],
                                              MAGIC, MAGIC, op0=ALU.add, op1=ALU.subtract))

        gq_marks = {}
        t_idx0 = gi * 1000
        for lf in range(nfc):
            fc = o0 // 128 + lf
            for i in range(NTT):
                tx = t_idx0 + lf * NTT + i
                pr = (lf * NTT + i) % 2
                sgv = sgt if pr == 0 else sgt2
                htv = htile[:, pr * TT:(pr + 1) * TT]
                prev = gq_marks.get(tx - 2)

                @C.block.tensor
                def _(te, lf=lf, i=i, ow=ow, pr=pr, prev=prev):
                    if prev is None:
                        te.wait_ge(s_dve.h, s_dve.n)
                        te.wait_ge(s_ev.h, s_ev.n)
                        te.wait_ge(s_act.h, s_act.n)
                    else:
                        te.wait_ge(s_act.h, prev[0])
                        te.wait_ge(s_ev.h, prev[1])
                    for kc in range(DC):
                        te.matmul(C.PS(2 * pr)[:, :],
                                  wqg[:, kc * ow + lf * 128: kc * ow + lf * 128 + 128],
                                  lnx[:, kc * T + i * TT: kc * T + (i + 1) * TT],
                                  start=(kc == 0), stop=(kc == DC - 1))
                    for kc in range(DC):
                        ins = te.matmul(C.PS(1 + 2 * pr)[:, :],
                                        wqu[:, kc * ow + lf * 128: kc * ow + lf * 128 + 128],
                                        lnx[:, kc * T + i * TT: kc * T + (i + 1) * TT],
                                        start=(kc == 0), stop=(kc == DC - 1))
                    s_mm.inc(ins)

                mmt = s_mm.n

                @C.block.scalar
                def _(sc, fc=fc, pr=pr, sgv=sgv, mmt=mmt, prev=prev):
                    sc.wait_ge(s_mm.h, mmt)
                    if prev is not None:
                        sc.wait_ge(s_dve.h, prev[2])
                    s_act.inc(sc.activation(out=sgv[:, :], in_=C.PS(2 * pr)[:, :],
                                            func=AF.Silu, scale=sgc[:, fc:fc + 1]))

                act_m = s_act.n

                @C.block.vector
                def _(v, fc=fc, pr=pr, sgv=sgv, htv=htv, act_m=act_m):
                    v.wait_ge(s_act.h, act_m)
                    v.wait_ge(s_ms.h, s_ms.n)
                    v.wait_ge(s_hd.h, s_hd.n)
                    s_ev.inc(v.scalar_tensor_tensor(htv, C.PS(1 + 2 * pr)[:, :],
                                                    suc[:, fc:fc + 1], sgv[:, :],
                                                    op0=ALU.mult, op1=ALU.mult))
                    v.wait_ge(s_ev.h, s_ev.n)
                    s_dve.inc(v.tensor_mul(sgv[:, :], htv, htv))
                    v.wait_ge(s_dve.h, s_dve.n)
                    s_dve.inc(v.tensor_reduce(qtmp[:, 0:1], sgv[:, :], axis=AXX, op=ALU.max))
                    v.wait_ge(s_dve.h, s_dve.n)
                    s_dve.inc(v.tensor_tensor(hmcol[:, 0:1], hmcol[:, 0:1], qtmp[:, 0:1],
                                              op=ALU.max))

                gq_marks[tx] = (s_act.n, s_ev.n, s_dve.n)

                @C.block.gpsimd
                def _(g, fc=fc, i=i, htv=htv):
                    g.wait_ge(s_ev.h, s_ev.n)
                    s_hd.inc(g.dma_start(out=h_scr[fc * 128:(fc + 1) * 128,
                                                   i * TT:(i + 1) * TT],
                                         in_=htv), 16)

    fold_scalar_max(C, hmcol[:, :], colB, gh11[0:1, 0:1], "h2")
    allreduce_max_scalar(C, gh11[0:1, 0:1], gh11[0:1, 0:1], "h")
    rsh_c = C.sb("l2rsh", [128, 1], F32)
    sh_c = C.sb("l2shc", [128, 1], F32)

    @C.block.scalar
    def _(sc):
        sc.wait_ge(s_fl.h, s_fl.n)
        sc.wait_ge(C.sem("swd").h, C.sem("swd").n)
        s_act.inc(sc.activation(out=gh11[0:1, 0:1], in_=gh11[0:1, 0:1], func=AF.Sqrt))
        s_act.inc(sc.activation(out=sig11[0:1, 0:1], in_=clipd[0:1, 0:1], func=AF.Sigmoid))

    @C.block.vector
    def _(v):
        v.wait_ge(s_act.h, s_act.n)
        s_dve.inc(v.tensor_mul(sh11[0:1, 0:1], gh11[0:1, 0:1], sig11[0:1, 0:1]))
        v.wait_ge(s_dve.h, s_dve.n)
        s_dve.inc(v.tensor_scalar(sh11[0:1, 0:1], sh11[0:1, 0:1], 1.0 / 127.0, None,
                                  op0=ALU.mult))

    bcast_scalar_col(C, sh11[0:1, 0:1], sh_c[:, :], bank=5)

    @C.block.vector
    def _(v):
        v.wait_ge(s_dve.h, s_dve.n)
        s_dve.inc(v.reciprocal(rsh_c[:, 0:1], sh_c[:, 0:1]))
        v.wait_ge(s_dve.h, s_dve.n)
        s_dve.inc(v.tensor_scalar(sdc[:, 0:DC], sdc[:, 0:DC], sh_c[:, 0:1], None,
                                  op0=ALU.mult))

    hq = C.sb("l2hq", [128, NIC * TT], BF16)
    for fcb in range(2):
        o0 = fcb * 512

        @C.block.gpsimd
        def _(g, o0=o0):
            g.wait_ge(s_dve.h, s_dve.n)
            g.wait_ge(s_mm.h, s_mm.n)
            s_w.inc(g.dma_start(out=lnx[:, 0:NIC * TT],
                                in_=bass.AP(io["wdT"], o0,
                                            [[1024, 128], [128 * 1024, NIC], [1, 512]])), 16)

        @C.block.tensor
        def _(te, o0=o0):
            te.wait_ge(s_dve.h, s_dve.n)
            s_pe.inc(te.matmul(C.PS(5)[:, :], C.ONEROWB[:, :], rdrow[0:1, o0:o0 + 512],
                               start=True, stop=True))

        @C.block.vector
        def _(v):
            v.wait_ge(s_pe.h, s_pe.n)
            s_dve.inc(v.tensor_copy(rqb[:, :], C.PS(5)[:, :]))
            v.wait_ge(s_w.h, s_w.n)
            for kc in range(NIC):
                v.wait_ge(s_dve.h, s_dve.n)
                s_dve.inc(v.tensor_tensor(qtmp[:, :], lnx[:, kc * TT:(kc + 1) * TT],
                                          rqb[:, :], op=ALU.mult))
                v.wait_ge(s_dve.h, s_dve.n)
                s_dve.inc(v.tensor_scalar(lnx[:, kc * TT:(kc + 1) * TT], qtmp[:, :],
                                          MAGIC, MAGIC, op0=ALU.add, op1=ALU.subtract))

        for i in range(NTT):
            hd_base = s_hd.n

            @C.block.gpsimd
            def _(g, i=i):
                g.wait_ge(s_mm.h, s_mm.n)
                for kc in range(NIC):
                    s_hd.inc(g.dma_start(out=hq[:, kc * TT:(kc + 1) * TT],
                                         in_=h_scr[kc * 128:(kc + 1) * 128,
                                                   i * TT:(i + 1) * TT]), 16)

            @C.block.vector
            def _(v, hd_base=hd_base):
                v.wait_ge(s_hd.h, hd_base + 16 * NIC)
                for kc in range(NIC):
                    v.wait_ge(s_dve.h, s_dve.n)
                    s_dve.inc(v.tensor_scalar(qtmp[:, :], hq[:, kc * TT:(kc + 1) * TT],
                                              rsh_c[:, 0:1], MAGIC, op0=ALU.mult, op1=ALU.add))
                    v.wait_ge(s_dve.h, s_dve.n)
                    s_dve.inc(v.tensor_scalar(qtmp[:, :], qtmp[:, :], MAGIC, -128.0,
                                              op0=ALU.subtract, op1=ALU.max))
                    v.wait_ge(s_dve.h, s_dve.n)
                    s_dve.inc(v.tensor_scalar(hq[:, kc * TT:(kc + 1) * TT], qtmp[:, :],
                                              127.0, None, op0=ALU.min))

            for lf in range(4):
                fc = fcb * 4 + lf

                @C.block.tensor
                def _(te, lf=lf, i=i):
                    te.wait_ge(s_dve.h, s_dve.n)
                    te.wait_ge(s_ev.h, s_ev.n)
                    for kc in range(NIC):
                        ins = te.matmul(C.PS(lf % 2)[:, :],
                                        lnx[:, kc * TT + lf * 128: kc * TT + lf * 128 + 128],
                                        hq[:, kc * TT:(kc + 1) * TT],
                                        start=(kc == 0), stop=(kc == NIC - 1))
                    s_mm.inc(ins)

                @C.block.vector
                def _(v, lf=lf, fc=fc, i=i):
                    v.wait_ge(s_mm.h, s_mm.n)
                    s_ev.inc(v.scalar_tensor_tensor(
                        Xc(C, fc, i * TT, TT), C.PS(lf % 2)[:, :], sdc[:, fc:fc + 1],
                        Xc(C, fc, i * TT, TT), op0=ALU.mult, op1=ALU.add))


def emit_L0_ffn(C):
    io = C.io
    E, R, FF = 8, 64, 4096
    lnx = C.sb("lnx0f", [128, DC * T], BF16)
    emit_ln(C, lnx, kind="ln")
    s_dve = C.sem("dve")
    s_act = C.sem("act")
    s_ev = C.sem("ev")
    s_pe = C.sem("pe")
    s_mm = C.sem("mm")
    s_w = C.sem("w")
    s_fl = C.sem("fl")

    w1at = C.sb("w1at", [128, 8 * 512], BF16)
    w1be = C.sb("w1be", [64, 2 * FF], BF16)
    w2ate = C.sb("w2ate", [128, 2 * 32 * 64], BF16)
    w2be = C.sb("w2be", [64, 2 * 1024], BF16)

    s_wc = C.sem("wc")
    s_we = [C.sem("we0"), C.sem("we1")]

    @C.block.gpsimd
    def _(g):
        s_wc.inc(g.dma_start(out=w1at[:, :], in_=whole_w_ap(C, "w1AT", 1024, 512)), 16)

    ew2 = C.sb("ew2sb", [128, 32], F32)
    eif = C.sb("eifsb", [128, 32], F32)
    ct1 = C.sb("ct1", [128, 32], F32)
    CC = C.sb("CC", [128, 8 * 16], F32)

    @C.block.gpsimd
    def _(g):
        s_wc.inc(g.dma_start(out=ew2[:, :], in_=io["ew2"][:, :]), 16)
        s_wc.inc(g.dma_start(out=eif[:, :], in_=io["ei2"][:, :]), 16)

    one_ap = C.constf(1.0)
    negs = [C.constf(-float(e)) for e in range(E)]

    for e in range(E):
        @C.block.scalar
        def _(sc, e=e):
            if e == 0:
                sc.wait_ge(s_wc.h, s_wc.n)
                sc.wait_ge(C.sem("ini").h, C.sem("ini").n)
            else:
                sc.wait_ge(s_dve.h, s_dve.n)
            s_act.inc(sc.activation(out=ct1[:, :], in_=eif[:, :], func=AF.Abs,
                                    bias=negs[e]))
            sc.wait_ge(s_act.h, s_act.n)
            s_act.inc(sc.activation(out=ct1[:, :], in_=ct1[:, :], func=AF.Relu,
                                    scale=-1.0, bias=one_ap))

        @C.block.vector
        def _(v, e=e):
            v.wait_ge(s_act.h, s_act.n)
            s_dve.inc(v.tensor_mul(ct1[:, :], ct1[:, :], ew2[:, :]))
            v.wait_ge(s_dve.h, s_dve.n)
            s_dve.inc(v.tensor_tensor(
                CC[:, e * 16:(e + 1) * 16],
                bass.AP(ct1, 0, [[32, 128], [2, 16]]),
                bass.AP(ct1, 1, [[32, 128], [2, 16]]), op=ALU.add))

    h1b = C.sb("l0h1", [64, TT], BF16)
    h2b = C.sb("l0h2", [128, 2 * TT], BF16)
    h3c = C.sb("l0h3", [64, TT], BF16)
    crow = C.row(0)

    for e in range(E):
        slot = e % 2

        @C.block.gpsimd
        def _(g, e=e, slot=slot):
            if e >= 2:
                g.wait_ge(s_mm.h, s_mm.n)
            s_we[slot].inc(g.dma_start(out=w1be[:, slot * FF:(slot + 1) * FF],
                                in_=io["w1B"][e * 64:(e + 1) * 64, :]), 16)
            s_we[slot].inc(g.dma_start(out=w2ate[:, slot * 2048:(slot + 1) * 2048],
                                in_=bass.AP(io["w2AT"], e * 64,
                                            [[512, 128], [128 * 512, 32], [1, 64]])), 16)
            s_we[slot].inc(g.dma_start(out=w2be[:, slot * 1024:(slot + 1) * 1024],
                                in_=io["w2B"][e * 64:(e + 1) * 64, :]), 16)

        @C.block.sync
        def _(sy, e=e):
            sy.wait_ge(s_dve.h, s_dve.n)
            sy.wait_ge(C.sem("pe").h, C.sem("pe").n)
            for ch in range(16):
                s_fl.inc(sy.dma_start(out=C.ROWS[0:1, ch * 128:(ch + 1) * 128],
                                      in_=CC[:, e * 16 + ch: e * 16 + ch + 1]), 16)

        for i in range(NTT):
            @C.block.tensor
            def _(te, e=e, i=i):
                if i == 0:
                    te.wait_ge(s_fl.h, s_fl.n)
                te.wait_ge(s_dve.h, s_dve.n)
                s_pe.inc(te.matmul(C.PS(6, rows=64)[:, :], C.ONEROWF[0:1, 0:64],
                          C.ROWS[0:1, i * TT:(i + 1) * TT], start=True, stop=True))

            @C.block.vector
            def _(v, i=i):
                v.wait_ge(s_pe.h, s_pe.n)
                s_dve.inc(v.tensor_copy(C.RB[0:64, i * TT:(i + 1) * TT],
                                        C.PS(6, rows=64)[:, :]))

        for i in range(NTT):
            t0 = i * TT

            @C.block.tensor
            def _(te, e=e, i=i, t0=t0, slot=slot):
                te.wait_ge(s_wc.h, s_wc.n)
                te.wait_ge(s_we[slot].h, 48 * (e // 2 + 1))
                te.wait_ge(s_dve.h, s_dve.n)
                te.wait_ge(s_ev.h, s_ev.n)
                for kc in range(DC):
                    ins = te.matmul(C.PS(0, rows=64)[:, :],
                                    w1at[:, kc * 512 + e * 64: kc * 512 + (e + 1) * 64],
                                    lnx[:, kc * T + t0: kc * T + t0 + TT],
                                    start=(kc == 0), stop=(kc == DC - 1))
                s_mm.inc(ins)

            mmh1 = s_mm.n

            @C.block.vector
            def _(v, mmh1=mmh1):
                v.wait_ge(s_mm.h, mmh1)
                s_ev.inc(v.tensor_copy(h1b[:, :], C.PS(0, rows=64)[:, :]))

            evh1 = s_ev.n
            act_base0 = s_act.n
            mm_base0 = s_mm.n

            @C.block.tensor
            def _(te, evh1=evh1, act_base0=act_base0, slot=slot):
                te.wait_ge(s_ev.h, evh1)
                for fc in range(2):
                    s_mm.inc(te.matmul(C.PS(1 + fc)[:, :],
                                       w1be[:, slot * FF + fc * 128: slot * FF + (fc + 1) * 128],
                                       h1b[:, :], start=True, stop=True))
                for fc in range(32):
                    te.wait_ge(s_act.h, act_base0 + fc + 1)
                    s_mm.inc(te.matmul(C.PS(3)[0:64, :],
                                       w2ate[:, slot * 2048 + fc * 64: slot * 2048 + (fc + 1) * 64],
                                       h2b[:, (fc % 2) * TT:(fc % 2 + 1) * TT],
                                       start=(fc == 0), stop=(fc == 31)))
                    if fc + 2 < 32:
                        s_mm.inc(te.matmul(C.PS(1 + fc % 2)[:, :],
                                           w1be[:, slot * FF + (fc + 2) * 128: slot * FF + (fc + 3) * 128],
                                           h1b[:, :], start=True, stop=True))

            @C.block.scalar
            def _(sc, act_base0=act_base0, mm_base0=mm_base0):
                for fc in range(32):
                    qk = mm_base0 + (fc + 1 if fc < 2 else 2 * fc)
                    sc.wait_ge(s_mm.h, qk)
                    s_act.inc(sc.activation(out=h2b[:, (fc % 2) * TT:(fc % 2 + 1) * TT],
                                            in_=C.PS(1 + fc % 2)[:, :], func=AF.Gelu))

            @C.block.vector
            def _(v, i=i):
                v.wait_ge(s_mm.h, s_mm.n)
                s_ev.inc(v.tensor_tensor(h3c[:, :], C.PS(3)[0:64, :],
                                         C.RB[0:64, i * TT:(i + 1) * TT], op=ALU.mult))

            evh3 = s_ev.n
            mm_base1 = s_mm.n

            @C.block.tensor
            def _(te, evh3=evh3, e=e):
                te.wait_ge(s_ev.h, evh3)
                slot2 = e % 2
                for fc in range(DC):
                    if fc >= 2:
                        te.wait_ge(s_ev.h, evh3 + fc - 1)
                    s_mm.inc(te.matmul(C.PS(4 + fc % 2)[:, :],
                                       w2be[:, slot2 * 1024 + fc * 128: slot2 * 1024 + (fc + 1) * 128],
                                       h3c[:, :], start=True, stop=True))

            @C.block.vector
            def _(v, mm_base1=mm_base1, i=i, t0=t0):
                for fc in range(DC):
                    v.wait_ge(s_mm.h, mm_base1 + fc + 1)
                    s_ev.inc(v.tensor_tensor(Xc(C, fc, t0, TT), Xc(C, fc, t0, TT),
                                             C.PS(4 + fc % 2)[:, :], op=ALU.add))


def emit_L2_attn(C):
    io = C.io
    DI = 2048
    NDD = DI // 128
    CH = 128
    NCH = T // CH
    HT = T // 2
    s_dve = C.sem("dve")
    s_act = C.sem("act")
    s_ev = C.sem("ev")
    s_pe = C.sem("pe")
    s_mm = C.sem("mm")
    s_w = C.sem("w")
    s_fl = C.sem("fl")
    s_cc = C.sem("cc")
    s_swd = C.sem("swd")
    s_ms = C.sem("ms")
    s_xs = C.sem("xs")
    s_ab = C.sem("abcmm")

    lnu = C.sb("l2a_u", [128, DC * T], BF16)
    emit_ln(C, lnu, kind="ln")

    xin_scr = C.nc.dram_tensor("xin_scr", [DI, T], BF16)
    sag_in = C.nc.dram_tensor("l2a_ag_in", [16, DI], F32)
    sag_out = C.nc.dram_tensor("l2a_ag_out", [32, DI], F32)

    aT = C.sb("l2a_aT", [16, T], F32)
    bT = C.sb("l2a_bT", [16, T], BF16)
    cT = C.sb("l2a_cT", [16, T], BF16)
    gcum = C.sb("l2a_gc", [16, T], F32)
    ones16c = C.sb("l2a_o16c", [16, CH], F32)
    wabc = C.sb("l2a_wabc", [128, NDD * 48], BF16)
    win = C.sb("l2a_win", [128, 2 * DC * 256], BF16)
    S = C.sb("l2a_S", [16, DI], F32)
    Sb = C.sb("l2a_Sb", [16, 512], BF16)
    xing = C.sb("l2a_xing", [128, 512], BF16)
    xt = C.sb("l2a_xt", [128, 512], BF16)
    gbt = C.sb("l2a_gbt", [128, 32], F32)
    bdt_all = C.sb("l2a_bdt", [128, NCH * 16], BF16)
    ctb_all = C.sb("l2a_ctb", [16, NCH * CH], BF16)
    egend_all = C.sb("l2a_eg", [16, NCH], F32)
    wball = C.sb("l2a_wball", [128, NCH * CH], BF16)
    tmp16 = C.sb("l2a_t16", [16, CH], F32)
    tmp16b = C.sb("l2a_t16b", [16, CH], BF16)
    maskt = C.sb("l2a_mask", [128, 128], F32)
    issec = C.sb("l2a_isec", [128, 1], F32)
    idf = C.sb("l2a_idf", [128, 128], F32)
    idb = C.sb("l2a_idb", [128, 128], BF16)
    nwc = C.sb("l2a_nw", [128, NDD], F32)
    xtile = C.sb("l2a_xtl", [128, 2 * TT], BF16)
    DD = C.MB
    WACC = C.RB
    wo = None
    sg1 = C.MB[:, 0:TT]
    sg2 = C.MB[:, TT:2 * TT]

    @C.block.gpsimd
    def _(g):
        s_w.inc(g.dma_start(out=wabc[:, :], in_=whole_w_ap(C, "WABC", DI, 48)), 16)
        s_w.inc(g.dma_start(out=idf[:, :], in_=io["ident_f32"][:, :]), 16)
        s_w.inc(g.dma_start(out=idb[:, :], in_=io["ident_bf"][:, :]), 16)
        s_w.inc(g.dma_start(out=maskt[:, :], in_=io["mask_lt"][:, :]), 16)
        s_w.inc(g.dma_start(out=issec[:, :], in_=io["is_second"][:, :]), 16)
        s_ms.inc(g.memset(ones16c[:, :], 1.0))
        s_w.inc(g.dma_start(out=nwc[:, :], in_=io["normw_fm"][:, :]), 16)
        s_ms.inc(g.memset(S[:, :], 0.0))

    for pg in range(8):
        @C.block.gpsimd
        def _(g, pg=pg):
            if pg >= 1:
                g.wait_ge(s_mm.h, s_mm.n)
            s_w.inc(g.dma_start(out=win[:, 0:DC * 256],
                                in_=bass.AP(io["W_in_a"], pg * 256,
                                            [[DI, 128], [128 * DI, DC], [1, 256]])), 16)
            s_w.inc(g.dma_start(out=win[:, DC * 256:2 * DC * 256],
                                in_=bass.AP(io["W_in_b"], pg * 256,
                                            [[DI, 128], [128 * DI, DC], [1, 256]])), 16)

        for i in range(NTT):
            @C.block.tensor
            def _(te, pg=pg, i=i):
                te.wait_ge(s_w.h, s_w.n)
                te.wait_ge(s_ev.h, s_ev.n)
                if pg == 0 and i == 0:
                    te.wait_ge(s_dve.h, s_dve.n)
                    te.wait_ge(s_ms.h, s_ms.n)
                for half in range(2):
                    for j in range(2):
                        for kc in range(DC):
                            ins = te.matmul(
                                C.PS(2 * half + j)[:, :],
                                win[:, half * DC * 256 + kc * 256 + j * 128:
                                    half * DC * 256 + kc * 256 + (j + 1) * 128],
                                lnu[:, kc * T + i * TT: kc * T + (i + 1) * TT],
                                start=(kc == 0), stop=(kc == DC - 1))
                s_mm.inc(ins)

            for j in range(2):
                dd = pg * 2 + j
                slot = j

                @C.block.scalar
                def _(sc, j=j):
                    sc.wait_ge(s_mm.h, s_mm.n)
                    sc.wait_ge(s_dve.h, s_dve.n)
                    s_act.inc(sc.activation(out=sg1, in_=C.PS(j)[:, :], func=AF.Silu))
                    s_act.inc(sc.activation(out=sg2, in_=C.PS(2 + j)[:, :],
                                            func=AF.Sigmoid))

                @C.block.vector
                def _(v, slot=slot):
                    v.wait_ge(s_act.h, s_act.n)
                    v.wait_ge(s_xs.h, s_xs.n)
                    v.wait_ge(s_ab.h, s_ab.n)
                    s_ev.inc(v.tensor_mul(xtile[:, slot * TT:(slot + 1) * TT], sg1, sg2))

                @C.block.tensor
                def _(te, dd=dd, i=i, pg=pg, j=j, slot=slot):
                    te.wait_ge(s_ev.h, s_ev.n)
                    if pg + j > 0:
                        te.wait_ge(s_dve.h, s_dve.n)
                    for q in range(3):
                        ins = te.matmul(C.PS(4 + q)[0:16, :],
                                        wabc[:, dd * 48 + q * 16: dd * 48 + (q + 1) * 16],
                                        xtile[:, slot * TT:(slot + 1) * TT],
                                        start=True, stop=True)
                        s_ab.inc(ins)

                @C.block.vector
                def _(v, dd=dd, i=i, pg=pg, j=j, slot=slot):
                    v.wait_ge(s_ab.h, s_ab.n)
                    for q, dstt in enumerate((aT, bT, cT)):
                        if pg == 0 and j == 0:
                            s_dve.inc(v.tensor_copy(dstt[:, i * TT:(i + 1) * TT],
                                                    C.PS(4 + q)[0:16, :]))
                        else:
                            s_dve.inc(v.tensor_tensor(dstt[:, i * TT:(i + 1) * TT],
                                                      dstt[:, i * TT:(i + 1) * TT],
                                                      C.PS(4 + q)[0:16, :], op=ALU.add))

                @C.block.gpsimd
                def _(g, dd=dd, i=i, slot=slot):
                    g.wait_ge(s_ev.h, s_ev.n)
                    s_xs.inc(g.dma_start(out=xin_scr[dd * 128:(dd + 1) * 128,
                                                     i * TT:(i + 1) * TT],
                                         in_=xtile[:, slot * TT:(slot + 1) * TT]), 16)

    @C.block.vector
    def _(v):
        v.wait_ge(s_dve.h, s_dve.n)
        s_dve.inc(v.tensor_scalar(aT[:, :], aT[:, :], -5.0, 0.0,
                                  op0=ALU.max, op1=ALU.min))

    @C.block.scalar
    def _(sc):
        sc.wait_ge(s_dve.h, s_dve.n)
        s_act.inc(sc.activation(out=aT[:, :], in_=aT[:, :], func=AF.Exp))

    @C.block.vector
    def _(v):
        v.wait_ge(s_act.h, s_act.n)
        s_dve.inc(v.tensor_scalar(aT[:, :], aT[:, :], -1.0, None, op0=ALU.mult))
        for cc_ in range(NCH):
            v.wait_ge(s_dve.h, s_dve.n)
            s_dve.inc(v.tensor_tensor_scan(gcum[:, cc_ * CH:(cc_ + 1) * CH],
                                           ones16c[:, :], aT[:, cc_ * CH:(cc_ + 1) * CH],
                                           0.0, op0=ALU.mult, op1=ALU.add))

    for c in range(NCH):
        c0 = c * CH

        @C.block.sync
        def _(sy, c=c, c0=c0):
            sy.wait_ge(s_dve.h, s_dve.n)
            if c > 0:
                sy.wait_ge(s_pe.h, s_pe.n)
            s_fl.inc(sy.dma_start(out=C.ROWS[0:1, T:T + 2048], in_=gcum[:, c0:c0 + CH]), 16)

        @C.block.tensor
        def _(te, c=c, c0=c0):
            te.wait_ge(s_fl.h, s_fl.n)
            te.wait_ge(s_dve.h, s_dve.n)
            te.transpose(C.PS(6)[:, 0:16], gcum[0:16, c0:c0 + CH], idf[0:16, 0:16])
            s_pe.inc(te.transpose(C.PSB[:, 16:32], bT[:, c0:c0 + CH],
                                  idb[0:16, 0:16]))
            for q in range(4):
                ins = te.matmul(C.PS(q)[:, :], C.ONEROWF[:, :],
                                C.ROWS[0:1, T + q * 512: T + (q + 1) * 512],
                                start=True, stop=True)
            s_pe.inc(ins)

        @C.block.vector
        def _(v, c=c):
            v.wait_ge(s_pe.h, s_pe.n)
            s_dve.inc(v.tensor_copy(gbt[:, 0:16], C.PS(6)[:, 0:16]))
            s_dve.inc(v.tensor_copy(gbt[:, 16:32], C.PSB[:, 16:32]))
            for n in range(16):
                v.wait_ge(s_dve.h, s_dve.n)
                s_dve.inc(v.scalar_tensor_tensor(
                    DD[:, n * 128:(n + 1) * 128],
                    C.PS(n // 4)[:, (n % 4) * 128:(n % 4 + 1) * 128],
                    gbt[:, n:n + 1], maskt[:, :], op0=ALU.subtract, op1=ALU.add))

        @C.block.scalar
        def _(sc):
            sc.wait_ge(s_dve.h, s_dve.n)
            s_act.inc(sc.activation(out=DD[:, :], in_=DD[:, :], func=AF.Exp))

        @C.block.gpsimd
        def _(g, c=c, c0=c0):
            g.wait_ge(s_act.h, s_act.n)
            g.wait_ge(s_pe.h, s_pe.n)
            s_swd.inc(g.dma_start(out=C.ROWS[0:1, 0:2048],
                                  in_=cT[:, c0:c0 + CH]), 16)

        @C.block.tensor
        def _(te):
            te.wait_ge(s_swd.h, s_swd.n)
            te.wait_ge(s_act.h, s_act.n)
            for q in range(4):
                ins = te.matmul(C.PS(q)[:, :], C.ONEROWF[:, :],
                                C.ROWS[0:1, 0 + q * 512: 0 + (q + 1) * 512],
                                start=True, stop=True)
            s_pe.inc(ins)

        @C.block.vector
        def _(v, c=c):
            v.wait_ge(s_pe.h, s_pe.n)
            for n in range(16):
                s_dve.inc(v.scalar_tensor_tensor(
                    WACC[:, n * 128:(n + 1) * 128], DD[:, n * 128:(n + 1) * 128],
                    gbt[:, 16 + n:16 + n + 1],
                    C.PS(n // 4)[:, (n % 4) * 128:(n % 4 + 1) * 128],
                    op0=ALU.mult, op1=ALU.mult))
            for wdt in (1024, 512, 256, 128):
                v.wait_ge(s_dve.h, s_dve.n)
                s_dve.inc(v.tensor_tensor(WACC[:, 0:wdt], WACC[:, 0:wdt],
                                          WACC[:, wdt:2 * wdt], op=ALU.add))
            v.wait_ge(s_dve.h, s_dve.n)
            s_dve.inc(v.tensor_copy(wball[:, c * CH:(c + 1) * CH], WACC[:, 0:128]))

        @C.block.scalar
        def _(sc, c=c, c0=c0):
            sc.wait_ge(s_dve.h, s_dve.n)
            s_act.inc(sc.activation(out=tmp16[:, :], in_=gcum[0:16, c0:c0 + CH],
                                    func=AF.Exp))

        @C.block.vector
        def _(v, c=c, c0=c0):
            v.wait_ge(s_act.h, s_act.n)
            s_dve.inc(v.tensor_mul(tmp16[:, :], tmp16[:, :], cT[:, c0:c0 + CH]))
            v.wait_ge(s_dve.h, s_dve.n)
            s_dve.inc(v.tensor_copy(ctb_all[:, c * CH:(c + 1) * CH], tmp16[:, :]))

        @C.block.scalar
        def _(sc, c=c, c0=c0):
            sc.wait_ge(s_dve.h, s_dve.n)
            s_act.inc(sc.activation(out=tmp16[:, :], in_=gcum[0:16, c0:c0 + CH],
                                    func=AF.Exp, scale=-1.0,
                                    bias=gcum[0:16, c0 + CH - 1:c0 + CH]))
            s_act.inc(sc.activation(out=egend_all[0:16, c:c + 1],
                                    in_=gcum[0:16, c0 + CH - 1:c0 + CH], func=AF.Exp))

        @C.block.vector
        def _(v, c=c, c0=c0):
            v.wait_ge(s_act.h, s_act.n)
            s_dve.inc(v.tensor_mul(tmp16[:, :], tmp16[:, :], bT[:, c0:c0 + CH]))
            v.wait_ge(s_dve.h, s_dve.n)
            s_dve.inc(v.tensor_copy(tmp16b[:, :], tmp16[:, :]))

        @C.block.tensor
        def _(te, c=c):
            te.wait_ge(s_dve.h, s_dve.n)
            s_pe.inc(te.transpose(C.PSB[:, 0:16], tmp16b[:, :], idb[0:16, 0:16]))

        @C.block.vector
        def _(v, c=c):
            v.wait_ge(s_pe.h, s_pe.n)
            s_dve.inc(v.tensor_copy(bdt_all[:, c * 16:(c + 1) * 16], C.PSB[:, 0:16]))

    def xt_load(ddg, c):
        c0 = c * CH

        @C.block.gpsimd
        def _(g, ddg=ddg, c0=c0):
            g.wait_ge(s_mm.h, s_mm.n)
            s_xs.inc(g.dma_start(
                out=xing[:, :],
                in_=bass.AP(xin_scr, (ddg * 4 * 128) * T + c0,
                            [[T, 128], [128 * T, 4], [1, CH]])), 16)

        @C.block.tensor
        def _(te):
            te.wait_ge(s_xs.h, s_xs.n)
            te.wait_ge(s_dve.h, s_dve.n)
            for j in range(4):
                ins = te.transpose(C.PSB[:, j * 128:(j + 1) * 128],
                                   xing[:, j * 128:(j + 1) * 128], idb[:, :])
            s_pe.inc(ins)

        @C.block.vector
        def _(v):
            v.wait_ge(s_pe.h, s_pe.n)
            s_dve.inc(v.tensor_copy(xt[:, :], C.PSB[:, 0:512]))

    def state_step(ddg, c):
        @C.block.tensor
        def _(te, ddg=ddg, c=c):
            te.wait_ge(s_dve.h, s_dve.n)
            ins = te.matmul(C.PS(6, rows=16)[:, :], bdt_all[:, c * 16:(c + 1) * 16],
                            xt[:, 0:512], start=True, stop=True)
            s_mm.inc(ins)

        @C.block.vector
        def _(v, ddg=ddg, c=c):
            v.wait_ge(s_mm.h, s_mm.n)
            s_dve.inc(v.tensor_scalar(S[0:16, ddg * 512:(ddg + 1) * 512],
                                      S[0:16, ddg * 512:(ddg + 1) * 512],
                                      egend_all[0:16, c:c + 1], None, op0=ALU.mult))
            v.wait_ge(s_dve.h, s_dve.n)
            s_dve.inc(v.tensor_tensor(S[0:16, ddg * 512:(ddg + 1) * 512],
                                      S[0:16, ddg * 512:(ddg + 1) * 512],
                                      C.PS(6, rows=16)[:, :], op=ALU.add))

    for ddg in range(4):
        for c in range(NCH):
            xt_load(ddg, c)
            state_step(ddg, c)

    @C.block.gpsimd
    def _(g):
        g.wait_ge(s_dve.h, s_dve.n)
        s_swd.inc(g.dma_start(out=sag_in[:, :], in_=S[0:16, :]), 16)
        g.wait_ge(s_swd.h, s_swd.n)
        s_cc.inc(g.collective_compute(
            "AllGather", ALU.bypass, replica_groups=[[0, 1], [2, 3], [4, 5], [6, 7]],
            ins=[sag_in.ap().opt()], outs=[sag_out.ap().opt()]))
        g.wait_ge(s_cc.h, s_cc.n)
        s_swd.inc(g.dma_start(out=S[0:16, :], in_=sag_out[0:16, :]), 16)

    @C.block.vector
    def _(v):
        v.wait_ge(s_swd.h, s_swd.n)
        s_dve.inc(v.tensor_scalar(S[0:16, :], S[0:16, :], issec[0:16, 0:1], None,
                                  op0=ALU.mult))

    for h in range(2):
        hc0 = h * (NCH // 2)

        for ddg in range(4):
            for lc in range(NCH // 2):
                c = hc0 + lc
                c0 = c * CH
                xt_load(ddg, c)

                @C.block.vector
                def _(v, ddg=ddg, c=c):
                    v.wait_ge(s_dve.h, s_dve.n)
                    s_dve.inc(v.tensor_copy(Sb[:, :], S[0:16, ddg * 512:(ddg + 1) * 512]))

                bank = 4 + c % 2

                @C.block.tensor
                def _(te, ddg=ddg, c=c, bank=bank):
                    te.wait_ge(s_dve.h, s_dve.n)
                    te.wait_ge(s_ev.h, s_ev.n)
                    for j in range(4):
                        te.matmul(C.PS(bank)[:, j * 128:(j + 1) * 128],
                                  Sb[:, j * 128:(j + 1) * 128],
                                  ctb_all[:, c * CH:(c + 1) * CH], start=True, stop=False)
                        ins = te.matmul(C.PS(bank)[:, j * 128:(j + 1) * 128],
                                        xt[:, j * 128:(j + 1) * 128],
                                        wball[:, c * CH:(c + 1) * CH],
                                        start=False, stop=True)
                    s_mm.inc(ins)

                @C.block.vector
                def _(v, ddg=ddg, c=c, lc=lc, bank=bank):
                    v.wait_ge(s_mm.h, s_mm.n)
                    s_ev.inc(v.tensor_copy(
                        bass.AP(lnu, (ddg * 4) * HT + lc * CH,
                                [[DC * T, 128], [HT, 4], [1, CH]]),
                        C.PS(bank)[:, 0:512]))

                state_step(ddg, c)


        emit_ln(C, lnu, src_tiles=lambda dd: lnu[:, dd * HT:(dd + 1) * HT], kind="rms",
                gamma_col=lambda dd: nwc[:, dd:dd + 1], nchunks=NDD,
                eps=1.1920929e-07, ntok=HT)

        for fc in range(DC):
            @C.block.gpsimd
            def _(g, fc=fc):
                g.wait_ge(s_mm.h, s_mm.n)
                s_w.inc(g.dma_start(out=win[:, 0:NDD * 128],
                                    in_=bass.AP(io["W_out"], fc * 128,
                                                [[1024, 128], [128 * 1024, NDD], [1, 128]])), 16)

            for i in range(HT // TT):
                @C.block.tensor
                def _(te, fc=fc, i=i, h=h):
                    te.wait_ge(s_w.h, s_w.n)
                    te.wait_ge(s_ev.h, s_ev.n)
                    if fc == 0 and i == 0:
                        te.wait_ge(s_dve.h, s_dve.n)
                    for kc in range(NDD):
                        ins = te.matmul(C.PS(i % 2)[:, :], win[:, kc * 128:(kc + 1) * 128],
                                        lnu[:, kc * HT + i * TT: kc * HT + (i + 1) * TT],
                                        start=(kc == 0), stop=(kc == NDD - 1))
                    s_mm.inc(ins)

                @C.block.vector
                def _(v, fc=fc, i=i, h=h):
                    v.wait_ge(s_mm.h, s_mm.n)
                    s_ev.inc(v.tensor_tensor(
                        Xc(C, fc, h * HT + i * TT, TT), Xc(C, fc, h * HT + i * TT, TT),
                        C.PS(i % 2)[:, :], op=ALU.add))


def emit_L0_ffn(C):
    io = C.io
    E, R, FF = 8, 64, 4096
    lnx = C.sb("lnx0f", [128, DC * T], BF16)
    emit_ln(C, lnx, kind="ln")
    s_dve = C.sem("dve")
    s_act = C.sem("act")
    s_ev = C.sem("ev")
    s_pe = C.sem("pe")
    s_mm = C.sem("mm")
    s_w = C.sem("w")
    s_fl = C.sem("fl")

    w1at = C.sb("w1at", [128, 8 * 512], BF16)
    w1be = C.sb("w1be", [64, 2 * FF], BF16)
    w2ate = C.sb("w2ate", [128, 2 * 32 * 64], BF16)
    w2be = C.sb("w2be", [64, 2 * 1024], BF16)

    s_wc = C.sem("wc")
    s_we = [C.sem("we0"), C.sem("we1")]

    @C.block.gpsimd
    def _(g):
        s_wc.inc(g.dma_start(out=w1at[:, :], in_=whole_w_ap(C, "w1AT", 1024, 512)), 16)

    ew2 = C.sb("ew2sb", [128, 32], F32)
    eif = C.sb("eifsb", [128, 32], F32)
    ct1 = C.sb("ct1", [128, 32], F32)
    CC = C.sb("CC", [128, 8 * 16], F32)

    @C.block.gpsimd
    def _(g):
        s_wc.inc(g.dma_start(out=ew2[:, :], in_=io["ew2"][:, :]), 16)
        s_wc.inc(g.dma_start(out=eif[:, :], in_=io["ei2"][:, :]), 16)

    one_ap = C.constf(1.0)
    negs = [C.constf(-float(e)) for e in range(E)]

    for e in range(E):
        @C.block.scalar
        def _(sc, e=e):
            if e == 0:
                sc.wait_ge(s_wc.h, s_wc.n)
                sc.wait_ge(C.sem("ini").h, C.sem("ini").n)
            else:
                sc.wait_ge(s_dve.h, s_dve.n)
            s_act.inc(sc.activation(out=ct1[:, :], in_=eif[:, :], func=AF.Abs,
                                    bias=negs[e]))
            sc.wait_ge(s_act.h, s_act.n)
            s_act.inc(sc.activation(out=ct1[:, :], in_=ct1[:, :], func=AF.Relu,
                                    scale=-1.0, bias=one_ap))

        @C.block.vector
        def _(v, e=e):
            v.wait_ge(s_act.h, s_act.n)
            s_dve.inc(v.tensor_mul(ct1[:, :], ct1[:, :], ew2[:, :]))
            v.wait_ge(s_dve.h, s_dve.n)
            s_dve.inc(v.tensor_tensor(
                CC[:, e * 16:(e + 1) * 16],
                bass.AP(ct1, 0, [[32, 128], [2, 16]]),
                bass.AP(ct1, 1, [[32, 128], [2, 16]]), op=ALU.add))

    h1b = C.sb("l0h1", [64, TT], BF16)
    h2b = C.sb("l0h2", [128, 2 * TT], BF16)
    h3c = C.sb("l0h3", [64, TT], BF16)
    crow = C.row(0)

    for e in range(E):
        slot = e % 2

        @C.block.gpsimd
        def _(g, e=e, slot=slot):
            if e >= 2:
                g.wait_ge(s_mm.h, s_mm.n)
            s_we[slot].inc(g.dma_start(out=w1be[:, slot * FF:(slot + 1) * FF],
                                in_=io["w1B"][e * 64:(e + 1) * 64, :]), 16)
            s_we[slot].inc(g.dma_start(out=w2ate[:, slot * 2048:(slot + 1) * 2048],
                                in_=bass.AP(io["w2AT"], e * 64,
                                            [[512, 128], [128 * 512, 32], [1, 64]])), 16)
            s_we[slot].inc(g.dma_start(out=w2be[:, slot * 1024:(slot + 1) * 1024],
                                in_=io["w2B"][e * 64:(e + 1) * 64, :]), 16)

        @C.block.sync
        def _(sy, e=e):
            sy.wait_ge(s_dve.h, s_dve.n)
            sy.wait_ge(C.sem("pe").h, C.sem("pe").n)
            for ch in range(16):
                s_fl.inc(sy.dma_start(out=C.ROWS[0:1, ch * 128:(ch + 1) * 128],
                                      in_=CC[:, e * 16 + ch: e * 16 + ch + 1]), 16)

        for i in range(NTT):
            @C.block.tensor
            def _(te, e=e, i=i):
                if i == 0:
                    te.wait_ge(s_fl.h, s_fl.n)
                te.wait_ge(s_dve.h, s_dve.n)
                s_pe.inc(te.matmul(C.PS(6, rows=64)[:, :], C.ONEROWF[0:1, 0:64],
                          C.ROWS[0:1, i * TT:(i + 1) * TT], start=True, stop=True))

            @C.block.vector
            def _(v, i=i):
                v.wait_ge(s_pe.h, s_pe.n)
                s_dve.inc(v.tensor_copy(C.RB[0:64, i * TT:(i + 1) * TT],
                                        C.PS(6, rows=64)[:, :]))

        for i in range(NTT):
            t0 = i * TT

            @C.block.tensor
            def _(te, e=e, i=i, t0=t0, slot=slot):
                te.wait_ge(s_wc.h, s_wc.n)
                te.wait_ge(s_we[slot].h, 48 * (e // 2 + 1))
                te.wait_ge(s_dve.h, s_dve.n)
                te.wait_ge(s_ev.h, s_ev.n)
                for kc in range(DC):
                    ins = te.matmul(C.PS(0, rows=64)[:, :],
                                    w1at[:, kc * 512 + e * 64: kc * 512 + (e + 1) * 64],
                                    lnx[:, kc * T + t0: kc * T + t0 + TT],
                                    start=(kc == 0), stop=(kc == DC - 1))
                s_mm.inc(ins)

            mmh1 = s_mm.n

            @C.block.vector
            def _(v, mmh1=mmh1):
                v.wait_ge(s_mm.h, mmh1)
                s_ev.inc(v.tensor_copy(h1b[:, :], C.PS(0, rows=64)[:, :]))

            evh1 = s_ev.n
            act_base0 = s_act.n
            mm_base0 = s_mm.n

            @C.block.tensor
            def _(te, evh1=evh1, act_base0=act_base0, slot=slot):
                te.wait_ge(s_ev.h, evh1)
                for fc in range(2):
                    s_mm.inc(te.matmul(C.PS(1 + fc)[:, :],
                                       w1be[:, slot * FF + fc * 128: slot * FF + (fc + 1) * 128],
                                       h1b[:, :], start=True, stop=True))
                for fc in range(32):
                    te.wait_ge(s_act.h, act_base0 + fc + 1)
                    s_mm.inc(te.matmul(C.PS(3)[0:64, :],
                                       w2ate[:, slot * 2048 + fc * 64: slot * 2048 + (fc + 1) * 64],
                                       h2b[:, (fc % 2) * TT:(fc % 2 + 1) * TT],
                                       start=(fc == 0), stop=(fc == 31)))
                    if fc + 2 < 32:
                        s_mm.inc(te.matmul(C.PS(1 + fc % 2)[:, :],
                                           w1be[:, slot * FF + (fc + 2) * 128: slot * FF + (fc + 3) * 128],
                                           h1b[:, :], start=True, stop=True))

            @C.block.scalar
            def _(sc, act_base0=act_base0, mm_base0=mm_base0):
                for fc in range(32):
                    qk = mm_base0 + (fc + 1 if fc < 2 else 2 * fc)
                    sc.wait_ge(s_mm.h, qk)
                    s_act.inc(sc.activation(out=h2b[:, (fc % 2) * TT:(fc % 2 + 1) * TT],
                                            in_=C.PS(1 + fc % 2)[:, :], func=AF.Gelu))

            @C.block.vector
            def _(v, i=i):
                v.wait_ge(s_mm.h, s_mm.n)
                s_ev.inc(v.tensor_tensor(h3c[:, :], C.PS(3)[0:64, :],
                                         C.RB[0:64, i * TT:(i + 1) * TT], op=ALU.mult))

            evh3 = s_ev.n
            mm_base1 = s_mm.n

            @C.block.tensor
            def _(te, evh3=evh3, e=e):
                te.wait_ge(s_ev.h, evh3)
                slot2 = e % 2
                for fc in range(DC):
                    if fc >= 2:
                        te.wait_ge(s_ev.h, evh3 + fc - 1)
                    s_mm.inc(te.matmul(C.PS(4 + fc % 2)[:, :],
                                       w2be[:, slot2 * 1024 + fc * 128: slot2 * 1024 + (fc + 1) * 128],
                                       h3c[:, :], start=True, stop=True))

            @C.block.vector
            def _(v, mm_base1=mm_base1, i=i, t0=t0):
                for fc in range(DC):
                    v.wait_ge(s_mm.h, mm_base1 + fc + 1)
                    s_ev.inc(v.tensor_tensor(Xc(C, fc, t0, TT), Xc(C, fc, t0, TT),
                                             C.PS(4 + fc % 2)[:, :], op=ALU.add))


def emit_L3_attn(C):
    io = C.io
    s_dve = C.sem("dve")
    s_act = C.sem("act")
    s_ev = C.sem("ev")
    s_pe = C.sem("pe")
    s_mm = C.sem("mm")
    s_w = C.sem("w")
    s_cc = C.sem("cc")
    s_swd = C.sem("swd")

    lnq = C.sb("l3a_u", [128, DC * T], BF16)
    emit_ln(C, lnq, kind="ln")

    k_in = C.nc.dram_tensor("l3a_k_in", [128, T], BF16)
    k_out = C.nc.dram_tensor("l3a_k_out", [256, T], BF16)
    v_in = C.nc.dram_tensor("l3a_vin", [T, 128], BF16)
    v_out = C.nc.dram_tensor("l3a_vout", [2 * T, 128], BF16)

    pq = C.sb("l3a_pq", [128, DC * 1024], BF16)
    Kf = C.sb("l3a_Kf", [128, 2 * T], BF16)
    Vtm = C.sb("l3a_Vtm", [128, 32 * 128], BF16)
    PT = C.sb("l3a_PT", [128, 2 * TT], BF16)
    Oar = lnq
    lrow = C.ROWS[0:1, 0:TT]
    linv = C.MB[:, 0:TT]
    wkv = C.sb("l3a_wkv", [128, DC * 256], BF16)
    wot = C.sb("l3a_wot", [128, DC * 128], BF16)

    @C.block.gpsimd
    def _(g):
        s_w.inc(g.dma_start(out=wkv[:, :], in_=whole_w_ap(C, "pkv_cat", 1024, 256)), 16)
        s_w.inc(g.dma_start(out=pq[:, :], in_=whole_w_ap(C, "pq_cat", 1024, 1024)), 16)

    for i in range(NTT):
        @C.block.tensor
        def _(te, i=i):
            te.wait_ge(s_w.h, s_w.n)
            if i == 0:
                te.wait_ge(s_dve.h, s_dve.n)
            else:
                te.wait_ge(s_ev.h, s_ev.n)
            for kc in range(DC):
                ins = te.matmul(C.PS(0)[:, :],
                                wkv[:, kc * 256: kc * 256 + 128],
                                lnq[:, kc * T + i * TT: kc * T + (i + 1) * TT],
                                start=(kc == 0), stop=(kc == DC - 1))
            s_mm.inc(ins)
            for j in range(4):
                tq = i * 4 + j
                for kc in range(DC):
                    ins = te.matmul(C.PS(2 + j)[:, :128],
                                    lnq[:, kc * T + tq * 128: kc * T + (tq + 1) * 128],
                                    wkv[:, kc * 256 + 128: kc * 256 + 256],
                                    start=(kc == 0), stop=(kc == DC - 1))
                s_mm.inc(ins)
                if j >= 1:
                    pass

        mmb = s_mm.n

        @C.block.vector
        def _(v, i=i, mmb=mmb):
            v.wait_ge(s_mm.h, mmb - 4)
            s_ev.inc(v.tensor_copy(Kf[:, i * TT:(i + 1) * TT], C.PS(0)[:, :]))
            for j in range(4):
                v.wait_ge(s_mm.h, mmb - 3 + j)
                tq = i * 4 + j
                s_ev.inc(v.tensor_copy(Vtm[:, tq * 128:(tq + 1) * 128],
                                       C.PS(2 + j)[:, :128]))

    @C.block.gpsimd
    def _(g):
        g.wait_ge(s_ev.h, s_ev.n)
        s_swd.inc(g.dma_start(out=k_in.ap(), in_=Kf[:, 0:T]), 16)
        s_swd.inc(g.dma_start(
            out=bass.AP(v_in, 0, [[128, 128], [128 * 128, 16], [1, 128]]),
            in_=Vtm[:, 0:16 * 128]), 16)
        g.wait_ge(s_swd.h, s_swd.n)
        s_cc.inc(g.collective_compute(
            "AllGather", ALU.bypass, replica_groups=[[0, 1], [2, 3], [4, 5], [6, 7]],
            ins=[k_in.ap().opt()], outs=[k_out.ap().opt()]))
        s_cc.inc(g.collective_compute(
            "AllGather", ALU.bypass, replica_groups=[[0, 1], [2, 3], [4, 5], [6, 7]],
            ins=[v_in.ap().opt()], outs=[v_out.ap().opt()]))
        g.wait_ge(s_cc.h, s_cc.n)
        s_swd.inc(g.dma_start(out=Kf[:, 0:T], in_=k_out[0:128, :]), 16)
        s_swd.inc(g.dma_start(out=Kf[:, T:2 * T], in_=k_out[128:256, :]), 16)
        s_swd.inc(g.dma_start(out=Vtm[:, :],
                              in_=bass.AP(v_out, 0, [[128, 128], [128 * 128, 32], [1, 128]])), 16)

    Qh = C.sb("l3a_Qh", [128, T // 2], BF16)
    Ohalf = C.sb("l3a_Oh", [128, 8 * (T // 2)], BF16)
    HTQ = T // 2

    for qh in range(2):
        for h in range(8):
            for il in range(2):
                i = qh * 2 + il

                @C.block.tensor
                def _(te, h=h, i=i, il=il, qh=qh):
                    if qh == 0 and h == 0 and il == 0:
                        te.wait_ge(s_swd.h, s_swd.n)
                    te.wait_ge(s_ev.h, s_ev.n)
                    for kc in range(DC):
                        ins = te.matmul(C.PS(0 + il)[:, :],
                                        pq[:, kc * 1024 + h * 128: kc * 1024 + (h + 1) * 128],
                                        lnq[:, kc * T + i * TT: kc * T + (i + 1) * TT],
                                        start=(kc == 0), stop=(kc == DC - 1))
                    s_mm.inc(ins)

                @C.block.vector
                def _(v, il=il):
                    v.wait_ge(s_mm.h, s_mm.n)
                    s_ev.inc(v.tensor_copy(Qh[:, il * TT:(il + 1) * TT],
                                           C.PS(0 + il)[:, :]))

            for qtl in range(2):
                act_base = C.sem("act").n
                mm_base = C.sem("mm").n

                @C.block.tensor
                def _(te, h=h, qtl=qtl, act_base=act_base, mm_base=mm_base):
                    te.wait_ge(s_ev.h, s_ev.n)
                    te.wait_ge(s_dve.h, s_dve.n)
                    for kt in range(2):
                        s_mm.inc(te.matmul(C.PS(kt)[:, :], Kf[:, kt * 128:(kt + 1) * 128],
                                           Qh[:, qtl * TT:(qtl + 1) * TT],
                                           start=True, stop=True))
                    for kt in range(32):
                        sb = kt % 2
                        te.wait_ge(s_act.h, act_base + kt + 1)
                        te.matmul(C.PS(4 + qtl)[:, :], Vtm[:, kt * 128:(kt + 1) * 128],
                                  PT[:, sb * TT:(sb + 1) * TT],
                                  start=(kt == 0), stop=(kt == 31))
                        ins = te.matmul(C.PS(6)[0:1, :], C.ONESB[:, :],
                                        PT[:, sb * TT:(sb + 1) * TT],
                                        start=(kt == 0), stop=(kt == 31))
                        s_mm.inc(ins)
                        if kt + 2 < 32:
                            ins = te.matmul(C.PS(sb)[:, :],
                                            Kf[:, (kt + 2) * 128:(kt + 3) * 128],
                                            Qh[:, qtl * TT:(qtl + 1) * TT],
                                            start=True, stop=True)
                            s_mm.inc(ins)

                @C.block.scalar
                def _(sc, act_base=act_base, mm_base=mm_base):
                    for kt in range(32):
                        qk = mm_base + (kt + 1 if kt < 2 else 2 * kt)
                        sc.wait_ge(s_mm.h, qk)
                        s_act.inc(sc.activation(out=PT[:, (kt % 2) * TT:(kt % 2 + 1) * TT],
                                                in_=C.PS(kt % 2)[:, :], func=AF.Exp))

                @C.block.vector
                def _(v):
                    v.wait_ge(s_mm.h, s_mm.n)
                    s_dve.inc(v.tensor_copy(lrow, C.PS(6)[0:1, :]))
                    v.wait_ge(s_dve.h, s_dve.n)
                    s_dve.inc(v.reciprocal(lrow, lrow))

                @C.block.tensor
                def _(te):
                    te.wait_ge(s_dve.h, s_dve.n)
                    s_pe.inc(te.matmul(C.PS(2)[:, :], C.ONEROWF[:, :], lrow,
                                       start=True, stop=True))

                @C.block.vector
                def _(v, h=h, qtl=qtl):
                    v.wait_ge(s_pe.h, s_pe.n)
                    s_dve.inc(v.tensor_copy(linv, C.PS(2)[:, :]))
                    v.wait_ge(s_dve.h, s_dve.n)
                    s_dve.inc(v.tensor_tensor(
                        Ohalf[:, h * HTQ + qtl * TT: h * HTQ + (qtl + 1) * TT],
                        C.PS(4 + qtl)[:, :], linv, op=ALU.mult))

        for fc in range(DC):
            @C.block.gpsimd
            def _(g, fc=fc, qh=qh):
                g.wait_ge(s_mm.h, s_mm.n)
                s_w.inc(g.dma_start(out=wot[:, :],
                                    in_=bass.AP(io["po_cat"], fc * 128,
                                                [[1024, 128], [128 * 1024, DC], [1, 128]])), 16)

            for il in range(2):
                @C.block.tensor
                def _(te, fc=fc, il=il, qh=qh):
                    te.wait_ge(s_w.h, s_w.n)
                    te.wait_ge(s_ev.h, s_ev.n)
                    if fc == 0 and il == 0:
                        te.wait_ge(s_dve.h, s_dve.n)
                    for kc in range(DC):
                        ins = te.matmul(C.PS(0 + il)[:, :], wot[:, kc * 128:(kc + 1) * 128],
                                        Ohalf[:, kc * HTQ + il * TT: kc * HTQ + (il + 1) * TT],
                                        start=(kc == 0), stop=(kc == DC - 1))
                    s_mm.inc(ins)

                @C.block.vector
                def _(v, fc=fc, il=il, qh=qh):
                    v.wait_ge(s_mm.h, s_mm.n)
                    i = qh * 2 + il
                    s_ev.inc(v.tensor_tensor(Xc(C, fc, i * TT, TT), Xc(C, fc, i * TT, TT),
                                             C.PS(0 + il)[:, :], op=ALU.add))


def emit_L0_attn(C):
    io = C.io
    DI = 2048
    NH = 32
    CH = 128
    NCH = T // CH
    NXB = 18
    s_dve = C.sem("dve")
    s_act = C.sem("act")
    s_ev = C.sem("ev")
    s_pe = C.sem("pe")
    s_mm = C.sem("mm")
    s_w = C.sem("w")
    s_fl = C.sem("fl")
    s_ms = C.sem("ms")
    s_xs = C.sem("xs")

    z_scr = C.nc.dram_tensor("l0_z_scr", [DI, T], BF16)
    xv_scr = C.nc.dram_tensor("l0_xv_scr", [DI, T], BF16)

    lnx = C.sb("l0_lnx", [128, DC * T], BF16)
    QT = T // 4
    Y = C.sb("l0_Y", [128, 16 * QT], BF16)
    Bfm = C.sb("l0_B", [128, T], BF16)
    Cfm = C.sb("l0_C", [128, T], BF16)
    dtsp = C.sb("l0_dt", [32, T], BF16)
    gcum = C.sb("l0_gc", [32, T], F32)
    WS = C.sb("l0_ws", [128, 8192], BF16)
    xg = WS[:, 4096:6144]
    xt = WS[:, 6144:8192]
    Sbv = None
    cw = C.sb("l0_cw", [128, NXB * 4], F32)
    cb = C.sb("l0_cb", [128, NXB], F32)
    wpre = C.sb("l0_wpre", [128, DC], F32)
    wpost = C.sb("l0_wpost", [128, DC], F32)
    wssm = C.sb("l0_wssm", [128, 16], F32)
    dpar = C.sb("l0_dp", [128, 16], F32)
    dtb = C.sb("l0_dtb", [32, 1], F32)
    alog = C.sb("l0_al", [32, 1], F32)
    acol = C.sb("l0_ac", [32, 1], F32)
    xh = C.sb("l0_xh", [128, DC * 4], F32)
    uh = C.sb("l0_uh", [128, DC * 4], BF16)
    idf = C.sb("l0_idf", [128, 128], F32)
    idb = C.sb("l0_idb", [128, 128], BF16)
    maskt = C.sb("l0_mask", [128, 128], F32)
    ones32 = C.sb("l0_o32", [32, 128], F32)
    gct = C.sb("l0_gct", [128, 32], F32)
    dtt = C.sb("l0_dtt", [128, 32], F32)
    edtf = C.sb("l0_edt", [32, CH], F32)
    edty = C.sb("l0_edty", [128, 32], F32)
    egr = C.sb("l0_egr", [32, 64], F32)
    btm = C.sb("l0_btm", [128, 128], BF16)
    bdtt = C.sb("l0_bdt", [128, 128], BF16)
    cpt = C.sb("l0_cpt", [128, 256], BF16)
    Gsb = C.sb("l0_G", [128, 128], F32)
    Pfm = C.sb("l0_P", [32, CH], F32)
    S = C.sb("l0_S", [128, DI], F32)
    PRE = S[:, 0:1026].bitcast(BF16)
    zt = C.sb("l0_zt", [128, TT], BF16)
    sqh = C.sb("l0_sqh", [128, DC * 4], F32)
    DD = C.MB
    mpb = C.RB[:, 0:1024].bitcast(BF16)
    AA = lambda: C.LNTMP[0:32, 0:T]

    s_wc = C.sem("wc")

    @C.block.gpsimd
    def _(g):
        s_wc.inc(g.dma_start(out=cw[:, :], in_=io["cw_fm"][:, :]), 16)
        s_wc.inc(g.dma_start(out=cb[:, :], in_=io["cb_fm"][:, :]), 16)
        s_wc.inc(g.dma_start(out=wpre[:, :], in_=io["wpre_fm"][:, :]), 16)
        s_wc.inc(g.dma_start(out=wpost[:, :], in_=io["wpost_fm"][:, :]), 16)
        s_wc.inc(g.dma_start(out=wssm[:, :], in_=io["wssm_fm"][:, :]), 16)
        s_wc.inc(g.dma_start(out=dpar[:, :], in_=io["dpar_fm"][:, :]), 16)
        s_wc.inc(g.dma_start(out=dtb[:, :], in_=io["dtb_c"][:, :]), 16)
        s_wc.inc(g.dma_start(out=alog[:, :], in_=io["alog_c"][:, :]), 16)
        s_wc.inc(g.dma_start(out=idf[:, :], in_=io["ident_f32"][:, :]), 16)
        s_wc.inc(g.dma_start(out=idb[:, :], in_=io["ident_bf"][:, :]), 16)
        s_wc.inc(g.dma_start(out=maskt[:, :], in_=io["mask_lt"][:, :]), 16)
        s_wc.inc(g.dma_start(out=xh[:, :], in_=io["xh_fm"][:, :]), 16)
        s_ms.inc(g.memset(ones32[:, :], 1.0))

    emit_ln(C, lnx, kind="ln")

    @C.block.vector
    def _(v):
        v.wait_ge(s_wc.h, s_wc.n)

    emit_ln(C, lnx, src_tiles=lambda dc: lnx[:, dc * T:(dc + 1) * T], kind="rms",
            gamma_col=lambda dc: wpre[:, dc:dc + 1], eps=1e-6)
    u = lnx

    @C.block.vector
    def _(v):
        v.wait_ge(s_wc.h, s_wc.n)
        v.wait_ge(s_dve.h, s_dve.n)
        s_dve.inc(v.tensor_mul(sqh[:, :], xh[:, :], xh[:, :]))

    @C.block.tensor
    def _(te):
        te.wait_ge(s_dve.h, s_dve.n)
        for kc in range(DC):
            te.matmul(C.PS(0)[0:1, 0:4], C.ONESF[:, :], xh[:, kc * 4:(kc + 1) * 4],
                      start=(kc == 0), stop=(kc == DC - 1))
            ins = te.matmul(C.PS(1)[0:1, 0:4], C.ONESF[:, :], sqh[:, kc * 4:(kc + 1) * 4],
                            start=(kc == 0), stop=(kc == DC - 1))
        s_pe.inc(ins)

    hr = C.ROWS

    @C.block.vector
    def _(v):
        v.wait_ge(s_pe.h, s_pe.n)
        s_dve.inc(v.tensor_scalar(hr[0:1, 0:4], C.PS(0)[0:1, 0:4], 1.0 / 1024, None,
                                  op0=ALU.mult))
        s_dve.inc(v.tensor_scalar(hr[0:1, 8:12], C.PS(1)[0:1, 0:4], 1.0 / 1024, None,
                                  op0=ALU.mult))
        v.wait_ge(s_dve.h, s_dve.n)
        s_dve.inc(v.tensor_mul(hr[0:1, 16:20], hr[0:1, 0:4], hr[0:1, 0:4]))
        v.wait_ge(s_dve.h, s_dve.n)
        s_dve.inc(v.tensor_sub(hr[0:1, 8:12], hr[0:1, 8:12], hr[0:1, 16:20]))

    @C.block.scalar
    def _(sc):
        sc.wait_ge(s_dve.h, s_dve.n)
        sc.wait_ge(C.sem("ini").h, C.sem("ini").n)
        s_act.inc(sc.activation(out=hr[0:1, 16:20], in_=hr[0:1, 8:12], func=AF.Sqrt,
                                bias=C.constf(1e-5)[0:1, :]))

    @C.block.vector
    def _(v):
        v.wait_ge(s_act.h, s_act.n)
        s_dve.inc(v.reciprocal(hr[0:1, 16:20], hr[0:1, 16:20]))
        v.wait_ge(s_dve.h, s_dve.n)
        s_dve.inc(v.tensor_mul(hr[0:1, 24:28], hr[0:1, 16:20], hr[0:1, 16:20]))
        v.wait_ge(s_dve.h, s_dve.n)
        s_dve.inc(v.tensor_mul(hr[0:1, 24:28], hr[0:1, 24:28], hr[0:1, 8:12]))

    @C.block.scalar
    def _(sc):
        sc.wait_ge(s_dve.h, s_dve.n)
        s_act.inc(sc.activation(out=hr[0:1, 24:28], in_=hr[0:1, 24:28], func=AF.Sqrt,
                                bias=C.constf(1e-6)[0:1, :]))

    @C.block.vector
    def _(v):
        v.wait_ge(s_act.h, s_act.n)
        s_dve.inc(v.reciprocal(hr[0:1, 24:28], hr[0:1, 24:28]))
        v.wait_ge(s_dve.h, s_dve.n)
        s_dve.inc(v.tensor_mul(hr[0:1, 16:20], hr[0:1, 16:20], hr[0:1, 24:28]))

    @C.block.tensor
    def _(te):
        te.wait_ge(s_dve.h, s_dve.n)
        te.matmul(C.PS(2)[:, 0:4], C.ONEROWF[:, :], hr[0:1, 16:20], start=True, stop=True)
        s_pe.inc(te.matmul(C.PS(2)[:, 4:8], C.ONEROWF[:, :], hr[0:1, 0:4],
                           start=True, stop=True))

    @C.block.vector
    def _(v):
        v.wait_ge(s_pe.h, s_pe.n)
        for kc in range(DC):
            v.wait_ge(s_dve.h, s_dve.n)
            s_dve.inc(v.tensor_sub(sqh[:, kc * 4:(kc + 1) * 4], xh[:, kc * 4:(kc + 1) * 4],
                                   C.PS(2)[:, 4:8]))
            v.wait_ge(s_dve.h, s_dve.n)
            s_dve.inc(v.scalar_tensor_tensor(uh[:, kc * 4:(kc + 1) * 4],
                                             sqh[:, kc * 4:(kc + 1) * 4],
                                             wpre[:, kc:kc + 1], C.PS(2)[:, 0:4],
                                             op0=ALU.mult, op1=ALU.mult))

    def stream_panel(wname, base, fc, width, dst0, nkc=DC):
        @C.block.gpsimd
        def _(g, wname=wname, base=base, fc=fc, width=width, dst0=dst0, nkc=nkc):
            g.wait_ge(s_mm.h, s_mm.n)
            s_w.inc(g.dma_start(
                out=WS[:, dst0: dst0 + nkc * width],
                in_=bass.AP(io[wname], fc * 128,
                            [[base, 128], [128 * base, nkc], [1, width]])), 16)

    for fc in range(16):
        slot = fc % 2
        stream_panel("ipT_z", DI, fc, 128, slot * 1024)
        for i in range(NTT):
            @C.block.tensor
            def _(te, fc=fc, i=i, slot=slot):
                te.wait_ge(s_w.h, s_w.n)
                te.wait_ge(s_ev.h, s_ev.n)
                if fc == 0 and i == 0:
                    te.wait_ge(s_dve.h, s_dve.n)
                for kc in range(DC):
                    ins = te.matmul(C.PS(i % 4)[:, :],
                                    WS[:, slot * 1024 + kc * 128: slot * 1024 + (kc + 1) * 128],
                                    u[:, kc * T + i * TT: kc * T + (i + 1) * TT],
                                    start=(kc == 0), stop=(kc == DC - 1))
                s_mm.inc(ins)

            @C.block.vector
            def _(v, i=i):
                v.wait_ge(s_mm.h, s_mm.n)
                v.wait_ge(s_xs.h, s_xs.n)
                s_ev.inc(v.tensor_copy(zt[:, :], C.PS(i % 4)[:, :]))

            @C.block.gpsimd
            def _(g, fc=fc, i=i):
                g.wait_ge(s_ev.h, s_ev.n)
                s_xs.inc(g.dma_start(out=z_scr[fc * 128:(fc + 1) * 128, i * TT:(i + 1) * TT],
                                     in_=zt[:, :]), 16)

    for ch in range(NXB):
        slot = ch % 2
        stream_panel("ipT_xbc", 2304, ch, 128, slot * 1024)

        @C.block.tensor
        def _(te, ch=ch, slot=slot):
            te.wait_ge(s_w.h, s_w.n)
            te.wait_ge(s_dve.h, s_dve.n)
            te.wait_ge(s_act.h, s_act.n)
            for kc in range(DC):
                ins = te.matmul(C.PS(4)[:, 0:4],
                                WS[:, slot * 1024 + kc * 128: slot * 1024 + (kc + 1) * 128],
                                uh[:, kc * 4:(kc + 1) * 4],
                                start=(kc == 0), stop=(kc == DC - 1))
            s_mm.inc(ins)
            for i in range(NTT):
                for kc in range(DC):
                    ins = te.matmul(C.PS(i % 4)[:, :],
                                    WS[:, slot * 1024 + kc * 128: slot * 1024 + (kc + 1) * 128],
                                    u[:, kc * T + i * TT: kc * T + (i + 1) * TT],
                                    start=(kc == 0), stop=(kc == DC - 1))
                s_mm.inc(ins)

        @C.block.vector
        def _(v, ch=ch):
            v.wait_ge(s_mm.h, s_mm.n)
            v.wait_ge(s_act.h, s_act.n)
            s_dve.inc(v.tensor_copy(PRE[:, 0:4], C.PS(4)[:, 0:4]))
            for i in range(NTT):
                s_dve.inc(v.tensor_copy(PRE[:, 4 + i * TT: 4 + (i + 1) * TT], C.PS(i % 4)[:, :]))

        @C.block.scalar
        def _(sc, ch=ch):
            sc.wait_ge(s_dve.h, s_dve.n)
            s_act.inc(sc.activation(out=C.LNTMP[:, :], in_=PRE[:, 1:1 + T], func=AF.Identity,
                                    scale=cw[:, ch * 4: ch * 4 + 1], bias=cb[:, ch:ch + 1]))

        @C.block.vector
        def _(v, ch=ch):
            v.wait_ge(s_act.h, s_act.n)
            for k in range(1, 4):
                v.wait_ge(s_dve.h, s_dve.n)
                s_dve.inc(v.scalar_tensor_tensor(C.LNTMP[:, :], PRE[:, 1 + k: 1 + k + T],
                                                 cw[:, ch * 4 + k: ch * 4 + k + 1],
                                                 C.LNTMP[:, :], op0=ALU.mult, op1=ALU.add))

        @C.block.scalar
        def _(sc, ch=ch):
            sc.wait_ge(s_dve.h, s_dve.n)
            if ch < 16:
                sc.wait_ge(s_xs.h, s_xs.n)
                s_act.inc(sc.activation(out=Y[:, 0:T], in_=C.LNTMP[:, :], func=AF.Silu))
            elif ch == 16:
                s_act.inc(sc.activation(out=Bfm[:, :], in_=C.LNTMP[:, :], func=AF.Silu))
            else:
                s_act.inc(sc.activation(out=Cfm[:, :], in_=C.LNTMP[:, :], func=AF.Silu))

        if ch < 16:
            @C.block.gpsimd
            def _(g, ch=ch):
                g.wait_ge(s_act.h, s_act.n)
                s_xs.inc(g.dma_start(out=xv_scr[ch * 128:(ch + 1) * 128, :],
                                     in_=Y[:, 0:T]), 16)

    stream_panel("ipT_dt", 32, 0, 32, 0)
    for i in range(NTT):
        @C.block.tensor
        def _(te, i=i):
            te.wait_ge(s_w.h, s_w.n)
            te.wait_ge(s_dve.h, s_dve.n)
            for kc in range(DC):
                ins = te.matmul(C.PS(5)[0:32, :],
                                WS[:, kc * 32:(kc + 1) * 32],
                                u[:, kc * T + i * TT: kc * T + (i + 1) * TT],
                                start=(kc == 0), stop=(kc == DC - 1))
            s_mm.inc(ins)

        @C.block.vector
        def _(v, i=i):
            v.wait_ge(s_mm.h, s_mm.n)
            s_dve.inc(v.tensor_copy(gcum[:, i * TT:(i + 1) * TT], C.PS(5)[0:32, :]))

    @C.block.vector
    def _(v):
        v.wait_ge(s_dve.h, s_dve.n)
        s_dve.inc(v.tensor_scalar(gcum[:, :], gcum[:, :], dtb[:, 0:1], None, op0=ALU.add))

    @C.block.scalar
    def _(sc):
        sc.wait_ge(s_dve.h, s_dve.n)
        sc.wait_ge(s_act.h, s_act.n)
        sc.wait_ge(C.sem("ini").h, C.sem("ini").n)
        s_act.inc(sc.activation(out=AA(), in_=gcum[:, :], func=AF.Abs))
        sc.wait_ge(s_act.h, s_act.n)
        s_act.inc(sc.activation(out=AA(), in_=AA(), func=AF.Exp, scale=-1.0))
        sc.wait_ge(s_act.h, s_act.n)
        s_act.inc(sc.activation(out=AA(), in_=AA(), func=AF.Ln, bias=C.constf(1.0)[0:32, :]))
        sc.wait_ge(s_act.h, s_act.n)
        s_act.inc(sc.activation(out=alog[:, 0:1], in_=alog[:, 0:1], func=AF.Exp))

    @C.block.vector
    def _(v):
        v.wait_ge(s_act.h, s_act.n)
        s_dve.inc(v.tensor_scalar(gcum[:, :], gcum[:, :], 0.0, None, op0=ALU.max))
        v.wait_ge(s_dve.h, s_dve.n)
        s_dve.inc(v.tensor_tensor(gcum[:, :], gcum[:, :], AA(), op=ALU.add))
        v.wait_ge(s_dve.h, s_dve.n)
        s_dve.inc(v.tensor_copy(dtsp[:, :], gcum[:, :]))
        v.wait_ge(s_dve.h, s_dve.n)
        s_dve.inc(v.tensor_scalar(acol[:, 0:1], alog[:, 0:1], -1.0, None, op0=ALU.mult))
        v.wait_ge(s_dve.h, s_dve.n)
        s_dve.inc(v.tensor_scalar(AA(), gcum[:, :], acol[:, 0:1], None, op0=ALU.mult))
        for c in range(NCH):
            v.wait_ge(s_dve.h, s_dve.n)
            s_dve.inc(v.tensor_tensor_scan(gcum[:, c * CH:(c + 1) * CH],
                                           ones32[:, :], AA()[:, c * CH:(c + 1) * CH],
                                           0.0, op0=ALU.mult, op1=ALU.add))

    emit_ln(C, lnx, kind="ln")

    @C.block.gpsimd
    def _(g):
        g.wait_ge(s_dve.h, s_dve.n)
        s_ms.inc(g.memset(S[:, :], 0.0))

    @C.block.vector
    def _(v):
        v.wait_ge(s_ms.h, s_ms.n)

    for half in range(4):
        for c in range(half * (NCH // 4), (half + 1) * (NCH // 4)):
            c0 = c * CH
            cl0 = c0 - half * QT

            @C.block.gpsimd
            def _(g, c=c, c0=c0):
                g.wait_ge(s_mm.h, s_mm.n)
                g.wait_ge(s_dve.h, s_dve.n)
                g.wait_ge(s_xs.h, s_xs.n)
                s_xs.inc(g.dma_start(out=xg[:, :],
                                     in_=bass.AP(xv_scr, c0, [[T, 128], [128 * T, 16], [1, CH]])), 16)

            for r in range(2):
                @C.block.tensor
                def _(te, r=r):
                    te.wait_ge(s_xs.h, s_xs.n)
                    te.wait_ge(s_dve.h, s_dve.n)
                    for j in range(8):
                        ins = te.transpose(C.PSB[:, j * 128:(j + 1) * 128],
                                           xg[:, (r * 8 + j) * 128:(r * 8 + j + 1) * 128],
                                           idb[:, :])
                    s_pe.inc(ins)

                @C.block.vector
                def _(v, r=r):
                    v.wait_ge(s_pe.h, s_pe.n)
                    s_dve.inc(v.tensor_copy(xt[:, r * 1024:(r + 1) * 1024], C.PSB[:, 0:1024]))

            @C.block.vector
            def _(v):
                v.wait_ge(s_dve.h, s_dve.n)
                v.wait_ge(s_mm.h, s_mm.n)
                s_dve.inc(v.tensor_copy(WS[:, 0:2048], S[:, :]))

            @C.block.tensor
            def _(te, c=c, c0=c0):
                te.wait_ge(s_dve.h, s_dve.n)
                te.wait_ge(s_act.h, s_act.n)
                te.transpose(C.PS(5)[:, 0:32], gcum[:, c0:c0 + CH], idf[0:32, 0:32])
                s_pe.inc(te.matmul(C.PS(6)[:, 0:128], Bfm[:, c0:c0 + CH], Cfm[:, c0:c0 + CH],
                                   start=True, stop=True))

            @C.block.vector
            def _(v):
                v.wait_ge(s_pe.h, s_pe.n)
                s_dve.inc(v.tensor_copy(gct[:, :], C.PS(5)[:, 0:32]))
                s_dve.inc(v.tensor_copy(Gsb[:, :], C.PS(6)[:, 0:128]))

            @C.block.scalar
            def _(sc, c=c, c0=c0):
                sc.wait_ge(s_dve.h, s_dve.n)
                s_act.inc(sc.activation(out=Pfm[:, :], in_=gcum[:, c0:c0 + CH], func=AF.Exp))
                s_act.inc(sc.activation(out=edtf[:, :], in_=gcum[:, c0:c0 + CH], func=AF.Exp,
                                        scale=-1.0, bias=gcum[:, c0 + CH - 1:c0 + CH]))

            @C.block.vector
            def _(v, c=c, c0=c0):
                v.wait_ge(s_act.h, s_act.n)
                s_dve.inc(v.tensor_mul(edtf[:, :], edtf[:, :], dtsp[:, c0:c0 + CH]))

            @C.block.tensor
            def _(te, c=c, c0=c0):
                te.wait_ge(s_dve.h, s_dve.n)
                te.transpose(C.PS(5)[:, 64:96], edtf[:, :], idf[0:32, 0:32])
                s_pe.inc(te.transpose(C.PSB[:, 0:128], Bfm[:, c0:c0 + CH], idb[:, :]))

            @C.block.vector
            def _(v):
                v.wait_ge(s_pe.h, s_pe.n)
                s_dve.inc(v.tensor_copy(edty[:, :], C.PS(5)[:, 64:96]))
                s_dve.inc(v.tensor_copy(btm[:, :], C.PSB[:, 0:128]))

            @C.block.tensor
            def _(te, c=c, c0=c0):
                te.wait_ge(s_dve.h, s_dve.n)
                s_pe.inc(te.transpose(C.PSB[:, 0:32], dtsp[:, c0:c0 + CH], idb[0:32, 0:32]))

            @C.block.vector
            def _(v):
                v.wait_ge(s_pe.h, s_pe.n)
                s_dve.inc(v.tensor_copy(dtt[:, :], C.PSB[:, 0:32]))

            for grp in range(2):
                @C.block.sync
                def _(sy, c=c, c0=c0, grp=grp):
                    sy.wait_ge(s_dve.h, s_dve.n)
                    sy.wait_ge(s_pe.h, s_pe.n)
                    s_fl.inc(sy.dma_start(out=C.ROWS[0:1, T:T + 2048],
                                          in_=gcum[grp * 16:(grp + 1) * 16, c0:c0 + CH]), 16)
                    s_fl.inc(sy.dma_start(out=C.ROWS[0:1, 0:2048],
                                          in_=Pfm[grp * 16:(grp + 1) * 16, :]), 16)

                @C.block.tensor
                def _(te, grp=grp):
                    te.wait_ge(s_fl.h, s_fl.n)
                    te.wait_ge(s_dve.h, s_dve.n)
                    for q in range(4):
                        ins = te.matmul(C.PS(q)[:, :], C.ONEROWF[:, :],
                                        C.ROWS[0:1, T + q * 512: T + (q + 1) * 512],
                                        start=True, stop=True)
                    s_pe.inc(ins)

                @C.block.vector
                def _(v, grp=grp):
                    v.wait_ge(s_pe.h, s_pe.n)
                    for n in range(16):
                        h = grp * 16 + n
                        v.wait_ge(s_dve.h, s_dve.n)
                        s_dve.inc(v.scalar_tensor_tensor(
                            DD[:, n * 128:(n + 1) * 128],
                            C.PS(n // 4)[:, (n % 4) * 128:(n % 4 + 1) * 128],
                            gct[:, h:h + 1], maskt[:, :], op0=ALU.subtract, op1=ALU.add))

                @C.block.scalar
                def _(sc):
                    sc.wait_ge(s_dve.h, s_dve.n)
                    s_act.inc(sc.activation(out=DD[:, :], in_=DD[:, :], func=AF.Exp))

                @C.block.vector
                def _(v, grp=grp):
                    v.wait_ge(s_act.h, s_act.n)
                    for n in range(16):
                        h = grp * 16 + n
                        v.wait_ge(s_dve.h, s_dve.n)
                        s_dve.inc(v.scalar_tensor_tensor(
                            mpb[:, n * 128:(n + 1) * 128], DD[:, n * 128:(n + 1) * 128],
                            dtt[:, h:h + 1], Gsb[:, :], op0=ALU.mult, op1=ALU.mult))

                for e2 in range(8):
                    ec = grp * 8 + e2
                    ybank = 4 + e2 // 4
                    ysl = (e2 % 4) * 128

                    for hh in range(2):
                        h = 2 * ec + hh
                        n = h - grp * 16

                        @C.block.tensor
                        def _(te, n=n):
                            te.wait_ge(s_dve.h, s_dve.n)
                            s_pe.inc(te.matmul(C.PS(6)[:, 128:256], C.ONEROWF[:, :],
                                               C.ROWS[0:1, n * 128: (n + 1) * 128],
                                               start=True, stop=True))

                        @C.block.vector
                        def _(v, c0=c0):
                            v.wait_ge(s_pe.h, s_pe.n)
                            v.wait_ge(s_mm.h, s_mm.n)
                            s_dve.inc(v.tensor_tensor(cpt[:, 0:128], Cfm[:, c0:c0 + CH],
                                                      C.PS(6)[:, 128:256], op=ALU.mult))

                        @C.block.tensor
                        def _(te, h=h, n=n, hh=hh, ybank=ybank, ysl=ysl):
                            te.wait_ge(s_dve.h, s_dve.n)
                            base = hh * 64
                            tp = (0, base)
                            te.matmul(C.PS(ybank)[base:base + 64, ysl:ysl + 128],
                                      xt[:, h * 64:(h + 1) * 64],
                                      mpb[:, n * 128:(n + 1) * 128],
                                      start=True, stop=False, tile_position=tp)
                            ins = te.matmul(C.PS(ybank)[base:base + 64, ysl:ysl + 128],
                                            WS[:, h * 64:(h + 1) * 64], cpt[:, 0:128],
                                            start=False, stop=True, tile_position=tp)
                            s_mm.inc(ins)

                    @C.block.vector
                    def _(v, cl0=cl0, ec=ec, ybank=ybank, ysl=ysl):
                        v.wait_ge(s_mm.h, s_mm.n)
                        s_ev.inc(v.scalar_tensor_tensor(
                            Y[:, ec * QT + cl0: ec * QT + cl0 + CH],
                            xg[:, ec * 128:(ec + 1) * 128],
                            dpar[:, ec:ec + 1], C.PS(ybank)[:, ysl:ysl + 128],
                            op0=ALU.mult, op1=ALU.add))

            @C.block.scalar
            def _(sc, c=c, c0=c0):
                sc.wait_ge(s_ev.h, s_ev.n)
                s_act.inc(sc.activation(out=acol[:, 0:1], in_=gcum[:, c0 + CH - 1:c0 + CH],
                                        func=AF.Exp))

            @C.block.vector
            def _(v):
                v.wait_ge(s_act.h, s_act.n)
                s_dve.inc(v.tensor_scalar(egr[:, :], ones32[:, 0:64], acol[:, 0:1],
                                          None, op0=ALU.mult))

            @C.block.sync
            def _(sy):
                sy.wait_ge(s_dve.h, s_dve.n)
                s_fl.inc(sy.dma_start(out=C.ROWS[0:1, T:T + 2048], in_=egr[:, :]), 16)

            @C.block.tensor
            def _(te):
                te.wait_ge(s_fl.h, s_fl.n)
                te.wait_ge(s_dve.h, s_dve.n)
                for q in range(4):
                    ins = te.matmul(C.PS(q)[:, :], C.ONEROWF[:, :],
                                    C.ROWS[0:1, T + q * 512: T + (q + 1) * 512],
                                    start=True, stop=True)
                s_pe.inc(ins)

            @C.block.vector
            def _(v):
                v.wait_ge(s_pe.h, s_pe.n)
                for q in range(4):
                    v.wait_ge(s_dve.h, s_dve.n)
                    s_dve.inc(v.tensor_tensor(S[:, q * 512:(q + 1) * 512],
                                              S[:, q * 512:(q + 1) * 512],
                                              C.PS(q)[:, :], op=ALU.mult))

            for h in range(NH):
                @C.block.vector
                def _(v, h=h):
                    v.wait_ge(s_dve.h, s_dve.n)
                    v.wait_ge(s_mm.h, s_mm.n)
                    s_dve.inc(v.tensor_scalar(bdtt[:, :], btm[:, :], edty[:, h:h + 1], None,
                                              op0=ALU.mult))

                @C.block.tensor
                def _(te, h=h):
                    te.wait_ge(s_dve.h, s_dve.n)
                    ins = te.matmul(C.PS(h // 8)[:, (h % 8) * 64:(h % 8) * 64 + 64],
                                    bdtt[:, :], xt[:, h * 64:(h + 1) * 64],
                                    start=True, stop=True)
                    s_mm.inc(ins)

            @C.block.vector
            def _(v):
                v.wait_ge(s_mm.h, s_mm.n)
                for q in range(4):
                    v.wait_ge(s_dve.h, s_dve.n)
                    s_dve.inc(v.tensor_tensor(S[:, q * 512:(q + 1) * 512],
                                              S[:, q * 512:(q + 1) * 512],
                                              C.PS(q)[:, :], op=ALU.add))

        for ec in range(16):
            @C.block.gpsimd
            def _(g, ec=ec, half=half):
                g.wait_ge(s_dve.h, s_dve.n)
                g.wait_ge(s_xs.h, s_xs.n)
                s_xs.inc(g.dma_start(out=zt[:, :],
                                     in_=z_scr[ec * 128:(ec + 1) * 128,
                                               half * TT:(half + 1) * TT]), 16)

            @C.block.scalar
            def _(sc):
                sc.wait_ge(s_xs.h, s_xs.n)
                s_act.inc(sc.activation(out=C.LNTMP[:, 0:TT], in_=zt[:, :], func=AF.Silu))

            @C.block.vector
            def _(v, ec=ec):
                v.wait_ge(s_act.h, s_act.n)
                s_dve.inc(v.tensor_tensor(Y[:, ec * QT:(ec + 1) * QT],
                                          Y[:, ec * QT:(ec + 1) * QT],
                                          C.LNTMP[:, 0:TT], op=ALU.mult))

        emit_ln(C, Y, src_tiles=lambda ec: Y[:, ec * QT:(ec + 1) * QT], kind="rms",
                gamma_col=lambda ec: wssm[:, ec:ec + 1], nchunks=16, eps=1e-6, ntok=QT)

        for fc in range(DC):
            slot = fc % 2
            stream_panel("opT", 1024, fc, 128, slot * 2048, nkc=16)

            @C.block.tensor
            def _(te, fc=fc, slot=slot):
                te.wait_ge(s_w.h, s_w.n)
                te.wait_ge(s_ev.h, s_ev.n)
                if fc == 0:
                    te.wait_ge(s_dve.h, s_dve.n)
                for kc in range(16):
                    ins = te.matmul(C.PS(fc % 2)[:, :],
                                    WS[:, slot * 2048 + kc * 128: slot * 2048 + (kc + 1) * 128],
                                    Y[:, kc * QT: kc * QT + TT],
                                    start=(kc == 0), stop=(kc == 15))
                s_mm.inc(ins)

            @C.block.vector
            def _(v, fc=fc, half=half):
                v.wait_ge(s_mm.h, s_mm.n)
                s_ev.inc(v.tensor_tensor(lnx[:, fc * T + half * TT: fc * T + (half + 1) * TT],
                                         lnx[:, fc * T + half * TT: fc * T + (half + 1) * TT],
                                         C.PS(fc % 2)[:, :], op=ALU.add))

    for fc in range(DC):
        @C.block.vector
        def _(v, fc=fc):
            v.wait_ge(s_ev.h, s_ev.n)
            v.wait_ge(s_dve.h, s_dve.n)
            s_dve.inc(v.tensor_tensor(Xc(C, fc), Xc(C, fc), lnx[:, fc * T:(fc + 1) * T],
                                      op=ALU.add))

    emit_ln(C, lnx, src_tiles=lambda dc: lnx[:, dc * T:(dc + 1) * T], kind="rms",
            gamma_col=lambda dc: wpost[:, dc:dc + 1], eps=1e-6)
    u = lnx

    MT = TT
    l0_pan = {0: s_mm.n, 1: s_mm.n}
    s_wg = [C.sem("l0wA"), C.sem("l0wB")]
    Sb16 = S.bitcast(BF16)
    _midx = [Y, Sb16, Bfm, Cfm]

    def mid(fc):
        if fc < 16:
            return Y[:, fc * TT:(fc + 1) * TT]
        if fc < 24:
            return Sb16[:, (fc - 16) * TT:(fc - 15) * TT]
        if fc < 28:
            return Bfm[:, (fc - 24) * TT:(fc - 23) * TT]
        return Cfm[:, (fc - 28) * TT:(fc - 27) * TT]
    for i in range(T // MT):
        for fc in range(32):
            sg = fc % 2
            gtgt = l0_pan[sg]

            @C.block.gpsimd
            def _(g, fc=fc, sg=sg, gtgt=gtgt):
                if gtgt:
                    g.wait_ge(s_mm.h, gtgt)
                s_wg[sg].inc(g.dma_start(
                    out=WS[:, sg * 1024: (sg + 1) * 1024],
                    in_=bass.AP(io["gateT"], fc * 128, [[4096, 128], [128 * 4096, DC], [1, 128]])), 16)
                s_wg[sg].inc(g.dma_start(
                    out=WS[:, 2048 + sg * 1024: 2048 + (sg + 1) * 1024],
                    in_=bass.AP(io["upT"], fc * 128, [[4096, 128], [128 * 4096, DC], [1, 128]])), 16)

            @C.block.tensor
            def _(te, fc=fc, i=i, sg=sg):
                te.wait_ge(s_wg[sg].h, s_wg[sg].n)
                te.wait_ge(s_dve.h, s_dve.n)
                for kc in range(DC):
                    te.matmul(C.PS(sg)[:, 0:MT],
                              WS[:, sg * 1024 + kc * 128: sg * 1024 + (kc + 1) * 128],
                              u[:, kc * T + i * MT: kc * T + (i + 1) * MT],
                              start=(kc == 0), stop=(kc == DC - 1))
                for kc in range(DC):
                    ins = te.matmul(C.PS(2 + sg)[:, 0:MT],
                                    WS[:, 2048 + sg * 1024 + kc * 128: 2048 + sg * 1024 + (kc + 1) * 128],
                                    u[:, kc * T + i * MT: kc * T + (i + 1) * MT],
                                    start=(kc == 0), stop=(kc == DC - 1))
                s_mm.inc(ins)

            l0_pan[sg] = s_mm.n

            @C.block.scalar
            def _(sc, sg=sg):
                sc.wait_ge(s_mm.h, s_mm.n)
                sc.wait_ge(s_dve.h, s_dve.n)
                s_act.inc(sc.activation(out=C.LNTMP[:, 0:MT], in_=C.PS(sg)[:, 0:MT],
                                        func=AF.Silu))

            @C.block.vector
            def _(v, fc=fc, sg=sg):
                v.wait_ge(s_act.h, s_act.n)
                s_dve.inc(v.tensor_tensor(mid(fc), C.LNTMP[:, 0:MT],
                                          C.PS(2 + sg)[:, 0:MT], op=ALU.mult))

        mx = max(l0_pan[0], l0_pan[1])
        l0_pan[0] = l0_pan[1] = mx
        for dfc in range(DC):
            sg = dfc % 2
            gtgt = l0_pan[sg]

            @C.block.gpsimd
            def _(g, dfc=dfc, sg=sg, gtgt=gtgt):
                if gtgt:
                    g.wait_ge(s_mm.h, gtgt)
                s_wg[sg].inc(g.dma_start(
                    out=WS[:, sg * 4096: (sg + 1) * 4096],
                    in_=bass.AP(io["downT"], dfc * 128,
                                [[1024, 128], [128 * 1024, 32], [1, 128]])), 16)

            @C.block.tensor
            def _(te, dfc=dfc, sg=sg):
                te.wait_ge(s_wg[sg].h, s_wg[sg].n)
                te.wait_ge(s_dve.h, s_dve.n)
                te.wait_ge(s_ev.h, s_ev.n)
                for kc in range(32):
                    ins = te.matmul(C.PS(4 + sg)[:, 0:MT],
                                    WS[:, sg * 4096 + kc * 128: sg * 4096 + (kc + 1) * 128],
                                    mid(kc),
                                    start=(kc == 0), stop=(kc == 31))
                s_mm.inc(ins)

            l0_pan[sg] = s_mm.n

            @C.block.vector
            def _(v, dfc=dfc, i=i, sg=sg):
                v.wait_ge(s_mm.h, s_mm.n)
                s_ev.inc(v.tensor_tensor(Xc(C, dfc, i * MT, MT), Xc(C, dfc, i * MT, MT),
                                         C.PS(4 + sg)[:, 0:MT], op=ALU.add))

def emit_L1_attn(C):
    io = C.io
    DK, DV, CH = 128, 256, 128
    NCHF = 32
    s_dve = C.sem("dve")
    s_act = C.sem("act")
    s_pe = C.sem("pe")
    s_mm = C.sem("mm")
    s_wc = C.sem("wc")
    s_wp = [C.sem("wA"), C.sem("wB")]
    s_fl = C.sem("fl")
    s_ms = C.sem("ms")
    s_xs = C.sem("xs")
    s_cc = C.sem("cc")
    s_pl = C.sem("pl")
    RG = [[0, 1], [2, 3], [4, 5], [6, 7]]
    cc0 = s_cc.n
    pan_after = {0: [], 1: []}

    def panel_gate(g, par):
        if pan_after[par]:
            g.wait_ge(s_mm.h, pan_after[par][-1])

    pre3_in = C.nc.dram_tensor("l1_pre3_in", [3072, 3], F32)
    pre3_out = C.nc.dram_tensor("l1_pre3_out", [6144, 3], F32)
    bg_in = C.nc.dram_tensor("l1_bg_in", [12, T], F32)
    bg_out = C.nc.dram_tensor("l1_bg_out", [24, T], F32)
    qkv_in = C.nc.dram_tensor("l1_qkv_in", [3072, T], BF16)
    qkv_out = C.nc.dram_tensor("l1_qkv_out", [6144, T], BF16)
    qkv_b = C.nc.dram_tensor("l1_qkv_b", [1536, 4096], BF16)
    o_in = C.nc.dram_tensor("l1_o_in", [768, 4096], BF16)
    o_out = C.nc.dram_tensor("l1_o_out", [1536, 4096], BF16)

    for _cv in (-1.0, 1e-12, 1.28e-10):
        C.constf(_cv)

    lnq = C.sb("l1_lnx", [128, DC * T], BF16)
    BGA = C.sb("l1_bga", [6, 2 * T], F32)
    SP6 = C.sb("l1_sp6", [6, TT], F32)
    gc = C.sb("l1_gc", [3, 4096], F32)
    lnbm = C.sb("l1_lnbm", [3, 4096], BF16)
    LNBR = C.sb("l1_lnbr", [1, 256], BF16)
    RSC = C.sb("l1_rsc", [3, CH], F32)
    ones3 = C.sb("l1_o3", [3, CH], F32)
    wba = C.sb("l1_wba", [128, 96], BF16)
    cwq = C.sb("l1_cwq", [128, 24], F32)
    cwk = C.sb("l1_cwk", [128, 24], F32)
    cwv = C.sb("l1_cwv", [128, 48], F32)
    alog = C.sb("l1_al", [6, 1], F32)
    dtb = C.sb("l1_dtb", [6, 1], F32)
    acol = C.sb("l1_ac", [6, 1], F32)
    nw = C.sb("l1_nw", [128, 2], F32)
    issec = C.sb("l1_sec", [128, 1], F32)
    isel = C.sb("l1_isel", [128, 1], F32)
    idb = C.sb("l1_idb", [128, 128], BF16)
    mst = C.sb("l1_mst", [128, 128], F32)
    minc = C.sb("l1_minc", [128, 128], F32)
    msk2 = C.sb("l1_msk2", [128, 256], F32)
    WSL = C.sb("l1_wsl", [128, 4096], BF16)
    CHD = C.sb("l1_chd", [128, 2 * 1536], BF16)
    OB = C.sb("l1_ob", [128, 6 * TT], BF16)

    preb = C.LNTMP[:, 0:1026].bitcast(BF16)
    qs_e = C.LNTMP[:, 1026:1538]
    sq_e = C.LNTMP[:, 1538:1794].bitcast(BF16)
    qs_o = C.RB[:, 1024:1536]
    sq_o = C.RB[:, 1536:1792].bitcast(BF16)
    h3a = C.RB[:, 896:900]
    h3b = C.RB[:, 920:924]
    m3a = C.RB[:, 900:904]
    m3b = C.RB[:, 924:928]
    Dtmp = C.RB[:, 0:128]
    Dtmp2 = C.RB[:, 1024:1280]
    Sf = C.RB[:, 128:896]
    CS = C.RB[:, 904:920]
    POW = C.MB[:, 0:448].bitcast(BF16)
    Pp = C.MB[:, 448:576].bitcast(BF16)
    Mqt = C.MB[:, 576:640].bitcast(BF16)
    qgt = C.MB[:, 640:704].bitcast(BF16)
    kdt = C.MB[:, 704:768].bitcast(BF16)
    Ub = C.MB[:, 768:896].bitcast(BF16)
    Sb = C.MB[:, 896:1280].bitcast(BF16)
    VBt = C.MB[:, 1280:1408].bitcast(BF16)
    ott = C.MB[:, 1408:1536].bitcast(BF16)

    @C.block.gpsimd
    def _(g):
        s_wc.inc(g.dma_start(out=wba[:, :], in_=whole_w_ap(C, "Wba", 1024, 12)), 16)
        s_wc.inc(g.dma_start(out=cwq[:, :], in_=io["cwq_fm"][:, :]), 16)
        s_wc.inc(g.dma_start(out=cwk[:, :], in_=io["cwk_fm"][:, :]), 16)
        s_wc.inc(g.dma_start(out=cwv[:, :], in_=io["cwv_fm"][:, :]), 16)
        s_wc.inc(g.dma_start(out=alog[:, :], in_=io["l1_alog_c"][:, :]), 16)
        s_wc.inc(g.dma_start(out=dtb[:, :], in_=io["l1_dtb_c"][:, :]), 16)
        s_wc.inc(g.dma_start(out=nw[:, :], in_=io["nw_fm"][:, :]), 16)
        s_wc.inc(g.dma_start(out=issec[:, :], in_=io["is_second"][:, :]), 16)
        s_wc.inc(g.dma_start(out=idb[:, :], in_=io["ident_bf"][:, :]), 16)
        s_wc.inc(g.dma_start(out=mst[:, :], in_=io["mask_st"][:, :]), 16)
        s_wc.inc(g.dma_start(out=minc[:, :], in_=io["mask_lt"][:, :]), 16)
        s_wc.inc(g.dma_start(out=msk2[:, 0:128], in_=io["mask_st"][:, :]), 16)
        s_wc.inc(g.dma_start(out=msk2[:, 128:256], in_=io["mask_lt"][:, :]), 16)
        s_ms.inc(g.memset(ones3[:, :], 1.0))

    @C.block.vector
    def _(v):
        v.wait_ge(s_wc.h, s_wc.n)
        v.wait_ge(C.sem("ini").h, C.sem("ini").n)
        v.wait_ge(C.sem("ld").h, C.sem("ld").n)
        s_dve.inc(v.scalar_tensor_tensor(isel[:, :], issec[:, :], C.constf(-1.0),
                                         C.ONESF[:, :], op0=ALU.mult, op1=ALU.add))

    @C.block.scalar
    def _(sc):
        sc.wait_ge(s_wc.h, s_wc.n)
        s_act.inc(sc.activation(out=acol[:, :], in_=alog[:, :], func=AF.Exp))

    @C.block.vector
    def _(v):
        v.wait_ge(s_act.h, s_act.n)
        s_dve.inc(v.tensor_scalar(acol[:, :], acol[:, :], -1.0, None, op0=ALU.mult))

    emit_ln(C, lnq, kind="ln")

    b6 = BGA[0:6, 0:T]
    g6 = BGA[0:6, T:2 * T]
    for i in range(NTT):
        @C.block.tensor
        def _(te, i=i):
            te.wait_ge(s_wc.h, s_wc.n)
            te.wait_ge(s_dve.h, s_dve.n)
            te.wait_ge(s_act.h, s_act.n)
            for kc in range(DC):
                te.matmul(C.PS(4)[0:6, 0:TT], wba[:, kc * 12: kc * 12 + 6],
                          lnq[:, kc * T + i * TT: kc * T + (i + 1) * TT],
                          start=(kc == 0), stop=(kc == DC - 1))
                ins = te.matmul(C.PS(5)[0:6, 0:TT], wba[:, kc * 12 + 6: kc * 12 + 12],
                                lnq[:, kc * T + i * TT: kc * T + (i + 1) * TT],
                                start=(kc == 0), stop=(kc == DC - 1))
            s_mm.inc(ins)

        @C.block.scalar
        def _(sc, i=i):
            sc.wait_ge(s_mm.h, s_mm.n)
            s_act.inc(sc.activation(out=b6[:, i * TT:(i + 1) * TT],
                                    in_=C.PS(4)[0:6, 0:TT], func=AF.Sigmoid))

        @C.block.vector
        def _(v, i=i):
            v.wait_ge(s_mm.h, s_mm.n)
            s_dve.inc(v.tensor_scalar(g6[:, i * TT:(i + 1) * TT], C.PS(5)[0:6, 0:TT],
                                      dtb[:, 0:1], None, op0=ALU.add))

    for i in range(NTT):
        g6t = g6[:, i * TT:(i + 1) * TT]

        @C.block.scalar
        def _(sc, g6t=g6t, i=i):
            sc.wait_ge(s_dve.h, s_dve.n)
            sc.wait_ge(s_act.h, s_act.n)
            s_act.inc(sc.activation(out=SP6[:, :], in_=g6t, func=AF.Abs))
            sc.wait_ge(s_act.h, s_act.n)
            s_act.inc(sc.activation(out=SP6[:, :], in_=SP6[:, :], func=AF.Exp, scale=-1.0))
            sc.wait_ge(s_act.h, s_act.n)
            s_act.inc(sc.activation(out=SP6[:, :], in_=SP6[:, :], func=AF.Ln,
                                    bias=C.constf(1.0)[0:6, :]))

        @C.block.vector
        def _(v, g6t=g6t):
            v.wait_ge(s_act.h, s_act.n)
            s_dve.inc(v.tensor_scalar(g6t, g6t, 0.0, None, op0=ALU.max))
            v.wait_ge(s_dve.h, s_dve.n)
            s_dve.inc(v.tensor_tensor(g6t, g6t, SP6[:, :], op=ALU.add))
            v.wait_ge(s_dve.h, s_dve.n)
            s_dve.inc(v.tensor_scalar(g6t, g6t, acol[:, 0:1], None, op0=ALU.mult))

    @C.block.scalar
    def _(sc):
        sc.wait_ge(s_dve.h, s_dve.n)
        s_act.inc(sc.activation(out=b6[:, :], in_=b6[:, :], func=AF.Ln))

    @C.block.sync
    def _(sy):
        sy.wait_ge(s_dve.h, s_dve.n)
        sy.wait_ge(s_act.h, s_act.n)
        s_fl.inc(sy.dma_start(out=bg_in[0:6, :], in_=b6[:, :]), 16)
        s_fl.inc(sy.dma_start(out=bg_in[6:12, :], in_=g6[:, :]), 16)

    fl_bg = s_fl.n

    @C.block.gpsimd
    def _(g):
        g.wait_ge(s_fl.h, fl_bg)
        s_cc.inc(g.collective_compute("AllGather", ALU.bypass, replica_groups=RG,
                                      ins=[bg_in.ap().opt()], outs=[bg_out.ap().opt()]))

    PARTS = [("Wq", 768, 6, 0), ("Wk", 768, 6, 768), ("Wv", 1536, 12, 1536)]
    mini_idx = 0
    for wname, dout, nch, roff0 in PARTS:
        for ch in range(nch):
            slot = mini_idx % 2

            @C.block.gpsimd
            def _(g, wname=wname, dout=dout, ch=ch, slot=slot):
                panel_gate(g, slot)
                s_wp[slot].inc(g.dma_start(
                    out=WSL[:, slot * 1024: slot * 1024 + 1024],
                    in_=bass.AP(io[wname], ch * 128,
                                [[dout, 128], [128 * dout, 8], [1, 128]])), 16)

            @C.block.tensor
            def _(te, slot=slot, mini_idx=mini_idx):
                te.wait_ge(s_wp[slot].h, s_wp[slot].n)
                te.wait_ge(s_dve.h, s_dve.n)
                for kc in range(DC):
                    ins = te.matmul(C.PS(4 + 2 * (mini_idx % 2))[:, 0:3],
                                    WSL[:, slot * 1024 + kc * 128: slot * 1024 + (kc + 1) * 128],
                                    lnq[:, kc * T + T - 3: kc * T + T],
                                    start=(kc == 0), stop=(kc == DC - 1))
                s_mm.inc(ins)

            pan_after[slot].append(s_mm.n)

            m3 = m3a if mini_idx % 2 == 0 else m3b
            flb = s_fl.n

            @C.block.vector
            def _(v, m3=m3, flb=flb, mini_idx=mini_idx):
                v.wait_ge(s_mm.h, s_mm.n)
                if mini_idx >= 2:
                    v.wait_ge(s_fl.h, flb)
                s_dve.inc(v.tensor_copy(m3[:, 0:3], C.PS(4 + 2 * (mini_idx % 2))[:, 0:3]))

            @C.block.sync
            def _(sy, m3=m3, roff=roff0 + ch * 128):
                sy.wait_ge(s_dve.h, s_dve.n)
                s_fl.inc(sy.dma_start(
                    out=bass.AP(pre3_in, roff * 3, [[3, 128], [1, 3]]),
                    in_=m3[:, 0:3]), 16)

            mini_idx += 1

    fl_minis = s_fl.n

    @C.block.gpsimd
    def _(g):
        g.wait_ge(s_fl.h, fl_minis)
        s_cc.inc(g.collective_compute("AllGather", ALU.bypass, replica_groups=RG,
                                      ins=[pre3_in.ap().opt()], outs=[pre3_out.ap().opt()]))

    pidx = 0
    for wname, dout, nch, roff0 in PARTS:
        cwt = {"Wq": cwq, "Wk": cwk, "Wv": cwv}[wname]
        for ch in range(nch):
            slot = pidx % 2

            @C.block.gpsimd
            def _(g, wname=wname, dout=dout, ch=ch, slot=slot):
                panel_gate(g, slot)
                s_wp[slot].inc(g.dma_start(
                    out=WSL[:, slot * 1024: slot * 1024 + 1024],
                    in_=bass.AP(io[wname], ch * 128,
                                [[dout, 128], [128 * dout, 8], [1, 128]])), 16)

            dve_conv_prev = s_dve.n
            pl_conv_prev = s_pl.n

            @C.block.tensor
            def _(te, slot=slot, pidx=pidx):
                te.wait_ge(s_wp[slot].h, s_wp[slot].n)
                te.wait_ge(s_act.h, s_act.n)
                te.wait_ge(s_dve.h, s_dve.n)
                te.wait_ge(s_pl.h, s_pl.n)
                for i in range(NTT):
                    for kc in range(DC):
                        ins = te.matmul(C.PS(i)[:, :],
                                        WSL[:, slot * 1024 + kc * 128: slot * 1024 + (kc + 1) * 128],
                                        lnq[:, kc * T + i * TT: kc * T + (i + 1) * TT],
                                        start=(kc == 0), stop=(kc == DC - 1))
                    s_mm.inc(ins)

            pan_after[slot].append(s_mm.n)
            mmb = s_mm.n

            @C.block.scalar
            def _(sc, pidx=pidx, mmb=mmb, dve_conv_prev=dve_conv_prev,
                  pl_conv_prev=pl_conv_prev):
                sc.wait_ge(s_dve.h, dve_conv_prev)
                sc.wait_ge(s_pl.h, pl_conv_prev)
                for i in range(NTT):
                    sc.wait_ge(s_mm.h, mmb - NTT + i + 1)
                    s_act.inc(sc.activation(out=preb[:, 3 + i * TT: 3 + (i + 1) * TT],
                                            in_=C.PS(i)[:, :], func=AF.Identity))

            h3 = h3a if pidx % 2 == 0 else h3b
            roff = roff0 + ch * 128

            @C.block.sync
            def _(sy, h3=h3, roff=roff):
                sy.wait_ge(s_cc.h, cc0 + 2)
                sy.wait_ge(s_dve.h, s_dve.n)
                s_fl.inc(sy.dma_start(out=h3[:, 0:3],
                                      in_=bass.AP(pre3_out, roff * 3, [[3, 128], [1, 3]])), 16)

            flh = s_fl.n

            @C.block.vector
            def _(v, h3=h3, flh=flh):
                v.wait_ge(s_fl.h, flh)
                v.wait_ge(s_act.h, s_act.n)
                s_dve.inc(v.tensor_scalar(preb[:, 0:3], h3[:, 0:3], issec[:, 0:1],
                                          None, op0=ALU.mult))

            for i in range(NTT):
                qs = qs_e if i % 2 == 0 else qs_o
                sqt = sq_e if i % 2 == 0 else sq_o

                @C.block.vector
                def _(v, i=i, qs=qs, cwt=cwt, ch=ch):
                    v.wait_ge(s_dve.h, s_dve.n)
                    v.wait_ge(s_act.h, s_act.n)
                    s_dve.inc(v.tensor_scalar(qs[:, :], preb[:, i * TT: i * TT + TT],
                                              cwt[:, ch * 4: ch * 4 + 1], None, op0=ALU.mult))
                    for k in range(1, 4):
                        v.wait_ge(s_dve.h, s_dve.n)
                        s_dve.inc(v.scalar_tensor_tensor(qs[:, :],
                                                         preb[:, i * TT + k: i * TT + k + TT],
                                                         cwt[:, ch * 4 + k: ch * 4 + k + 1],
                                                         qs[:, :], op0=ALU.mult, op1=ALU.add))

                @C.block.scalar
                def _(sc, qs=qs):
                    sc.wait_ge(s_dve.h, s_dve.n)
                    s_act.inc(sc.activation(out=qs[:, :], in_=qs[:, :], func=AF.Silu))

                if wname == "Wv":
                    @C.block.vector
                    def _(v, qs=qs, sqt=sqt):
                        v.wait_ge(s_act.h, s_act.n)
                        v.wait_ge(s_xs.h, s_xs.n)
                        s_dve.inc(v.tensor_copy(sqt[:, :], qs[:, :]))

                    @C.block.gpsimd
                    def _(g, sqt=sqt, roff=roff, i=i):
                        g.wait_ge(s_dve.h, s_dve.n)
                        s_xs.inc(g.dma_start(
                            out=bass.AP(qkv_in, roff * T + i * TT, [[T, 128], [1, TT]]),
                            in_=sqt[:, :]), 16)
                else:
                    scl = 128.0 if wname == "Wq" else 1.0
                    bia = 1.28e-10 if wname == "Wq" else 1e-12

                    @C.block.vector
                    def _(v, qs=qs, sqt=sqt):
                        v.wait_ge(s_act.h, s_act.n)
                        v.wait_ge(s_mm.h, s_mm.n)
                        v.wait_ge(s_xs.h, s_xs.n)
                        s_dve.inc(v.tensor_mul(sqt[:, :], qs[:, :], qs[:, :]))

                    @C.block.tensor
                    def _(te, sqt=sqt, i=i):
                        te.wait_ge(s_dve.h, s_dve.n)
                        s_mm.inc(te.matmul(C.PS(4 + i % 2)[0:1, 0:TT], C.ONESB[:, :],
                                           sqt[:, :], start=True, stop=True))

                    rrow = C.ROWS[0:1, 2048 + (i % 2) * TT: 2048 + (i % 2 + 1) * TT]

                    @C.block.scalar
                    def _(sc, rrow=rrow, i=i, scl=scl, bia=bia):
                        sc.wait_ge(s_mm.h, s_mm.n)
                        s_act.inc(sc.activation(out=rrow, in_=C.PS(4 + i % 2)[0:1, 0:TT],
                                                func=AF.Sqrt, scale=scl,
                                                bias=C.constf(bia)[0:1, :]))

                    @C.block.vector
                    def _(v, rrow=rrow, i=i):
                        v.wait_ge(s_act.h, s_act.n)
                        s_dve.inc(v.reciprocal(rrow, rrow))

                    @C.block.tensor
                    def _(te, rrow=rrow, i=i):
                        te.wait_ge(s_dve.h, s_dve.n)
                        s_pe.inc(te.matmul(C.PS(6)[:, 0:TT], C.ONEROWF[:, :], rrow,
                                           start=True, stop=True))

                    @C.block.vector
                    def _(v, qs=qs, sqt=sqt, i=i):
                        v.wait_ge(s_pe.h, s_pe.n)
                        v.wait_ge(s_xs.h, s_xs.n)
                        s_dve.inc(v.tensor_tensor(sqt[:, :], qs[:, :],
                                                  C.PS(6)[:, 0:TT], op=ALU.mult))

                    @C.block.gpsimd
                    def _(g, sqt=sqt, roff=roff, i=i):
                        g.wait_ge(s_dve.h, s_dve.n)
                        s_xs.inc(g.dma_start(
                            out=bass.AP(qkv_in, roff * T + i * TT, [[T, 128], [1, TT]]),
                            in_=sqt[:, :]), 16)

            pidx += 1

    xs_qkv = s_xs.n

    @C.block.gpsimd
    def _(g):
        g.wait_ge(s_xs.h, xs_qkv)
        s_cc.inc(g.collective_compute("AllGather", ALU.bypass, replica_groups=RG,
                                      ins=[qkv_in.ap().opt()], outs=[qkv_out.ap().opt()]))

    rowA = C.ROWS[0:1, 0:T]
    rowB = C.ROWS[0:1, T:2 * T]
    rowT = C.LNTMP[0:1, 0:T]
    rowO = C.RB[0:1, 0:T]
    for which in range(2):
        for hl in range(3):
            for half in range(2):
                ra = half * 12 + which * 6 + hl
                rb = ra + 3

                @C.block.sync
                def _(sy, ra=ra, rb=rb):
                    sy.wait_ge(s_cc.h, cc0 + 1)
                    sy.wait_ge(s_dve.h, s_dve.n)
                    s_fl.inc(sy.dma_start(out=rowA, in_=bg_out[ra:ra + 1, :]), 16)
                    s_fl.inc(sy.dma_start(out=rowB, in_=bg_out[rb:rb + 1, :]), 16)

                flr = s_fl.n

                @C.block.vector
                def _(v, flr=flr):
                    v.wait_ge(s_fl.h, flr)
                    v.wait_ge(s_xs.h, s_xs.n)
                    s_dve.inc(v.tensor_scalar(rowT, rowA, isel[0:1, 0:1], None, op0=ALU.mult))
                    v.wait_ge(s_dve.h, s_dve.n)
                    s_dve.inc(v.scalar_tensor_tensor(rowO, rowB, issec[0:1, 0:1], rowT,
                                                     op0=ALU.mult, op1=ALU.add))

                @C.block.gpsimd
                def _(g, which=which, hl=hl, half=half):
                    g.wait_ge(s_dve.h, s_dve.n)
                    dst = (lnbm if which == 0 else gc)
                    s_xs.inc(g.dma_start(out=dst[hl:hl + 1, half * T:(half + 1) * T],
                                         in_=rowO), 16)

    @C.block.vector
    def _(v):
        v.wait_ge(s_xs.h, s_xs.n)
        v.wait_ge(s_ms.h, s_ms.n)
        for c in range(NCHF):
            v.wait_ge(s_dve.h, s_dve.n)
            s_dve.inc(v.tensor_tensor_scan(RSC[:, :], ones3[:, :],
                                           gc[:, c * CH:(c + 1) * CH], 0.0,
                                           op0=ALU.mult, op1=ALU.add))
            v.wait_ge(s_dve.h, s_dve.n)
            s_dve.inc(v.tensor_copy(gc[:, c * CH:(c + 1) * CH], RSC[:, :]))

    @C.block.gpsimd
    def _(g):
        g.wait_ge(s_dve.h, s_dve.n)
        g.wait_ge(s_xs.h, s_xs.n)
        g.wait_ge(s_act.h, s_act.n)
        s_ms.inc(g.memset(Sf[:, :], 0.0))
        s_ms.inc(g.memset(Sb[:, :], 0.0))

    st_tiles = []
    for hl in range(3):
        st_tiles.append((hl * 128, (hl + 3) * 128, hl * 128))
        st_tiles.append((768 + hl * 128, 768 + (hl + 3) * 128, 384 + hl * 128))
        for vh in range(2):
            st_tiles.append((1536 + hl * 256 + vh * 128, 1536 + (hl + 3) * 256 + vh * 128,
                             768 + hl * 256 + vh * 128))
    sidx = 0
    for ra, rb, rd in st_tiles:
        for half in range(2):
            for i in range(NTT):
                sl = (sidx % 2) * 1536
                ldA = WSL[:, sl: sl + TT]
                ldB = WSL[:, sl + TT: sl + 2 * TT]
                out = WSL[:, sl + 2 * TT: sl + 3 * TT]
                dvb = s_dve.n

                @C.block.sync
                def _(sy, ra=ra, rb=rb, half=half, i=i, ldA=ldA, ldB=ldB,
                      dvb=dvb, sidx=sidx):
                    sy.wait_ge(s_cc.h, cc0 + 3)
                    sy.wait_ge(s_dve.h, dvb)
                    s_fl.inc(sy.dma_start(
                        out=ldA, in_=bass.AP(qkv_out, (3072 * half + ra) * T + i * TT,
                                             [[T, 128], [1, TT]])), 16)
                    s_fl.inc(sy.dma_start(
                        out=ldB, in_=bass.AP(qkv_out, (3072 * half + rb) * T + i * TT,
                                             [[T, 128], [1, TT]])), 16)

                fls = s_fl.n

                @C.block.vector
                def _(v, ldA=ldA, ldB=ldB, out=out, fls=fls):
                    v.wait_ge(s_fl.h, fls)
                    v.wait_ge(s_xs.h, s_xs.n)
                    s_dve.inc(v.tensor_scalar(out, ldA, isel[:, 0:1], None, op0=ALU.mult))
                    v.wait_ge(s_dve.h, s_dve.n)
                    s_dve.inc(v.scalar_tensor_tensor(out, ldB, issec[:, 0:1], out,
                                                     op0=ALU.mult, op1=ALU.add))

                @C.block.gpsimd
                def _(g, out=out, rd=rd, half=half, i=i):
                    g.wait_ge(s_dve.h, s_dve.n)
                    s_xs.inc(g.dma_start(
                        out=bass.AP(qkv_b, rd * 4096 + half * T + i * TT,
                                    [[4096, 128], [1, TT]]),
                        in_=out), 16)

                sidx += 1

    chunk_mm_end = {}
    xs_stage = s_xs.n
    dve_floor = s_dve.n
    act_floor = s_act.n
    mm_floor = s_mm.n
    mk_dve_end = {}
    mk_act_rows = {}
    mk_mm_g1 = {}
    mk_dve_ub = {}
    mk_act_sb = {}
    s_fr = [C.sem("frA"), C.sem("frB")]
    s_fc = [C.sem("fcA"), C.sem("fcB")]
    for c in range(NCHF):
        csl = (c % 2) * 1536

        @C.block.gpsimd
        def _(g, c=c, csl=csl):
            if c == 0:
                g.wait_ge(s_xs.h, xs_stage)
            if c >= 2:
                g.wait_ge(s_mm.h, chunk_mm_end[c - 2])
            s_xs.inc(g.dma_start(
                out=CHD[:, csl: csl + 1536],
                in_=bass.AP(qkv_b, c * 128, [[4096, 128], [128 * 4096, 12], [1, 128]])), 16)

        xs_chd = s_xs.n

        for hl in range(3):
            it = c * 3 + hl
            p = it % 2
            po = p * 1024
            pc8 = p * 8
            gbrow = C.ROWS[0:1, po + 0: po + 128]
            grow = C.ROWS[0:1, po + 128: po + 256]
            egrow = C.ROWS[0:1, po + 256: po + 384]
            edgrow = C.ROWS[0:1, po + 384: po + 512]
            nebgrow = C.ROWS[0:1, po + 512: po + 640]
            brow = C.ROWS[0:1, po + 640: po + 768]
            lnbr = LNBR[0:1, p * 128:(p + 1) * 128]
            gcol = CS[:, pc8 + 0: pc8 + 1]
            edgcol = CS[:, pc8 + 1: pc8 + 2]
            nebgcol = CS[:, pc8 + 2: pc8 + 3]
            bcol = CS[:, pc8 + 3: pc8 + 4]
            egcol = CS[:, pc8 + 4: pc8 + 5]
            egG = CS[:, pc8 + 5: pc8 + 6]
            qc = CHD[:, csl + hl * 128: csl + (hl + 1) * 128]
            kc = CHD[:, csl + 384 + hl * 128: csl + 384 + (hl + 1) * 128]
            vc0 = CHD[:, csl + 768 + hl * 256: csl + 768 + hl * 256 + 128]
            vc1 = CHD[:, csl + 768 + hl * 256 + 128: csl + 768 + (hl + 1) * 256]
            Sbh = Sb[:, hl * 256:(hl + 1) * 256]
            Sfh = Sf[:, hl * 256:(hl + 1) * 256]

            dve_tgt = mk_dve_end.get(it - 2, dve_floor)
            act_tgt = mk_act_rows.get(it - 2, act_floor)
            mm_tgt = mk_mm_g1.get(it - 2, mm_floor)

            @C.block.sync
            def _(sy, c=c, hl=hl, grow=grow, lnbr=lnbr, dve_tgt=dve_tgt,
                  act_tgt=act_tgt, mm_tgt=mm_tgt, p=p):
                sy.wait_ge(s_dve.h, dve_tgt)
                sy.wait_ge(s_act.h, act_tgt)
                sy.wait_ge(s_mm.h, mm_tgt)
                s_fr[p].inc(sy.dma_start(out=grow, in_=gc[hl:hl + 1, c * CH:(c + 1) * CH]), 16)
                s_fr[p].inc(sy.dma_start(out=lnbr, in_=lnbm[hl:hl + 1, c * CH:(c + 1) * CH]), 16)

            flr = s_fr[p].n

            @C.block.vector
            def _(v, grow=grow, gbrow=gbrow, lnbr=lnbr, flr=flr, p=p):
                v.wait_ge(s_fr[p].h, flr)
                s_dve.inc(v.tensor_tensor(gbrow, grow, lnbr, op=ALU.add))

            @C.block.scalar
            def _(sc, grow=grow, gbrow=gbrow, egrow=egrow, edgrow=edgrow,
                  nebgrow=nebgrow, brow=brow, lnbr=lnbr, po=po):
                sc.wait_ge(s_dve.h, s_dve.n)
                s_act.inc(sc.activation(out=egrow, in_=grow, func=AF.Exp))
                s_act.inc(sc.activation(out=edgrow, in_=grow, func=AF.Exp, scale=-1.0,
                                        bias=C.ROWS[0:1, po + 255: po + 256]))
                s_act.inc(sc.activation(out=nebgrow, in_=gbrow, func=AF.Exp))
                s_act.inc(sc.activation(out=brow, in_=lnbr, func=AF.Exp))

            mk_act_rows[it] = s_act.n

            @C.block.vector
            def _(v, nebgrow=nebgrow):
                v.wait_ge(s_act.h, s_act.n)
                s_dve.inc(v.tensor_scalar(nebgrow, nebgrow, -1.0, None, op0=ALU.mult))

            @C.block.sync
            def _(sy, grow=grow, edgrow=edgrow, nebgrow=nebgrow, brow=brow,
                  egrow=egrow, gcol=gcol, edgcol=edgcol, nebgcol=nebgcol,
                  bcol=bcol, egcol=egcol, p=p):
                sy.wait_ge(s_dve.h, s_dve.n)
                sy.wait_ge(s_act.h, s_act.n)
                s_fc[p].inc(sy.dma_start(out=gcol, in_=grow), 16)
                s_fc[p].inc(sy.dma_start(out=edgcol, in_=edgrow), 16)
                s_fc[p].inc(sy.dma_start(out=nebgcol, in_=nebgrow), 16)
                s_fc[p].inc(sy.dma_start(out=bcol, in_=brow), 16)
                s_fc[p].inc(sy.dma_start(out=egcol, in_=egrow), 16)

            flc = s_fc[p].n

            @C.block.vector
            def _(v, egG=egG, edgcol=edgcol, egcol=egcol, flc=flc, p=p):
                v.wait_ge(s_fc[p].h, flc)
                s_dve.inc(v.tensor_tensor(egG, edgcol, egcol, op=ALU.mult))

            dve_tgt = mk_dve_ub.get(it - 1, dve_floor)
            act_tgt = mk_act_sb.get(it - 3, act_floor)

            @C.block.tensor
            def _(te, qc=qc, kc=kc, vc0=vc0, vc1=vc1, Sbh=Sbh, grow=grow,
                  gbrow=gbrow, egrow=egrow, flc=flc, xs_chd=xs_chd,
                  dve_tgt=dve_tgt, act_tgt=act_tgt, p=p):
                te.wait_ge(s_xs.h, xs_chd)
                te.wait_ge(s_fc[p].h, flc)
                te.wait_ge(s_dve.h, dve_tgt)
                te.wait_ge(s_act.h, act_tgt)
                s_mm.inc(te.matmul(C.PS(0)[:, 0:128], kc, kc, start=True, stop=True))
                s_mm.inc(te.matmul(C.PS(0)[:, 128:256], kc, qc, start=True, stop=True))
                s_mm.inc(te.matmul(C.PS(1)[:, 0:384], C.ONEROWF[:, :],
                                   C.ROWS[0:1, po + 0: po + 384],
                                   start=True, stop=True))
                s_pe.inc(te.transpose(C.PSB[:, 0:128], kc, idb[:, :]))
                s_pe.inc(te.transpose(C.PSB[:, 128:256], vc0, idb[:, :]))
                s_pe.inc(te.transpose(C.PSB[:, 256:384], vc1, idb[:, :]))
                s_mm.inc(te.matmul(C.PS(2)[:, 0:256], kc, Sbh, start=True, stop=True))

            mm_g1 = s_mm.n
            pe_g1 = s_pe.n
            mk_mm_g1[it] = s_mm.n

            @C.block.vector
            def _(v, gcol=gcol, mm_g1=mm_g1):
                v.wait_ge(s_mm.h, mm_g1)
                v.wait_ge(s_wc.h, s_wc.n)
                s_dve.inc(v.scalar_tensor_tensor(Dtmp2[:, :], C.PS(1)[:, 0:256], gcol,
                                                 msk2[:, :], op0=ALU.subtract, op1=ALU.add))

            @C.block.scalar
            def _(sc):
                sc.wait_ge(s_dve.h, s_dve.n)
                s_act.inc(sc.activation(out=Dtmp2[:, :], in_=Dtmp2[:, :], func=AF.Exp))

            @C.block.vector
            def _(v):
                v.wait_ge(s_act.h, s_act.n)
                s_dve.inc(v.scalar_tensor_tensor(POW[:, 0:128], Dtmp2[:, 0:128],
                                                 C.constf(-1.0), C.PS(0)[:, 0:128],
                                                 op0=ALU.mult, op1=ALU.mult))
                s_dve.inc(v.tensor_tensor(Mqt[:, :], Dtmp2[:, 128:256],
                                          C.PS(0)[:, 128:256], op=ALU.mult))

            @C.block.tensor
            def _(te):
                te.wait_ge(s_dve.h, s_dve.n)
                s_pe.inc(te.transpose(C.PSB[:, 384:512], POW[:, 0:128], idb[:, :]))

            @C.block.vector
            def _(v):
                v.wait_ge(s_pe.h, s_pe.n)
                s_dve.inc(v.tensor_copy(Pp[:, 0:128], C.PSB[:, 384:512]))

            @C.block.vector
            def _(v, qc=qc, edgcol=edgcol, bcol=bcol, pe_g1=pe_g1):
                v.wait_ge(s_pe.h, pe_g1)
                v.wait_ge(s_mm.h, s_mm.n)
                s_dve.inc(v.tensor_tensor(qgt[:, :], qc, C.PS(1)[:, 256:384], op=ALU.mult))
                s_dve.inc(v.tensor_scalar(kdt[:, :], C.PSB[:, 0:128], edgcol, None,
                                          op0=ALU.mult))
                s_dve.inc(v.tensor_scalar(VBt[:, :], C.PSB[:, 128:384], bcol, None,
                                          op0=ALU.mult))

            @C.block.vector
            def _(v, nebgcol=nebgcol):
                v.wait_ge(s_mm.h, s_mm.n)
                v.wait_ge(s_dve.h, s_dve.n)
                s_dve.inc(v.scalar_tensor_tensor(Ub[:, :], C.PS(2)[:, 0:256], nebgcol,
                                                 VBt[:, :], op0=ALU.mult, op1=ALU.add))

            mk_dve_ub[it] = s_dve.n

            for i in range(7):
                @C.block.tensor
                def _(te, i=i):
                    te.wait_ge(s_dve.h, s_dve.n)
                    s_mm.inc(te.matmul(C.PS(3)[:, 0:256], POW[:, i * 128:(i + 1) * 128],
                                       Ub[:, :], start=True, stop=True))
                    if i < 5:
                        s_mm.inc(te.matmul(C.PS(0)[:, 256:384],
                                           POW[:, i * 128:(i + 1) * 128],
                                           Pp[:, (i % 2) * 128:(i % 2 + 1) * 128],
                                           start=True, stop=True))
                    if i < 6:
                        s_mm.inc(te.matmul(C.PS(0)[:, 384:512],
                                           Pp[:, (i % 2) * 128:(i % 2 + 1) * 128],
                                           POW[:, i * 128:(i + 1) * 128],
                                           start=True, stop=True))

                @C.block.vector
                def _(v, i=i):
                    v.wait_ge(s_mm.h, s_mm.n)
                    s_dve.inc(v.tensor_tensor(Ub[:, :], Ub[:, :], C.PS(3)[:, 0:256],
                                              op=ALU.add))
                    if i < 6:
                        s_dve.inc(v.tensor_copy(POW[:, (i + 1) * 128:(i + 2) * 128],
                                                C.PS(0)[:, 384:512]))
                    if i < 5:
                        s_dve.inc(v.tensor_copy(Pp[:, ((i + 1) % 2) * 128:((i + 1) % 2 + 1) * 128],
                                                C.PS(0)[:, 256:384]))

            @C.block.tensor
            def _(te, Sbh=Sbh):
                te.wait_ge(s_dve.h, s_dve.n)
                te.wait_ge(s_act.h, s_act.n)
                for vh in range(2):
                    te.matmul(C.PS(5)[:, vh * 128:(vh + 1) * 128],
                              Ub[:, vh * 128:(vh + 1) * 128], Mqt[:, :],
                              start=True, stop=False)
                    s_mm.inc(te.matmul(C.PS(5)[:, vh * 128:(vh + 1) * 128],
                                       Sbh[:, vh * 128:(vh + 1) * 128], qgt[:, :],
                                       start=False, stop=True))
                s_mm.inc(te.matmul(C.PS(4)[:, 0:256], kdt[:, :], Ub[:, :],
                                   start=True, stop=True))

            mm_o = s_mm.n

            @C.block.vector
            def _(v, mm_o=mm_o):
                v.wait_ge(s_mm.h, mm_o - 1)
                v.wait_ge(s_xs.h, s_xs.n)
                s_dve.inc(v.tensor_copy(ott[:, :], C.PS(5)[:, 0:256]))

            @C.block.gpsimd
            def _(g, hl=hl, c=c):
                g.wait_ge(s_dve.h, s_dve.n)
                for vh in range(2):
                    s_xs.inc(g.dma_start(
                        out=bass.AP(o_in, (hl * 256 + vh * 128) * 4096 + c * 128,
                                    [[4096, 128], [1, 128]]),
                        in_=ott[:, vh * 128:(vh + 1) * 128]), 16)

            @C.block.vector
            def _(v, Sfh=Sfh, Sbh=Sbh, egG=egG, mm_o=mm_o):
                v.wait_ge(s_mm.h, mm_o)
                s_dve.inc(v.scalar_tensor_tensor(Sfh, Sfh, egG, C.PS(4)[:, 0:256],
                                                 op0=ALU.mult, op1=ALU.add))
                v.wait_ge(s_dve.h, s_dve.n)
                s_dve.inc(v.tensor_copy(Sbh, Sfh))

            mk_dve_end[it] = s_dve.n
            mk_act_sb[it] = s_act.n

        chunk_mm_end[c] = s_mm.n

    xs_o = s_xs.n

    @C.block.gpsimd
    def _(g):
        g.wait_ge(s_xs.h, xs_o)
        s_cc.inc(g.collective_compute("AllGather", ALU.bypass, replica_groups=RG,
                                      ins=[o_in.ap().opt()], outs=[o_out.ap().opt()]))

    OL = C.LNTMP[:, 0:1024].bitcast(BF16)
    GT = C.LNTMP[:, 1024:1280].bitcast(BF16)
    sqp = C.LNTMP[:, 1280:1536].bitcast(BF16)
    for i in range(NTT):
        for grp in range(2):
            for lv in range(6):
                vc = grp * 6 + lv
                sl = (lv % 2) * 1024
                ldA = OL[:, sl: sl + TT]
                ldB = OL[:, sl + TT: sl + 2 * TT]
                obv = OB[:, lv * TT:(lv + 1) * TT]
                dvb = s_dve.n

                @C.block.sync
                def _(sy, vc=vc, i=i, ldA=ldA, ldB=ldB, dvb=dvb):
                    sy.wait_ge(s_cc.h, cc0 + 4)
                    sy.wait_ge(s_dve.h, dvb)
                    s_fl.inc(sy.dma_start(
                        out=ldA, in_=bass.AP(o_out, vc * 128 * 4096 + i * TT,
                                             [[4096, 128], [1, TT]])), 16)
                    s_fl.inc(sy.dma_start(
                        out=ldB, in_=bass.AP(o_out, vc * 128 * 4096 + T + i * TT,
                                             [[4096, 128], [1, TT]])), 16)

                fll = s_fl.n

                @C.block.vector
                def _(v, ldA=ldA, ldB=ldB, obv=obv, fll=fll):
                    v.wait_ge(s_fl.h, fll)
                    v.wait_ge(s_mm.h, s_mm.n)
                    v.wait_ge(s_dve.h, s_dve.n)
                    s_dve.inc(v.tensor_scalar(obv, ldA, isel[:, 0:1], None, op0=ALU.mult))
                    v.wait_ge(s_dve.h, s_dve.n)
                    s_dve.inc(v.scalar_tensor_tensor(obv, ldB, issec[:, 0:1], obv,
                                                     op0=ALU.mult, op1=ALU.add))

            for hh in range(3):
                for j in range(2):
                    lv = 2 * hh + j

                    @C.block.vector
                    def _(v, lv=lv):
                        v.wait_ge(s_pl.h, s_pl.n)
                        v.wait_ge(s_mm.h, s_mm.n)
                        s_dve.inc(v.tensor_mul(sqp[:, :], OB[:, lv * TT:(lv + 1) * TT],
                                               OB[:, lv * TT:(lv + 1) * TT]))

                    @C.block.tensor
                    def _(te, j=j, hh=hh):
                        te.wait_ge(s_dve.h, s_dve.n)
                        s_mm.inc(te.matmul(C.PS(4 + hh % 2)[0:1, 0:TT], C.ONESB[:, :],
                                           sqp[:, :], start=(j == 0), stop=(j == 1)))

                prow = C.ROWS[0:1, 2048 + (hh % 2) * TT: 2048 + (hh % 2 + 1) * TT]

                @C.block.scalar
                def _(sc, prow=prow, hh=hh):
                    sc.wait_ge(s_mm.h, s_mm.n)
                    s_act.inc(sc.activation(out=prow, in_=C.PS(4 + hh % 2)[0:1, 0:TT],
                                            func=AF.Sqrt, scale=1.0 / 256.0,
                                            bias=C.constf(1e-5)[0:1, :]))

                @C.block.vector
                def _(v, prow=prow, hh=hh):
                    v.wait_ge(s_act.h, s_act.n)
                    s_dve.inc(v.reciprocal(prow, prow))

                @C.block.tensor
                def _(te, prow=prow, hh=hh):
                    te.wait_ge(s_dve.h, s_dve.n)
                    s_pe.inc(te.matmul(C.PS(6)[:, 0:TT], C.ONEROWF[:, :], prow,
                                       start=True, stop=True))

                @C.block.vector
                def _(v, hh=hh):
                    v.wait_ge(s_pe.h, s_pe.n)
                    for j in range(2):
                        lv = 2 * hh + j
                        v.wait_ge(s_dve.h, s_dve.n)
                        s_dve.inc(v.scalar_tensor_tensor(
                            OB[:, lv * TT:(lv + 1) * TT], OB[:, lv * TT:(lv + 1) * TT],
                            nw[:, lv % 2: lv % 2 + 1], C.PS(6)[:, 0:TT],
                            op0=ALU.mult, op1=ALU.mult))

            mxg = max(pan_after[0][-1] if pan_after[0] else 0,
                      pan_after[1][-1] if pan_after[1] else 0, s_mm.n if i + grp == 0 else 0)
            pan_after[0].append(mxg)
            pan_after[1].append(mxg)
            for lv in range(6):
                vc = grp * 6 + lv
                slot = lv % 2

                @C.block.gpsimd
                def _(g, vc=vc, slot=slot):
                    panel_gate(g, slot)
                    s_wp[slot].inc(g.dma_start(
                        out=WSL[:, slot * 1024: slot * 1024 + 1024],
                        in_=bass.AP(io["Wg"], vc * 128,
                                    [[1536, 128], [128 * 1536, 8], [1, 128]])), 16)

                @C.block.tensor
                def _(te, slot=slot, i=i):
                    te.wait_ge(s_wp[slot].h, s_wp[slot].n)
                    te.wait_ge(s_act.h, s_act.n)
                    for kc in range(DC):
                        ins = te.matmul(C.PS(2)[:, 0:TT],
                                        WSL[:, slot * 1024 + kc * 128: slot * 1024 + (kc + 1) * 128],
                                        lnq[:, kc * T + i * TT: kc * T + (i + 1) * TT],
                                        start=(kc == 0), stop=(kc == DC - 1))
                    s_mm.inc(ins)

                pan_after[slot].append(s_mm.n)

                @C.block.scalar
                def _(sc):
                    sc.wait_ge(s_mm.h, s_mm.n)
                    sc.wait_ge(s_dve.h, s_dve.n)
                    s_act.inc(sc.activation(out=GT[:, :], in_=C.PS(2)[:, 0:TT], func=AF.Silu))

                @C.block.vector
                def _(v, lv=lv):
                    v.wait_ge(s_act.h, s_act.n)
                    v.wait_ge(s_dve.h, s_dve.n)
                    s_dve.inc(v.tensor_tensor(OB[:, lv * TT:(lv + 1) * TT],
                                              OB[:, lv * TT:(lv + 1) * TT], GT[:, :],
                                              op=ALU.mult))

            mxw = max(pan_after[0][-1], pan_after[1][-1])
            pan_after[0].append(mxw)
            pan_after[1].append(mxw)
            for dc in range(DC):
                slot = dc % 2

                @C.block.gpsimd
                def _(g, dc=dc, slot=slot, grp=grp):
                    panel_gate(g, slot)
                    s_wp[slot].inc(g.dma_start(
                        out=WSL[:, slot * 768: slot * 768 + 768],
                        in_=bass.AP(io["Wo"], (grp * 6 * 128) * 1024 + dc * 128,
                                    [[1024, 128], [128 * 1024, 6], [1, 128]])), 16)

                @C.block.tensor
                def _(te, dc=dc, slot=slot):
                    te.wait_ge(s_wp[slot].h, s_wp[slot].n)
                    te.wait_ge(s_dve.h, s_dve.n)
                    for lv in range(6):
                        ins = te.matmul(C.PS(3)[:, 0:TT],
                                        WSL[:, slot * 768 + lv * 128: slot * 768 + (lv + 1) * 128],
                                        OB[:, lv * TT:(lv + 1) * TT],
                                        start=(lv == 0), stop=(lv == 5))
                    s_mm.inc(ins)

                pan_after[slot].append(s_mm.n)

                @C.block.vector
                def _(v, dc=dc, i=i):
                    v.wait_ge(s_mm.h, s_mm.n)
                    s_dve.inc(v.tensor_tensor(Xc(C, dc, i * TT, TT), Xc(C, dc, i * TT, TT),
                                              C.PS(3)[:, 0:TT], op=ALU.add))


BLOCK_IO = {}
BLOCK_EMITTERS = {}


def register(name, emitter, io_spec):
    BLOCK_EMITTERS[name] = emitter
    BLOCK_IO[name] = io_spec


register("L0_attn", emit_L0_attn, {
    "ipT_z": ([1024, 2048], F32),
    "ipT_xbc": ([1024, 2304], F32),
    "ipT_dt": ([1024, 32], F32),
    "opT": ([2048, 1024], F32),
    "gateT": ([1024, 4096], F32),
    "upT": ([1024, 4096], F32),
    "downT": ([4096, 1024], F32),
    "cw_fm": ([128, 72], F32),
    "cb_fm": ([128, 18], F32),
    "wpre_fm": ([128, 8], F32),
    "wpost_fm": ([128, 8], F32),
    "wssm_fm": ([128, 16], F32),
    "dpar_fm": ([128, 16], F32),
    "dtb_c": ([32, 1], F32),
    "alog_c": ([32, 1], F32),
    "ident_f32": ([128, 128], F32),
    "ident_bf": ([128, 128], BF16),
    "mask_lt": ([128, 128], F32),
    "xh_fm": ([128, 32], F32),
})

register("L3_attn", emit_L3_attn, {
    "pkv_cat": ([1024, 256], F32),
    "pq_cat": ([1024, 1024], F32),
    "po_cat": ([1024, 1024], F32),
})

register("L2_attn", emit_L2_attn, {
    "ident_f32": ([128, 128], F32),
    "ident_bf": ([128, 128], BF16),
    "W_in_a": ([1024, 2048], F32),
    "W_in_b": ([1024, 2048], F32),
    "WABC": ([2048, 48], F32),
    "W_out": ([2048, 1024], F32),
    "normw_fm": ([128, 16], F32),
    "mask_lt": ([128, 128], F32),
    "is_second": ([128, 1], F32),
})

register("L0_ffn", emit_L0_ffn, {
    "w1AT": ([1024, 512], F32),
    "w1B": ([512, 4096], F32),
    "w2AT": ([4096, 512], F32),
    "w2B": ([512, 1024], F32),
    "ew2": ([128, 32], F32),
    "ei2": ([128, 32], F32),
})

register("L2_ffn", emit_L2_ffn, {
    "wgT": ([1024, 2816], F32),
    "wuT": ([1024, 2816], F32),
    "wdT": ([2816, 1024], F32),
    "wg_oi": ([2816, 1024], F32),
    "wu_oi": ([2816, 1024], F32),
    "wd_oi": ([1024, 2816], F32),
    "clip_x": ([1, 1], F32),
    "clip_down": ([1, 1], F32),
})

register("L3_ffn", emit_L3_ffn, {
    "wqT_i8": ([1024, 1024], I8),
    "wsc_fm": ([128, 8], F32),
    "bia_fm": ([128, 8], F32),
    "ident_bf": ([128, 128], BF16),
})

register("L1_attn", emit_L1_attn, {
    "Wq": ([1024, 768], F32),
    "Wk": ([1024, 768], F32),
    "Wv": ([1024, 1536], F32),
    "Wg": ([1024, 1536], F32),
    "Wo": ([1536, 1024], F32),
    "Wba": ([1024, 12], F32),
    "cwq_fm": ([128, 24], F32),
    "cwk_fm": ([128, 24], F32),
    "cwv_fm": ([128, 48], F32),
    "l1_alog_c": ([6, 1], F32),
    "l1_dtb_c": ([6, 1], F32),
    "nw_fm": ([128, 2], F32),
    "ident_bf": ([128, 128], BF16),
    "mask_lt": ([128, 128], F32),
    "mask_st": ([128, 128], F32),
    "is_second": ([128, 1], F32),
})

register("L1_ffn", emit_L1_ffn, {
    "L1_ffn_U1": ([1024, 256], F32),
    "L1_ffn_V1": ([256, 4096], F32),
    "L1_ffn_U2": ([4096, 256], F32),
    "L1_ffn_V2": ([256, 1024], F32),
    "b1_fm": ([128, 32], F32),
    "b2_fm": ([128, 8], F32),
})


def emit_barrier(C):
    items = [(s.h, s.n) for nm, s in C.sems.items() if s.n > 0]
    for eng in ("sync", "gpsimd", "scalar", "vector", "tensor"):
        sec = getattr(C.block, eng)

        @sec
        def _(e, items=items):
            for h, n in items:
                e.wait_ge(h, n)


_FUSED_CACHE = {}


def build_fused_program(names):
    key = tuple(names)
    if key in _FUSED_CACHE:
        return _FUSED_CACHE[key]
    nc = bass.Bass("TRN2", target_bir_lowering=False)
    ctx = contextlib.ExitStack()
    merged = {}
    with ctx:
        C = BC(nc, ctx)
        C.io["x_fm"] = nc.dram_tensor("x_fm", [D, T], F32, kind="ExternalInput")
        C.io["y_fm"] = nc.dram_tensor("y_fm", [D, T], F32, kind="ExternalOutput")
        ci = const_inputs()
        for k, arr in ci.items():
            dt = BF16 if arr.dtype == BF else F32
            C.io[k] = nc.dram_tensor(k, list(arr.shape), dt, kind="ExternalInput")
        for name in names:
            for k, (shp, dt) in BLOCK_IO[name].items():
                if k in merged:
                    assert merged[k] == (tuple(shp), dt), k
                    continue
                merged[k] = (tuple(shp), dt)
                C.io[k] = nc.dram_tensor(k, list(shp), dt, kind="ExternalInput")
        C.X = C.sb("X", [128, DC * T], F32)
        C.ROWS = C.sb("ROWS", [1, 2 * T], F32)
        C.LNTMP = C.sb("LNTMP", [128, T], F32)
        C.PSUM = C.ps("PSUM", [128, 7 * TT], F32)
        C.PSB = C.ps("PSB", [128, 1024], BF16)
        C.RB = C.sb("RB", [128, T], F32)
        C.MB = C.sb("MB", [128, T], F32)
        C.LNXB = C.sb("LNXB", [128, 2 * TT], BF16)
        C.LNSQ = C.sb("LNSQ", [128, 2 * TT], BF16)
        emit_consts(C)
        C.block = ctx.enter_context(nc.Block())
        for _cv in (1.0, -1.0, 1e-5, 1e-6, 1.1920929e-07, 1e-12, 1.28e-10):
            C.constf(_cv)
        emit_load_x(C)
        for bi, name in enumerate(names):
            with contextlib.ExitStack() as bctx:
                C.cur = bctx
                C.prefix = "B%d" % bi
                BLOCK_EMITTERS[name](C)
                emit_barrier(C)
            C.cur = None
            C.prefix = ""
        emit_store_x(C)
    _FUSED_CACHE[key] = (nc, merged)
    return nc, merged


_PROGRAM_CACHE = {}


def build_block_program(name):
    if name in _PROGRAM_CACHE:
        return _PROGRAM_CACHE[name]
    nc = bass.Bass("TRN2", target_bir_lowering=False)
    ctx = contextlib.ExitStack()
    with ctx:
        C = BC(nc, ctx)
        C.io["x_fm"] = nc.dram_tensor("x_fm", [D, T], F32, kind="ExternalInput")
        C.io["y_fm"] = nc.dram_tensor("y_fm", [D, T], F32, kind="ExternalOutput")
        ci = const_inputs()
        for k, arr in ci.items():
            dt = BF16 if arr.dtype == BF else F32
            C.io[k] = nc.dram_tensor(k, list(arr.shape), dt, kind="ExternalInput")
        for ioname, (shp, dt) in BLOCK_IO[name].items():
            C.io[ioname] = nc.dram_tensor(ioname, list(shp), dt, kind="ExternalInput")
        C.X = C.sb("X", [128, DC * T], F32)
        C.ROWS = C.sb("ROWS", [1, 2 * T], F32)
        C.LNTMP = C.sb("LNTMP", [128, T], F32)
        C.PSUM = C.ps("PSUM", [128, 7 * TT], F32)
        C.PSB = C.ps("PSB", [128, 1024], BF16)
        C.RB = C.sb("RB", [128, T], F32)
        C.MB = C.sb("MB", [128, T], F32)
        emit_consts(C)
        C.block = ctx.enter_context(nc.Block())
        for _cv in (1.0, 1e-5, 1e-6, 1.1920929e-07):
            C.constf(_cv)
        emit_load_x(C)
        BLOCK_EMITTERS[name](C)
        emit_store_x(C)
    _PROGRAM_CACHE[name] = nc
    return nc


def fm_col(vec, n):
    v = np.asarray(vec, np.float32).reshape(n, 128)
    return np.ascontiguousarray(v.T)


def _wabc_pack(inputs):
    w = np.zeros((2048, 48), np.float32)
    w[:, 0:16] = np.asarray(inputs["L2_attn_W_A"], np.float32)
    w[:, 16:32] = np.asarray(inputs["L2_attn_W_B"], np.float32)
    w[:, 32:48] = np.asarray(inputs["L2_attn_W_C"], np.float32)
    return np.ascontiguousarray(w)


def host_prep(inputs):
    pre = {"shared": dict(const_inputs())}
    percore = [dict() for _ in range(NCORE)]
    pre["percore"] = percore
    x = np.ascontiguousarray(np.asarray(inputs["L0_attn_x"], np.float32))
    xs = []
    for c in range(NCORE):
        b, h = c // 2, c % 2
        xs.append(np.ascontiguousarray(x[b, h * T:(h + 1) * T, :].T))
    pre["x_fm"] = xs

    ipw = np.asarray(inputs["L0_attn_in_proj_w"], np.float32)
    cwf = np.asarray(inputs["L0_attn_conv_w"], np.float32)
    pre["L0_attn"] = {
        "ipT_z": np.ascontiguousarray(ipw[0:2048, :].T),
        "ipT_xbc": np.ascontiguousarray(ipw[2048:4352, :].T),
        "ipT_dt": np.ascontiguousarray(ipw[4352:4384, :].T),
        "opT": np.ascontiguousarray(np.asarray(inputs["L0_attn_out_proj_w"], np.float32).T),
        "gateT": np.ascontiguousarray(np.asarray(inputs["L0_attn_gate_w"], np.float32).T),
        "upT": np.ascontiguousarray(np.asarray(inputs["L0_attn_up_w"], np.float32).T),
        "downT": np.ascontiguousarray(np.asarray(inputs["L0_attn_down_w"], np.float32).T),
        "cw_fm": np.ascontiguousarray(cwf.reshape(18, 128, 4).transpose(1, 0, 2).reshape(128, 72)),
        "cb_fm": fm_col(inputs["L0_attn_conv_b"], 18),
        "wpre_fm": fm_col(inputs["L0_attn_prenorm_w"], 8),
        "wpost_fm": fm_col(inputs["L0_attn_postnorm_w"], 8),
        "wssm_fm": fm_col(inputs["L0_attn_ssm_norm_w"], 16),
        "dpar_fm": np.ascontiguousarray(
            np.repeat(np.asarray(inputs["L0_attn_D_param"], np.float32), 64)
            .reshape(16, 128).T),
        "dtb_c": np.asarray(inputs["L0_attn_dt_bias"], np.float32).reshape(32, 1),
        "alog_c": np.asarray(inputs["L0_attn_A_log"], np.float32).reshape(32, 1),
        "ident_f32": np.eye(128, dtype=np.float32),
        "ident_bf": np.eye(128, dtype=BF),
        "mask_lt": np.where(np.arange(128)[:, None] <= np.arange(128)[None, :],
                            0.0, -1e30).astype(np.float32),
    }
    x_in_full = np.ascontiguousarray(np.asarray(inputs["L0_attn_x"], np.float32))
    for c in range(NCORE):
        b, hh = c // 2, c % 2
        if hh == 0:
            xhist = np.zeros((4, 1024), np.float32)
        else:
            xhist = x_in_full[b, T - 4:T, :]
        percore[c]["xh_fm"] = np.ascontiguousarray(
            xhist.T.reshape(128, 8, 4).transpose(0, 1, 2).reshape(128, 32)) if False \
            else np.ascontiguousarray(xhist.reshape(4, 8, 128).transpose(2, 1, 0).reshape(128, 32))
    pk = np.asarray(inputs["L3_attn_proj_k"], np.float32)
    pv = np.asarray(inputs["L3_attn_proj_v"], np.float32)
    pq_ = np.asarray(inputs["L3_attn_proj_q"], np.float32)
    po_ = np.asarray(inputs["L3_attn_proj_o"], np.float32)
    pre["L3_attn"] = {
        "pkv_cat": np.ascontiguousarray(np.concatenate([pk, pv], axis=1)),
        "pq_cat": np.ascontiguousarray(pq_.transpose(1, 0, 2).reshape(1024, 1024)),
        "po_cat": np.ascontiguousarray(po_.transpose(0, 2, 1).reshape(1024, 1024)),
    }
    W_in = np.asarray(inputs["L2_attn_W_in"], np.float32)
    mask_lt = np.where(np.arange(128)[:, None] <= np.arange(128)[None, :],
                       0.0, -1e30).astype(np.float32)
    pre["L2_attn"] = {
        "ident_f32": np.eye(128, dtype=np.float32),
        "ident_bf": np.eye(128, dtype=BF),
        "W_in_a": np.ascontiguousarray(W_in[:, 0:2048]),
        "W_in_b": np.ascontiguousarray(W_in[:, 2048:4096]),
        "WABC": _wabc_pack(inputs),
        "W_out": np.ascontiguousarray(np.asarray(inputs["L2_attn_W_out"], np.float32)),
        "normw_fm": fm_col(inputs["L2_attn_norm_w"], 16),
        "mask_lt": mask_lt,
    }
    for c in range(NCORE):
        percore[c]["is_second"] = np.full((128, 1), float(c % 2), np.float32)
    ew_full = np.asarray(inputs["L0_ffn_expert_weights"], np.float32)
    ei_full = np.asarray(inputs["L0_ffn_expert_indices"])
    for c in range(NCORE):
        t0 = c * T
        if t0 + T <= ew_full.shape[0]:
            ew = ew_full[t0:t0 + T]
            ei = ei_full[t0:t0 + T].astype(np.float32)
        else:
            ew = np.zeros((T, 2), np.float32)
            ei = np.full((T, 2), -1.0, np.float32)
        percore[c]["ew2"] = np.ascontiguousarray(
            ew.reshape(16, 128, 2).transpose(1, 0, 2).reshape(128, 32))
        percore[c]["ei2"] = np.ascontiguousarray(
            ei.reshape(16, 128, 2).transpose(1, 0, 2).reshape(128, 32))
    pre["L0_ffn"] = {
        "w1AT": np.ascontiguousarray(np.asarray(inputs["L0_ffn_w1_A"], np.float32).T),
        "w1B": np.ascontiguousarray(np.asarray(inputs["L0_ffn_w1_B"], np.float32)),
        "w2AT": np.ascontiguousarray(np.asarray(inputs["L0_ffn_w2_A"], np.float32).T),
        "w2B": np.ascontiguousarray(np.asarray(inputs["L0_ffn_w2_B"], np.float32)),
    }
    pre["L2_ffn"] = {
        "wgT": np.ascontiguousarray(np.asarray(inputs["L2_ffn_w_gate"], np.float32).T),
        "wuT": np.ascontiguousarray(np.asarray(inputs["L2_ffn_w_up"], np.float32).T),
        "wdT": np.ascontiguousarray(np.asarray(inputs["L2_ffn_w_down"], np.float32).T),
        "wg_oi": np.ascontiguousarray(np.asarray(inputs["L2_ffn_w_gate"], np.float32)),
        "wu_oi": np.ascontiguousarray(np.asarray(inputs["L2_ffn_w_up"], np.float32)),
        "wd_oi": np.ascontiguousarray(np.asarray(inputs["L2_ffn_w_down"], np.float32)),
        "clip_x": np.asarray(inputs["L2_ffn_clip_x"], np.float32).reshape(1, 1),
        "clip_down": np.asarray(inputs["L2_ffn_clip_down"], np.float32).reshape(1, 1),
    }
    pre["L3_ffn"] = {
        "wqT_i8": np.ascontiguousarray(np.asarray(inputs["L3_ffn_weight_q"]).T),
        "wsc_fm": fm_col(inputs["L3_ffn_weight_scale"], 8),
        "bia_fm": fm_col(inputs["L3_ffn_bias"], 8),
        "ident_bf": np.eye(128, dtype=BF),
    }
    cq = np.asarray(inputs["L1_attn_conv_q"], np.float32)
    ck = np.asarray(inputs["L1_attn_conv_k"], np.float32)
    cv = np.asarray(inputs["L1_attn_conv_v"], np.float32)
    pre["L1_attn"] = {
        "Wq": np.ascontiguousarray(np.asarray(inputs["L1_attn_Wq"], np.float32)),
        "Wk": np.ascontiguousarray(np.asarray(inputs["L1_attn_Wk"], np.float32)),
        "Wv": np.ascontiguousarray(np.asarray(inputs["L1_attn_Wv"], np.float32)),
        "Wg": np.ascontiguousarray(np.asarray(inputs["L1_attn_Wg"], np.float32)),
        "Wo": np.ascontiguousarray(np.asarray(inputs["L1_attn_Wo"], np.float32)),
        "Wba": np.ascontiguousarray(np.concatenate(
            [np.asarray(inputs["L1_attn_Wb"], np.float32),
             np.asarray(inputs["L1_attn_Wa"], np.float32)], axis=1)),
        "cwq_fm": np.ascontiguousarray(cq.reshape(6, 128, 4).transpose(1, 0, 2).reshape(128, 24)),
        "cwk_fm": np.ascontiguousarray(ck.reshape(6, 128, 4).transpose(1, 0, 2).reshape(128, 24)),
        "cwv_fm": np.ascontiguousarray(cv.reshape(12, 128, 4).transpose(1, 0, 2).reshape(128, 48)),
        "l1_alog_c": np.asarray(inputs["L1_attn_A_log"], np.float32).reshape(6, 1),
        "l1_dtb_c": np.asarray(inputs["L1_attn_dt_bias"], np.float32).reshape(6, 1),
        "nw_fm": np.ascontiguousarray(
            np.asarray(inputs["L1_attn_norm_w"], np.float32).reshape(2, 128).T),
        "ident_bf": np.eye(128, dtype=BF),
        "mask_lt": mask_lt,
        "mask_st": np.where(np.arange(128)[:, None] < np.arange(128)[None, :],
                            0.0, -1e30).astype(np.float32),
    }
    pre["L1_ffn"] = {
        "L1_ffn_U1": np.ascontiguousarray(np.asarray(inputs["L1_ffn_U1"], np.float32)),
        "L1_ffn_V1": np.ascontiguousarray(np.asarray(inputs["L1_ffn_V1"], np.float32)),
        "L1_ffn_U2": np.ascontiguousarray(np.asarray(inputs["L1_ffn_U2"], np.float32)),
        "L1_ffn_V2": np.ascontiguousarray(np.asarray(inputs["L1_ffn_V2"], np.float32)),
        "b1_fm": fm_col(inputs["L1_ffn_b1"], 32),
        "b2_fm": fm_col(inputs["L1_ffn_b2"], 8),
    }
    return pre


BLOCK_ORDER = ["L0_attn", "L0_ffn", "L1_attn", "L1_ffn", "L2_attn", "L2_ffn",
               "L3_attn", "L3_ffn"]


def run_blocks(pre, names, xs=None):
    xs = xs if xs is not None else pre["x_fm"]
    for name in names:
        nc = build_block_program(name)
        in_maps = []
        for c in range(NCORE):
            m = {"x_fm": xs[c]}
            m.update(pre["shared"])
            m.update(pre.get(name, {}))
            need = set(BLOCK_IO[name])
            for k, v in pre.get("percore", [{}] * NCORE)[c].items():
                if k in need:
                    m[k] = v
            in_maps.append(m)
        res = run_bass_kernel_spmd(nc, in_maps, core_ids=list(range(NCORE)))
        xs = [np.ascontiguousarray(np.asarray(res.results[c]["y_fm"])) for c in range(NCORE)]
    return xs


def gather(xs):
    out = np.empty((4, 4096, 1024), np.float32)
    for c in range(NCORE):
        b, h = c // 2, c % 2
        out[b, h * T:(h + 1) * T, :] = xs[c].T
    return out


def kernel(**inputs):
    pre = host_prep(inputs)
    names = [n for n in BLOCK_ORDER if n in BLOCK_EMITTERS]
    nc, merged = build_fused_program(names)
    in_maps = []
    for c in range(NCORE):
        m = {"x_fm": pre["x_fm"][c]}
        m.update(pre["shared"])
        for name in names:
            m.update(pre.get(name, {}))
        for k, v in pre.get("percore", [{}] * NCORE)[c].items():
            if k in merged:
                m[k] = v
        in_maps.append(m)
    res = run_bass_kernel_spmd(nc, in_maps, core_ids=list(range(NCORE)))
    xs = [np.ascontiguousarray(np.asarray(res.results[c]["y_fm"])) for c in range(NCORE)]
    return gather(xs)
